# version 17
# baseline (speedup 1.0000x reference)
import dataclasses
import math

import numpy as np
import concourse.bass as bass
import concourse.mybir as mybir
from concourse.bass_utils import run_bass_kernel_spmd

F32 = mybir.dt.float32
BF16 = mybir.dt.bfloat16
AF = mybir.ActivationFunctionType
ALU = mybir.AluOpType
ET = mybir.EngineType
AX = mybir.AxisListType

DEBUG = False
UPTO = 99
FSUB = 99
W2CAP = 22
W2SUB = 2
W2EVAC = 1
W2LOAD = 1
W2ALT = 0
NMM = 1
DMA_TWICE = 0
SKIPP3 = 0
PARANOID3 = False

B2, N, D = 2, 8192, 1024
DH = 64
TOK = 2048
HALO = 1026
QEXT = 3072
HID = 2730
FF1 = 5460
NEG = -1.0e30
LN32 = math.log(32.0)

GROUPS4 = [[0, 1, 2, 3], [4, 5, 6, 7]]
F1_CH = [128] * 21 + [42]
F1_OFF = [128 * i for i in range(22)]
KB = 1024


def brep(ap, dims):
    old = [list(x) for x in ap.ap]
    return dataclasses.replace(ap, ap=[old[0]] + [list(d) for d in dims])



def _rotary_tables(s0):
    p = np.arange(QEXT, dtype=np.float64)
    t = np.maximum(s0 - 1024 + p, 0.0)
    half = np.arange(0, DH, 2, dtype=np.float64)
    inv_freq = 1.0 / 10000.0 ** (half / DH)
    freqs = t[None, :] * inv_freq[:, None]
    freqs = np.concatenate([freqs, freqs], axis=0)
    base = (half + 0.4 * DH) / (1.4 * DH)
    power = (t - N // 2) / 256.0
    scale = base[:, None] ** power[None, :]
    scale = np.concatenate([scale, scale], axis=0)
    cos, sin = np.cos(freqs), np.sin(freqs)
    sgn = np.concatenate([-np.ones(32), np.ones(32)])[:, None]
    c1q = (8.0 * cos * scale).T
    c2q = (8.0 * sin * scale * sgn).T
    c1k = (cos / scale).T
    c2k = (sin / scale * sgn).T
    return np.ascontiguousarray(
        np.concatenate([c1q, c2q, c1k, c2k], axis=1)).astype(np.float32)


def stage_inputs(inp):
    x = np.asarray(inp["L0_attn_x"])
    gamma = np.asarray(inp["L0_attn_gamma"])
    w_qkv = np.asarray(inp["L0_attn_w_qkv"])
    w_out = np.asarray(inp["L0_attn_w_out"])
    fwqkv = np.asarray(inp["L0_ffn_wqkv"])
    fqs = np.asarray(inp["L0_ffn_q_scale"])
    fks = np.asarray(inp["L0_ffn_k_scale"])
    fwo = np.asarray(inp["L0_ffn_wo"])
    flng = np.asarray(inp["L0_ffn_ln_g"])
    flnb = np.asarray(inp["L0_ffn_ln_b"])
    fw1 = np.asarray(inp["L0_ffn_w1"])
    fb1 = np.asarray(inp["L0_ffn_b1"])
    fw2 = np.asarray(inp["L0_ffn_w2"])
    fb2 = np.asarray(inp["L0_ffn_b2"])

    woutT = np.ascontiguousarray(w_out.T)
    gamma_row = np.ascontiguousarray(gamma.reshape(1, D))

    def col128(v):
        return np.ascontiguousarray(v.reshape(8, 128).T)

    def colsF1(v, off):
        out = np.zeros((128, 22), v.dtype)
        for c in range(22):
            n = F1_CH[c]
            out[:n, c] = v[off + F1_OFF[c]: off + F1_OFF[c] + n]
        return np.ascontiguousarray(out)

    maps = []
    for r in range(8):
        B, Qq = r // 4, r % 4
        s0 = 2048 * Qq
        xT = np.ascontiguousarray(x[B, s0:s0 + TOK, :].T)
        hs = [4 * Qq + i for i in range(4)]
        rows = []
        for pi in range(2):
            for h in (hs[2 * pi], hs[2 * pi + 1]):
                rows.append(w_qkv[DH * h: DH * h + DH])
        for pi in range(2):
            for h in (hs[2 * pi], hs[2 * pi + 1]):
                rows.append(w_qkv[1024 + DH * h: 1024 + DH * h + DH])
        for h in hs:
            rows.append(w_qkv[2048 + DH * h: 2048 + DH * h + DH])
        wqkvT = np.ascontiguousarray(np.concatenate(rows, axis=0).T)

        mval = NEG if Qq == 0 else 0.0
        mP0 = np.full((512, 512), mval, np.float32)

        m = {
            "xT": xT.astype(np.float32),
            "wqkvT": wqkvT.astype(np.float32),
            "gamma_row": gamma_row.astype(np.float32),
            "woutT": woutT.astype(np.float32),
            "rotq": _rotary_tables(s0),
            "mP0": mP0,
        }
        for li in range(2):
            m[f"fwqkv{li}"] = np.ascontiguousarray(fwqkv[li]).astype(np.float32)
            m[f"fwo{li}"] = np.ascontiguousarray(fwo[li]).astype(np.float32)
            m[f"fw1{li}"] = np.ascontiguousarray(fw1[li]).astype(np.float32)
            m[f"fw2{li}"] = np.ascontiguousarray(fw2[li]).astype(np.float32)
            m[f"qs_row{li}"] = np.ascontiguousarray(
                fqs[li].reshape(1, DH)).astype(np.float32)
            m[f"ks_row{li}"] = np.ascontiguousarray(
                fks[li].reshape(1, DH)).astype(np.float32)
            m[f"lng{li}"] = col128(flng[li]).astype(np.float32)
            m[f"lnb{li}"] = col128(flnb[li]).astype(np.float32)
            m[f"b2c{li}"] = col128(fb2[li]).astype(np.float32)
            m[f"b1a{li}"] = colsF1(fb1[li], 0).astype(np.float32)
            m[f"b1g{li}"] = colsF1(fb1[li], HID).astype(np.float32)
        maps.append(m)
    return maps



class Ct:
    __slots__ = ("h", "n", "name")

    def __init__(self, h, name):
        self.h, self.n, self.name = h, 0, name


class RingCt:
    __slots__ = ("subs", "idx", "name")

    def __init__(self, subs, name):
        self.subs, self.idx, self.name = subs, 0, name

    @property
    def n(self):
        return tuple(c.n for c in self.subs)


class P:
    ENGS = ("sync", "tensor", "vector", "scalar", "gpsimd")

    DMA_DEPTH = {"sync": 4, "gpsimd": 4}

    def __init__(self, nc):
        self.nc = nc
        self.q = {e: [] for e in self.ENGS}
        self.sems = {}
        self.t = {}
        self.pb_st = {}
        self.w2_slot_mm = [0, 0]
        self.dma_fifo = {}
        self.st_ye = {}
        self.wst_ev = 0
        self.w1_slot_mm = [0, 0, 0, 0]

    def sem(self, name):
        if name not in self.sems:
            self.sems[name] = Ct(self.nc.alloc_semaphore(name), name)
        return self.sems[name]

    def wait(self, eng, ct, val=None):
        if isinstance(ct, RingCt):
            vals = ct.n if val is None else val
            if not isinstance(vals, tuple):
                return
            for c, v in zip(ct.subs, vals):
                self.wait(eng, c, v)
            return
        v = ct.n if val is None else val
        if v <= 0:
            return
        self.q[eng].append(lambda e, h=ct.h, v=v: e.wait_ge(h, v))

    def op(self, eng, fn, incs=()):
        snap = tuple(incs)

        def run(e, fn=fn, snap=snap):
            ins = fn(e)
            if ins is not None:
                for ct, amt in snap:
                    ins.then_inc(ct.h, amt)
        self.q[eng].append(run)
        for ct, amt in snap:
            ct.n += amt

    def raw(self, eng, fn):
        self.q[eng].append(fn)

    def dma(self, eng, dst, src, sem, slow=False):
        if isinstance(sem, RingCt):
            sub = sem.subs[sem.idx % len(sem.subs)]
            sem.idx += 1
            self.wait(eng, sub, sub.n)
            self.op(eng, lambda e, d=dst, s=src, sl=slow: e.dma_start(
                out=d, in_=s, allow_slow_non_contiguous=sl), [(sub, 16)])
            return
        self.op(eng, lambda e, d=dst, s=src, sl=slow: e.dma_start(
            out=d, in_=s, allow_slow_non_contiguous=sl), [(sem, 16)])

    def mm(self, out, lhsT, rhs, start, stop, inc=None):
        incs = [(inc, 1)] if inc is not None else []
        self.op("tensor",
                lambda e, o=out, l=lhsT, r=rhs, s=start, st=stop:
                e.matmul(o, l, r, start=s, stop=st), incs)

    def finalize(self):
        nc = self.nc
        with nc.Block() as block:
            @block.sync
            def _(e):
                for fn in self.q["sync"]:
                    fn(e)

            @block.tensor
            def _(e):
                for fn in self.q["tensor"]:
                    fn(e)

            @block.vector
            def _(e):
                for fn in self.q["vector"]:
                    fn(e)

            @block.scalar
            def _(e):
                for fn in self.q["scalar"]:
                    fn(e)

            @block.gpsimd
            def _(e):
                for fn in self.q["gpsimd"]:
                    fn(e)



def build(debug=False):
    nc = bass.Bass("TRN2", target_bir_lowering=False, debug=False)
    p = P(nc)
    t = p.t

    def ein(name, shape, dt=F32):
        t[name] = nc.dram_tensor(name, list(shape), dt, kind="ExternalInput").ap()

    def eout(name, shape, dt=F32):
        t[name] = nc.dram_tensor(name, list(shape), dt, kind="ExternalOutput").ap()

    def idram(name, shape, dt):
        t[name] = nc.dram_tensor(name, list(shape), dt, kind="Internal").ap()

    ein("xT", (D, TOK))
    ein("wqkvT", (D, 768))
    ein("gamma_row", (1, D))
    ein("woutT", (D, D))
    ein("rotq", (QEXT, 256))
    ein("mP0", (512, 512))
    for li in range(2):
        ein(f"fwqkv{li}", (D, 1536))
        ein(f"fwo{li}", (512, D))
        ein(f"fw1{li}", (D, FF1))
        ein(f"fw2{li}", (HID, D))
        ein(f"qs_row{li}", (1, DH))
        ein(f"ks_row{li}", (1, DH))
        ein(f"lng{li}", (128, 8))
        ein(f"lnb{li}", (128, 8))
        ein(f"b2c{li}", (128, 8))
        ein(f"b1a{li}", (128, 22))
        ein(f"b1g{li}", (128, 22))
    eout("outT", (D, TOK))

    idram("ag_xn_in", (D, TOK), BF16)
    idram("ag_xn_out", (4 * D, TOK), BF16)
    idram("q_d", (256, N), BF16)
    idram("ag_o_in", (260, N), BF16)
    idram("ag_o_out", (1040, N), BF16)
    idram("o_mine", (1040, TOK), BF16)
    idram("ag_l_in", (4, N), F32)
    idram("ag_l_out", (16, N), F32)
    idram("l_d", (16, TOK), F32)
    idram("linv_d", (16, TOK), F32)
    idram("x1_full", (D, TOK), F32)
    idram("y1_full", (D, TOK), F32)
    idram("y2_full", (D, TOK), F32)
    for nm in ("x1", "y1"):
        idram(f"ag_{nm}_in", (D, HALO), F32)
        idram(f"ag_{nm}_out", (4 * D, HALO), F32)
        idram(f"{nm}_halo", (D, HALO), F32)
    idram("xe_d", (D, 2 + QEXT), BF16)
    idram("ye_d", (D, 2 + QEXT), BF16)
    idram("qkv_d", (1024, QEXT), BF16)
    idram("v_d", (QEXT, 520), BF16)
    idram("o_d", (520, 2560), BF16)
    idram("il_row", (128, 160), F32)
    idram("xmid_d", (D, 2049), F32)

    if debug:
        eout("dbg_xn", (4 * D, TOK))
        eout("dbg_q", (256, N))
        eout("dbg_k", (256, N))
        eout("dbg_v", (N, 260))
        eout("dbg_o", (1040, TOK))
        eout("dbg_oin", (260, N))
        eout("dbg_lin", (4, N))
        eout("dbg_onm", (128, 4096))
        eout("dbg_t1", (128, 4096))
        eout("dbg_linv", (16, TOK))
        eout("dbg_x1", (D, TOK))
        eout("dbg_ph", (128, 2048))
        eout("dbg_il", (128, 160))
        eout("dbg_ln", (128, 8192))
        eout("dbg_gt", (128, 22528))
        eout("dbg_pb", (128, 2048))
        eout("dbg_ba", (128, 2048))
        eout("dbg_xe", (D, 2 + QEXT))
        eout("dbg_qkv", (1024, QEXT))
        eout("dbg_v2", (QEXT, 520))
        eout("dbg_oi", (520, 2560))
        eout("dbg_xm", (D, 2049))
        eout("dbg_y1", (D, TOK))

    idx = np.arange(128)
    tri_pf = np.where(idx[:, None] >= idx[None, :], 0.0, NEG).astype(np.float32)
    tri_fp = np.where(idx[None, :] >= idx[:, None], 0.0, NEG).astype(np.float32)
    e2 = np.zeros((2, 128), np.float32)
    e2[0, :64] = 1.0
    e2[1, 64:] = 1.0
    t["c_tri_pf"] = nc.inline_tensor(tri_pf, "c_tri_pf").ap()
    t["c_tri_fp"] = nc.inline_tensor(tri_fp, "c_tri_fp").ap()
    t["c_ident"] = nc.inline_tensor(np.eye(128, dtype=np.float32), "c_ident").ap()
    t["c_ones_r"] = nc.inline_tensor(np.ones((1, 128), np.float32),
                                     "c_ones_r").ap()
    t["c_mones_r"] = nc.inline_tensor(-np.ones((1, 128), np.float32),
                                      "c_mones_r").ap()
    t["c_ones_c"] = nc.inline_tensor(np.ones((128, 1), np.float32),
                                     "c_ones_c").ap()
    t["c_e2"] = nc.inline_tensor(e2, "c_e2").ap()
    t["c_m30"] = nc.inline_tensor(np.full((128, 1), -30.0, np.float32),
                                  "c_m30").ap()

    def sb(name, shape, dt, off):
        h = nc.alloc_sbuf_tensor_at(name, list(shape), dt,
                                    offset=int((off + 16.0) * KB))
        t[name] = h.ap()
        return t[name]

    sb("s_trip", (128, 128), F32, 0.0)
    sb("s_trif", (128, 128), F32, 0.5)
    sb("s_id", (128, 128), F32, 1.0)
    sb("s_ones_r", (1, 128), F32, 1.5)
    sb("s_mones_r", (1, 128), F32, 2.0)
    sb("s_ones_cf", (128, 1), F32, 2.5)
    sb("s_ones_cb", (128, 1), BF16, 3.0)
    sb("s_e2", (2, 128), F32, 3.5)
    sb("s_m30", (128, 1), F32, 5.75)
    sb("s_lng", (128, 8), F32, 4.0)
    sb("s_lnb", (128, 8), F32, 4.25)
    sb("s_b2c", (128, 8), F32, 4.5)
    sb("s_qsb", (128, 64), F32, 4.75)
    sb("s_ksb", (128, 64), F32, 5.0)
    sb("s_b1a", (128, 22), F32, 5.25)
    sb("s_b1g", (128, 22), F32, 5.5)
    sb("lr2", (2, 512), F32, 6.0)
    sb("lst", (1, 1024), F32, 200.0)
    sb("dbgrow", (1, 1024), F32, 186.0)
    sb("li_sb", (128, 160), F32, 8.0)
    sb("s_gp1", (1, D), F32, 8.75)
    sb("qs_r", (1, 64), F32, 12.75)
    sb("ks_r", (1, 64), F32, 13.0)
    sb("mcol", (128, 8), F32, 13.25)
    sb("r0", (1, 2048), F32, 14.0)
    sb("r1", (1, 2048), F32, 22.0)
    sb("r2", (1, 2048), F32, 30.0)
    sb("r3", (1, 2048), F32, 38.0)
    sb("Tb", (128, 4096), F32, 46.0)
    sb("Pb", (128, 4096), BF16, 62.0)
    sb("ost", (128, 1024), BF16, 70.0)
    sb("evA", (128, 2048), F32, 72.0)
    sb("evB", (128, 2048), F32, 80.0)
    sb("bfA", (128, 2048), BF16, 88.0)
    sb("bfB", (128, 2048), BF16, 92.0)
    sb("xslab", (128, 4096), BF16, 88.0)
    sb("xa", (128, 2048), F32, 96.0)
    sb("xb", (128, 2048), F32, 104.0)
    sb("l_sb", (16, 2048), F32, 112.0)
    sb("KT", (128, 2 * N), BF16, 120.0)
    sb("VR", (128, 64 * 260), BF16, 152.0)
    sb("wq_sb", (128, 8 * 768), BF16, 186.0)
    sb("qbuf", (128, 2 * 512), BF16, 198.0)
    sb("wo_sb", (128, 8 * 1024), BF16, 120.0)
    sb("kb", (128, 4 * 1024), BF16, 120.0)
    sb("qb2", (128, 4 * 512), BF16, 128.0)
    sb("s_mP0", (128, 4 * 512), F32, 132.0)
    sb("vti", (128, 2 * 520), BF16, 140.0)
    sb("Cbuf", (128, 2 * 512), F32, 142.5)
    sb("subT", (128, 2 * 1024), F32, 146.5)
    sb("Pi", (128, 2 * 1024), BF16, 154.5)
    sb("rot_sb", (128, 2 * 256), F32, 158.5)
    sb("fqkv_sb", (128, 8 * 1536), BF16, 164.0)
    sb("fwo_sb", (128, 4 * 1024), BF16, 188.0)
    sb("gated", (128, 22 * 1024), BF16, 120.0)
    sb("w1t", (128, 4 * 1024), BF16, 164.0)
    sb("w2u", (128, 2 * 1024), BF16, 180.0)
    sb("wst", (128, 2048), F32, 172.0)
    sb("lnout", (128, 8 * 1024), BF16, 96.0)

    ps = []
    for i in range(4):
        h = nc.alloc_psum_tensor(f"ps{i}", [128, 1024], F32)
        ps.append(h.ap())
    p.ps = ps
    p.banks = [(ps[i], 512 * hh) for i in range(4) for hh in range(2)]

    for nm in ("mm", "ev", "sc", "cc", "sw"):
        p.sem(nm)
    rings = {}
    for nm, ring in (("ld", 1), ("st", 4), ("gl", 4)):
        subs = [p.sem(f"{nm}{i}") for i in range(ring)]
        rings[nm] = RingCt(subs, nm)
    p.S = {"MM": p.sems["mm"], "EV": p.sems["ev"], "SC": p.sems["sc"],
           "CC": p.sems["cc"], "SW": p.sems["sw"],
           "LD": rings["ld"], "ST": rings["st"], "GL": rings["gl"]}

    build_init(p, debug)
    if UPTO >= 1:
        build_p0_rms(p, debug)
    if UPTO >= 2:
        build_p2_qkv(p, debug)
    if UPTO >= 3 and not SKIPP3:
        build_p3_attn(p, debug)
    elif UPTO >= 3:
        p.st_p3 = p.S["ST"].n
    if UPTO >= 4:
        build_p5_x1(p, debug)
    if UPTO >= 5:
        build_ffn_layer(p, 0, debug)
    if UPTO >= 6:
        build_ffn_layer(p, 1, debug)
    if UPTO >= 7:
        build_final(p, debug)
    for eng in ("sync", "gpsimd"):
        p.wait(eng, p.S["ST"])
        p.wait(eng, p.S["LD"])
        p.wait(eng, p.S["GL"])
        p.wait(eng, p.S["SW"])
    p.finalize()
    return nc


def build_init(p, debug):
    t, S = p.t, p.S
    LD, GL, EV = S["LD"], S["GL"], S["EV"]
    for nm, src in (("s_trip", "c_tri_pf"), ("s_trif", "c_tri_fp"),
                    ("s_id", "c_ident"), ("s_ones_r", "c_ones_r"),
                    ("s_mones_r", "c_mones_r"), ("s_ones_cf", "c_ones_c"),
                    ("s_e2", "c_e2"), ("s_m30", "c_m30"),
                    ("s_gp1", "gamma_row")):
        p.dma("sync", t[nm][:], t[src], LD)
    p.dma("gpsimd", t["s_ones_cb"][:], t["c_ones_c"], GL)
    p.dma("gpsimd", t["wq_sb"][:].rearrange("p (c f) -> p c f", f=768),
          t["wqkvT"].rearrange("(c p) f -> p c f", p=128), GL)
    p.wait("vector", LD)
    p.op("vector", lambda e: e.tensor_scalar(
        t["s_gp1"][:], t["s_gp1"][:], 32.0, 32.0, ALU.mult, ALU.add),
        [(EV, 1)])
    p.op("vector", lambda e: e.memset(
        t["VR"][:, :].rearrange("p (b h d) -> p b h d", h=4, d=65)
        [:, :, :, 64:65], 1.0), [(EV, 1)])
    p.op("vector", lambda e: e.memset(t["Tb"][:], 0.0), [(EV, 1)])
    p.ev_init = EV.n
    p.gl_init = GL.n


def build_p0_rms(p, debug):
    t, S, ps = p.t, p.S, p.ps
    LD, MM, EV, SC, ST, CC = (S[k] for k in ("LD", "MM", "EV", "SC", "ST", "CC"))
    xa, xb, evA, evB, bfA = t["xa"], t["xb"], t["evA"], t["evB"], t["bfA"]
    r0, r1 = t["r0"], t["r1"]
    ss_banks = [(ps[2], 0), (ps[2], 512), (ps[3], 0), (ps[3], 512)]

    sq_ev = []
    for c in range(8):
        xt = xa if c % 2 == 0 else xb
        sq = evA if c % 2 == 0 else evB
        if c >= 2:
            p.wait("sync", EV, sq_ev[c - 2])
        p.dma("sync", xt[:], t["xT"][128 * c:128 * c + 128, :], LD)
        p.wait("vector", LD, LD.n)
        if c >= 2:
            p.wait("vector", MM, MM.n)
        p.op("vector", lambda e, xt=xt, sq=sq: e.tensor_mul(sq[:], xt[:], xt[:]),
             [(EV, 1)])
        sq_ev.append(EV.n)
        p.wait("tensor", EV, EV.n)
        for j in range(4):
            bank, col = ss_banks[j]
            p.mm(bank[0:1, col:col + 512], t["s_ones_cf"][:],
                 sq[:, 512 * j:512 * j + 512], start=(c == 0), stop=(c == 7),
                 inc=MM)
    mm_ss = MM.n
    p.wait("vector", MM, mm_ss)
    for j in range(4):
        bank, col = ss_banks[j]
        p.op("vector", lambda e, b=bank, col=col, j=j: e.tensor_copy(
            r0[0:1, 512 * j:512 * j + 512], b[0:1, col:col + 512]), [])
    p.op("vector", lambda e: e.tensor_single_scalar(
        r0[:], r0[:], 1e-24, ALU.max), [(EV, 1)])
    p.wait("scalar", EV, EV.n)
    p.op("scalar", lambda e: e.activation(r1[:], r0[:], AF.Ln), [])
    p.op("scalar", lambda e: e.activation(
        r1[:], r1[:], AF.Exp, scale=-0.5), [(SC, 1)])

    p.wait("tensor", SC, SC.n)
    st_list = []
    ev_list = []
    for c in range(8):
        xt = xa if c % 2 == 0 else xb
        if c == 0:
            p.wait("sync", MM, mm_ss)
        if c >= 2:
            p.wait("sync", EV, ev_list[4 * (c - 2) + 3])
        p.dma("sync", xt[:], t["xT"][128 * c:128 * c + 128, :], LD)
        p.wait("tensor", LD, LD.n)
        for j in range(4):
            i = 4 * c + j
            bank = ps[0] if j % 2 == 0 else ps[1]
            if i >= 2:
                p.wait("tensor", EV, ev_list[i - 2])
            p.mm(bank[:, 0:512], t["s_gp1"][0:1, 128 * c:128 * c + 128],
                 r1[0:1, 512 * j:512 * j + 512], start=True, stop=True, inc=MM)
            p.wait("vector", MM, MM.n)
            if i >= 2:
                p.wait("vector", ST, st_list[i - 2])
            p.op("vector", lambda e, xt=xt, j=j, bank=bank: e.tensor_mul(
                bfA[:, 512 * (j % 2):512 * (j % 2) + 512],
                xt[:, 512 * j:512 * j + 512], bank[:, 0:512]), [(EV, 1)])
            ev_list.append(EV.n)
            p.wait("sync", EV, EV.n)
            p.dma("sync",
                  t["ag_xn_in"][128 * c:128 * c + 128, 512 * j:512 * j + 512],
                  bfA[:, 512 * (j % 2):512 * (j % 2) + 512], ST)
            st_list.append(ST.n)

    p.wait("gpsimd", ST, ST.n)
    p.op("gpsimd", lambda e: e.collective_compute(
        "AllGather", ALU.bypass, replica_groups=GROUPS4,
        ins=[t["ag_xn_in"]], outs=[t["ag_xn_out"]]), [(CC, 1)])
    p.cc_xn = CC.n


def build_p2_qkv(p, debug):
    t, S, ps = p.t, p.S, p.ps
    LD, GL, MM, EV, ST, CC = (S[k] for k in ("LD", "GL", "MM", "EV", "ST", "CC"))
    bfA, bfB = t["bfA"], t["bfB"]
    qk_dst = [(ps[0], 0), (ps[0], 512), (ps[1], 0), (ps[1], 512)]
    p.wait("sync", CC, p.cc_xn)
    p.wait("tensor", GL, p.gl_init)
    p.wait("tensor", EV, p.ev_init)

    mm_slot = []
    ev_tt = 0
    for tt in range(16):
        q4, col0 = tt // 4, 512 * (tt % 4)
        for c in range(8):
            li = tt * 8 + c
            slot = c % 4
            xnb = bfA if slot < 2 else bfB
            scol = 512 * (slot % 2)
            if li >= 4:
                p.wait("sync", MM, mm_slot[li - 4])
            p.dma("sync", xnb[:, scol:scol + 512],
                  t["ag_xn_out"][1024 * q4 + 128 * c:1024 * q4 + 128 * c + 128,
                                 col0:col0 + 512], LD)
            p.wait("tensor", LD, LD.n)
            if c == 0 and tt > 0:
                p.wait("tensor", EV, ev_tt)
            for f in range(4):
                bank, bcol = qk_dst[f]
                p.mm(bank[:, bcol:bcol + 512],
                     t["wq_sb"][:, 768 * c + 128 * f:768 * c + 128 * f + 128],
                     xnb[:, scol:scol + 512], start=(c == 0), stop=(c == 7))
            for tsub in range(4):
                p.mm(ps[2][:, 256 * tsub:256 * tsub + 256],
                     xnb[:, scol + 128 * tsub:scol + 128 * tsub + 128],
                     t["wq_sb"][:, 768 * c + 512:768 * c + 768],
                     start=(c == 0 and tsub % 2 == 0), stop=(c == 7),
                     inc=MM if tsub == 3 else None)
            mm_slot.append(MM.n)
        p.wait("vector", MM, MM.n)
        qst = t["Pb"]
        p.wait("vector", ST, ST.n)
        for f in range(2):
            bank, bcol = qk_dst[f]
            p.op("vector", lambda e, b=bank, bc=bcol, f=f: e.tensor_copy(
                qst[:, 512 * f:512 * f + 512], b[:, bc:bc + 512]), [(EV, 1)])
        for f in range(2, 4):
            bank, bcol = qk_dst[f]
            p.op("vector", lambda e, b=bank, bc=bcol, f=f, q4=q4, col0=col0:
                 e.tensor_copy(
                     t["KT"][:, N * (f - 2) + 2048 * q4 + col0:
                             N * (f - 2) + 2048 * q4 + col0 + 512],
                     b[:, bc:bc + 512]), [(EV, 1)])
        for tsub in range(4):
            tok = 4 * tt + tsub
            p.op("vector", lambda e, ts=tsub, tok=tok: e.tensor_copy(
                t["VR"][:, 260 * tok:260 * tok + 260].rearrange(
                    "p (h d) -> p h d", d=65)[:, :, 0:64],
                ps[2][:, 256 * ts:256 * ts + 256].rearrange(
                    "p (h d) -> p h d", d=64)), [(EV, 1)])
        ev_tt = EV.n
        p.wait("sync", EV, EV.n)
        for f in range(2):
            p.dma("sync", t["q_d"][128 * f:128 * f + 128,
                                   2048 * q4 + col0:2048 * q4 + col0 + 512],
                  qst[:, 512 * f:512 * f + 512], ST)
    p.st_p2 = ST.n
    p.ev_p2 = EV.n

    if debug:
        p.wait("gpsimd", ST, ST.n)
        p.wait("gpsimd", EV, EV.n)
        p.dma("gpsimd", t["dbg_xn"], t["ag_xn_out"], GL)
        p.dma("gpsimd", t["dbg_q"], t["q_d"], GL)
        for f in range(2):
            p.dma("gpsimd", t["dbg_k"][128 * f:128 * f + 128, :],
                  t["KT"][:, N * f:N * f + N], GL)
        p.dma("gpsimd", t["dbg_v"].rearrange("(b p) c -> p b c", p=128),
              t["VR"][:, :].rearrange("p (b c) -> p b c", c=260), GL)


def build_p3_attn(p, debug):
    t, S, ps = p.t, p.S, p.ps
    LD, MM, EV, SC, ST = (S[k] for k in ("LD", "MM", "EV", "SC", "ST"))
    KT, VR, qbuf, Tb, Pb, ost = (t[k] for k in
                                 ("KT", "VR", "qbuf", "Tb", "Pb", "ost"))
    p.wait("scalar", EV, p.ev_init)
    p.wait("sync", ST, p.st_p2)
    p.wait("scalar", ST, p.st_p2)
    p.wait("tensor", EV, p.ev_p2)
    smm_last = []
    ev_o_prev = 0
    for un in range(32):
        qt, hp = un // 2, un % 2
        qslot = un % 2
        if un >= 2:
            p.wait("sync", MM, smm_last[un - 2])
        p.dma("sync", qbuf[:, 512 * qslot:512 * qslot + 512],
              t["q_d"][128 * hp:128 * hp + 128, 512 * qt:512 * qt + 512], LD)
        p.wait("tensor", LD, LD.n)
        nkc = 4 * qt + 4
        tanh_sc = {}
        pv_mm = {}
        dsub_ev = {}
        for kc in range(nkc):
            j = kc - 4 * qt
            n0 = 128 * j if j >= 0 else 0
            Sps = ps[kc % 2]
            if kc >= 2:
                p.wait("tensor", SC, tanh_sc[kc - 2])
            p.mm(Sps[:, n0:512],
                 KT[0:64, N * hp + 128 * kc:N * hp + 128 * kc + 128],
                 qbuf[0:64, 512 * qslot + n0:512 * qslot + 512], True, True)
            p.mm(Sps[:, 512 + n0:1024],
                 KT[64:128, N * hp + 128 * kc:N * hp + 128 * kc + 128],
                 qbuf[64:128, 512 * qslot + n0:512 * qslot + 512], True, True,
                 inc=MM)
            tslot = kc % 4
            p.wait("scalar", MM, MM.n)
            p.op("scalar", lambda e, ts=tslot, Sps=Sps: e.activation(
                Tb[:, 1024 * ts:1024 * ts + 1024], Sps[:, 0:1024], AF.Tanh,
                scale=1.0 / 400.0), [(SC, 1)])
            tanh_sc[kc] = SC.n
            if j >= 0:
                p.wait("vector", SC, SC.n)
                for head in range(2):
                    p.op("vector", lambda e, ts=tslot, h=head, n0=n0:
                         e.tensor_add(
                             Tb[:, 1024 * ts + 512 * h + n0:
                                1024 * ts + 512 * h + n0 + 128],
                             Tb[:, 1024 * ts + 512 * h + n0:
                                1024 * ts + 512 * h + n0 + 128],
                             t["s_trif"][:]), [(EV, 1)])
                dsub_ev[kc] = EV.n
            if kc % 2 == 1 or kc == nkc - 1:
                pair = kc // 2
                pb = pair % 2
                base_kc = 2 * pair
                width = 1024 * (kc - base_kc + 1)
                for kk in range(base_kc, kc + 1):
                    if kk in dsub_ev:
                        p.wait("scalar", EV, dsub_ev[kk])
                if pair >= 2:
                    p.wait("scalar", MM, pv_mm[pair - 2])
                p.op("scalar", lambda e, pb=pb, b=base_kc, w=width:
                     e.activation(Pb[:, 2048 * pb:2048 * pb + w],
                                  Tb[:, 1024 * (b % 4):1024 * (b % 4) + w],
                                  AF.Exp, scale=50.0, bias=t["s_m30"][:]),
                     [(SC, 1)])
                p.wait("tensor", SC, SC.n)
                if pair == 0:
                    p.wait("tensor", EV, ev_o_prev)
                for kk in range(base_kc, kc + 1):
                    jj = kk - 4 * qt
                    nn0 = 128 * jj if jj >= 0 else 0
                    for head in range(2):
                        hloc = 2 * hp + head
                        p.mm(ps[2][0:65, 512 * head + nn0:512 * head + 512],
                             VR[:, 260 * kk + 65 * hloc:
                                260 * kk + 65 * hloc + 65],
                             Pb[:, 2048 * pb + 1024 * (kk - base_kc) +
                                512 * head + nn0:
                                2048 * pb + 1024 * (kk - base_kc) +
                                512 * head + 512],
                             start=(kk == 0), stop=(kk == nkc - 1),
                             inc=MM if (kk == kc and head == 1) else None)
                pv_mm[pair] = MM.n
            if PARANOID3:
                p.wait("tensor", EV, EV.n)
                p.wait("tensor", SC, SC.n)
                p.wait("scalar", EV, EV.n)
                p.wait("scalar", MM, MM.n)
                p.wait("vector", MM, MM.n)
                p.wait("vector", SC, SC.n)
                p.wait("sync", MM, MM.n)
        smm_last.append(MM.n)
        p.wait("vector", MM, MM.n)
        p.wait("vector", ST, ST.n)
        for head in range(2):
            p.op("vector", lambda e, h=head: e.tensor_copy(
                ost[0:65, 512 * h:512 * h + 512],
                ps[2][0:65, 512 * h:512 * h + 512]), [])
            p.op("vector", lambda e, h=head: e.tensor_copy(
                t["lst"][0:1, 512 * h:512 * h + 512],
                ps[2][64:65, 512 * h:512 * h + 512]), [(EV, 1)])
        ev_o_prev = EV.n
        p.wait("sync", EV, EV.n)
        for head in range(2):
            p.dma("sync", t["ag_o_in"][65 * (2 * hp + head):
                                       65 * (2 * hp + head) + 65,
                                       512 * qt:512 * qt + 512],
                  ost[0:65, 512 * head:512 * head + 512], ST)
            p.dma("sync", t["ag_l_in"][2 * hp + head:2 * hp + head + 1,
                                       512 * qt:512 * qt + 512],
                  t["lst"][0:1, 512 * head:512 * head + 512], ST)
    p.st_p3 = ST.n


def build_p5_x1(p, debug):
    t, S, ps = p.t, p.S, p.ps
    LD, GL, MM, EV, SC, ST, CC, SW = (S[k] for k in
                                      ("LD", "GL", "MM", "EV", "SC", "ST",
                                       "CC", "SW"))
    nc = p.nc
    p.wait("gpsimd", ST, p.st_p3)
    if debug:
        p.dma("gpsimd", t["dbg_oin"], t["ag_o_in"], GL)
        p.dma("gpsimd", t["dbg_lin"], t["ag_l_in"], GL)
        p.wait("gpsimd", GL, GL.n)
    p.op("gpsimd", lambda e: e.collective_compute(
        "AllGather", ALU.bypass, replica_groups=GROUPS4,
        ins=[t["ag_o_in"]], outs=[t["ag_o_out"]]), [(CC, 1)])
    p.op("gpsimd", lambda e: e.collective_compute(
        "AllGather", ALU.bypass, replica_groups=GROUPS4,
        ins=[t["ag_l_in"]], outs=[t["ag_l_out"]]), [(CC, 1)])
    p.wait("gpsimd", CC, CC.n)

    sw_base = SW.n

    def sw_omine(e):
        pid = nc.partition_id(engines=[ET.Pool])
        for arm in e.Switch(pid, 8):
            qq = arm % 4
            e.dma_start(out=t["o_mine"],
                        in_=t["ag_o_out"][:, 2048 * qq:2048 * qq + 2048]
                        ).then_inc(SW.h, 16)
            e.dma_start(out=t["l_d"],
                        in_=t["ag_l_out"][:, 2048 * qq:2048 * qq + 2048]
                        ).then_inc(SW.h, 16)
    p.raw("gpsimd", sw_omine)
    SW.n += 32

    p.wait("sync", SW, SW.n)
    p.dma("sync", t["l_sb"][:], t["l_d"], LD)
    p.wait("vector", LD, LD.n)
    p.op("vector", lambda e: e.reciprocal(t["l_sb"][:], t["l_sb"][:]), [(EV, 1)])
    p.wait("sync", EV, EV.n)
    p.dma("sync", t["linv_d"], t["l_sb"][:], ST)
    st_linv = ST.n
    p.dma("gpsimd", t["wo_sb"][:].rearrange("p (c f) -> p c f", f=1024),
          t["woutT"].rearrange("(c p) f -> p c f", p=128), GL)
    p.wait("tensor", GL, GL.n)
    p.wait("sync", ST, st_linv)

    bfA, bfB, Pb, Tb = t["bfA"], t["bfB"], t["Pb"], t["Tb"]
    xa, xb = t["xa"], t["xb"]
    r0, r1, r2, r3 = t["r0"], t["r1"], t["r2"], t["r3"]
    lr2 = t["lr2"]

    for tt in range(4):
        ev_norm = []
        for fc in range(8):
            oslot = fc % 2
            if fc >= 2:
                p.wait("sync", EV, ev_norm[fc - 2])
            p.dma("sync", bfA[0:64, 512 * oslot:512 * oslot + 512],
                  t["o_mine"][65 * 2 * fc:65 * 2 * fc + 64,
                              512 * tt:512 * tt + 512], LD)
            p.dma("sync", bfA[64:128, 512 * oslot:512 * oslot + 512],
                  t["o_mine"][65 * (2 * fc + 1):65 * (2 * fc + 1) + 64,
                              512 * tt:512 * tt + 512], LD)
            p.wait("sync", MM, MM.n)
            p.dma("sync", lr2[:], t["linv_d"][2 * fc:2 * fc + 2,
                                              512 * tt:512 * tt + 512], LD)
            p.wait("tensor", LD, LD.n)
            if fc >= 1 or tt >= 1:
                p.wait("tensor", EV, EV.n)
            p.mm(ps[3][:, 0:512], t["s_e2"][:], lr2[:], True, True, inc=MM)
            p.wait("vector", MM, MM.n)
            p.wait("vector", LD, LD.n)
            p.op("vector", lambda e, fc=fc, osl=oslot: e.tensor_mul(
                Pb[:, 512 * fc:512 * fc + 512],
                bfA[:, 512 * osl:512 * osl + 512], ps[3][:, 0:512]), [(EV, 1)])
            ev_norm.append(EV.n)
        p.wait("tensor", EV, EV.n)
        p.wait("vector", ST, ST.n)
        ev_t1 = []
        for ddc in range(8):
            dst = ps[ddc % 2][:, 512 * ((ddc // 2) % 2):
                              512 * ((ddc // 2) % 2) + 512]
            if ddc >= 4:
                p.wait("tensor", EV, ev_t1[ddc - 4])
            for fc in range(8):
                p.mm(dst, t["wo_sb"][:, 1024 * fc + 128 * ddc:
                                     1024 * fc + 128 * ddc + 128],
                     Pb[:, 512 * fc:512 * fc + 512],
                     start=(fc == 0), stop=(fc == 7),
                     inc=MM if fc == 7 else None)
            xt = xa if ddc % 2 == 0 else xb
            if ddc >= 2:
                p.wait("sync", EV, ev_t1[ddc - 2])
            p.dma("sync", xt[:, 0:512],
                  t["xT"][128 * ddc:128 * ddc + 128, 512 * tt:512 * tt + 512],
                  LD)
            p.wait("vector", MM, MM.n)
            p.wait("vector", LD, LD.n)
            p.op("vector", lambda e, d=ddc, dst=dst, xt=xt: e.tensor_add(
                Tb[:, 512 * d:512 * d + 512], dst, xt[:, 0:512]), [])
            p.op("vector", lambda e, d=ddc: e.tensor_copy(
                bfB[:, 512 * (d % 4):512 * (d % 4) + 512],
                Tb[:, 512 * d:512 * d + 512]), [])
            p.op("vector", lambda e, d=ddc: e.tensor_mul(
                bfA[:, 512 * (d % 4):512 * (d % 4) + 512],
                bfB[:, 512 * (d % 4):512 * (d % 4) + 512],
                bfB[:, 512 * (d % 4):512 * (d % 4) + 512]), [(EV, 1)])
            ev_t1.append(EV.n)
            p.wait("tensor", EV, EV.n)
            p.mm(ps[2][0:1, 0:512], t["s_ones_cb"][:],
                 bfB[:, 512 * (ddc % 4):512 * (ddc % 4) + 512],
                 start=(ddc == 0), stop=(ddc == 7))
            p.mm(ps[2][0:1, 512:1024], t["s_ones_cb"][:],
                 bfA[:, 512 * (ddc % 4):512 * (ddc % 4) + 512],
                 start=(ddc == 0), stop=(ddc == 7), inc=MM)
        if debug and tt == 0:
            p.wait("gpsimd", EV, EV.n)
            p.dma("gpsimd", t["dbg_onm"], t["Pb"][:, :], GL)
            p.dma("gpsimd", t["dbg_t1"], t["Tb"][:, :], GL)
            p.dma("gpsimd", t["dbg_linv"], t["l_sb"][:], GL)
            gl_tap = GL.n
        _ln_rows_512(p)
        p.wait("tensor", EV, EV.n)
        p.mm(ps[2][:, 0:512], t["s_ones_r"][:], r2[0:1, 0:512], True, True)
        p.mm(ps[2][:, 512:1024], t["s_ones_r"][:], r3[0:1, 0:512], True, True,
             inc=MM)
        p.wait("vector", MM, MM.n)
        if debug and tt == 0:
            p.wait("vector", GL, gl_tap)
        st_xe = {}
        for ddc in range(8):
            p.op("vector", lambda e, d=ddc: e.tensor_mul(
                Tb[:, 512 * d:512 * d + 512], Tb[:, 512 * d:512 * d + 512],
                ps[2][:, 0:512]), [])
            p.op("vector", lambda e, d=ddc: e.tensor_add(
                Tb[:, 512 * d:512 * d + 512], Tb[:, 512 * d:512 * d + 512],
                ps[2][:, 512:1024]), [])
            if ddc >= 4:
                p.wait("vector", ST, st_xe[ddc - 4])
            p.op("vector", lambda e, d=ddc: e.tensor_copy(
                bfB[:, 512 * (d % 4):512 * (d % 4) + 512],
                Tb[:, 512 * d:512 * d + 512]), [(EV, 1)])
            p.wait("sync", EV, EV.n)
            p.dma("sync", t["x1_full"][128 * ddc:128 * ddc + 128,
                                       512 * tt:512 * tt + 512],
                  Tb[:, 512 * ddc:512 * ddc + 512], ST)
            p.dma("sync", t["xe_d"][128 * ddc:128 * ddc + 128,
                                    HALO + 512 * tt:HALO + 512 * tt + 512],
                  bfB[:, 512 * (ddc % 4):512 * (ddc % 4) + 512], ST)
            st_xe[ddc] = ST.n
            u0 = max(0, 1022 - 512 * tt)
            if u0 < 512:
                p.dma("sync",
                      t["ag_x1_in"][128 * ddc:128 * ddc + 128,
                                    512 * tt + u0 - 1022:
                                    512 * tt + 512 - 1022],
                      Tb[:, 512 * ddc + u0:512 * ddc + 512], ST)
        p.wait("tensor", EV, EV.n)

    p.wait("gpsimd", ST, ST.n)
    p.op("gpsimd", lambda e: e.collective_compute(
        "AllGather", ALU.bypass, replica_groups=GROUPS4,
        ins=[t["ag_x1_in"]], outs=[t["ag_x1_out"]]), [(CC, 1)])
    p.wait("gpsimd", CC, CC.n)
    p.op("vector", lambda e: e.memset(t["evA"][:, 0:HALO], 0.0), [(EV, 1)])
    p.wait("gpsimd", EV, EV.n)
    _emit_halo_switch(p, "x1")
    if debug:
        p.wait("gpsimd", ST, ST.n)
        p.wait("gpsimd", SW, SW.n)
        p.dma("gpsimd", t["dbg_o"], t["o_mine"], GL)
        p.dma("gpsimd", t["dbg_x1"], t["x1_full"], GL)


def _ln_rows_512(p):
    t, S, ps = p.t, p.S, p.ps
    MM, EV, SC = S["MM"], S["EV"], S["SC"]
    r0, r1, r2, r3 = t["r0"], t["r1"], t["r2"], t["r3"]
    p.wait("vector", MM, MM.n)
    p.op("vector", lambda e: e.tensor_scalar_mul(
        r0[0:1, 0:512], ps[2][0:1, 0:512], 1.0 / 1024.0), [])
    p.op("vector", lambda e: e.tensor_scalar_mul(
        r1[0:1, 0:512], ps[2][0:1, 512:1024], 1.0 / 1024.0), [])
    p.op("vector", lambda e: e.tensor_mul(
        r2[0:1, 0:512], r0[0:1, 0:512], r0[0:1, 0:512]), [])
    p.op("vector", lambda e: e.tensor_sub(
        r1[0:1, 0:512], r1[0:1, 0:512], r2[0:1, 0:512]), [])
    p.op("vector", lambda e: e.tensor_scalar_add(
        r1[0:1, 0:512], r1[0:1, 0:512], 1e-5), [(EV, 1)])
    p.wait("scalar", EV, EV.n)
    p.op("scalar", lambda e: e.activation(
        r2[0:1, 0:512], r1[0:1, 0:512], AF.Ln), [])
    p.op("scalar", lambda e: e.activation(
        r2[0:1, 0:512], r2[0:1, 0:512], AF.Exp, scale=-0.5), [(SC, 1)])
    p.wait("vector", SC, SC.n)
    p.op("vector", lambda e: e.tensor_mul(
        r3[0:1, 0:512], r0[0:1, 0:512], r2[0:1, 0:512]), [])
    p.op("vector", lambda e: e.tensor_scalar_mul(
        r3[0:1, 0:512], r3[0:1, 0:512], -1.0), [(EV, 1)])


def _emit_halo_switch(p, nm):
    t, S = p.t, p.S
    SW = S["SW"]
    nc = p.nc
    ext = t["xe_d"] if nm == "x1" else t["ye_d"]

    base = SW.n

    def fn(e):
        pid = nc.partition_id(engines=[ET.Pool])
        for arm in e.Switch(pid, 8):
            qq = arm % 4
            k = 0
            for c in range(8):
                if qq == 0:
                    src = t["evA"][:, 0:HALO]
                else:
                    src = t[f"ag_{nm}_out"][1024 * (qq - 1) + 128 * c:
                                            1024 * (qq - 1) + 128 * c + 128, :]
                e.dma_start(out=t[f"{nm}_halo"][128 * c:128 * c + 128, :],
                            in_=src).then_inc(SW.h, 16)
                e.dma_start(out=ext[128 * c:128 * c + 128, 0:HALO],
                            in_=src).then_inc(SW.h, 16)
    p.raw("gpsimd", fn)
    SW.n += 16 * 16


def build_ffn_layer(p, li, debug):
    t, S, ps = p.t, p.S, p.ps
    LD, GL, MM, EV, SC, ST, CC, SW = (S[k] for k in
                                      ("LD", "GL", "MM", "EV", "SC", "ST",
                                       "CC", "SW"))
    in_ext = t["xe_d"] if li == 0 else t["ye_d"]
    in_halo = t["x1_halo"] if li == 0 else t["y1_halo"]
    in_full = t["x1_full"] if li == 0 else t["y1_full"]
    out_full = t["y1_full"] if li == 0 else t["y2_full"]
    bfA, bfB, ost = t["bfA"], t["bfB"], t["ost"]
    evA, evB, Tb, Pb = t["evA"], t["evB"], t["Tb"], t["Pb"]
    banks = p.banks

    p.wait("gpsimd", SW, SW.n)
    p.wait("sync", SW, SW.n)
    p.wait("sync", ST, ST.n)
    if debug and li == 0:
        p.wait("gpsimd", ST, ST.n)
        p.dma("gpsimd", t["dbg_xe"], in_ext, GL)
    p.dma("gpsimd", t["fqkv_sb"][:].rearrange("p (c f) -> p c f", f=1536),
          t[f"fwqkv{li}"].rearrange("(c p) f -> p c f", p=128), GL)
    p.dma("gpsimd", t["fwo_sb"][:].rearrange("p (c f) -> p c f", f=1024),
          t[f"fwo{li}"].rearrange("(c p) f -> p c f", p=128), GL)
    p.dma("sync", t["qs_r"][:], t[f"qs_row{li}"], LD)
    p.dma("sync", t["ks_r"][:], t[f"ks_row{li}"], LD)
    p.dma("sync", t["s_mP0"][:].rearrange("p (a q) -> p a q", q=512),
          t["mP0"].rearrange("(a p) q -> p a q", p=128), LD)
    p.dma("sync", t["s_lng"][:], t[f"lng{li}"], LD)
    p.dma("sync", t["s_lnb"][:], t[f"lnb{li}"], LD)
    p.dma("sync", t["s_b2c"][:], t[f"b2c{li}"], LD)
    p.dma("sync", t["s_b1a"][:], t[f"b1a{li}"], LD)
    p.dma("sync", t["s_b1g"][:], t[f"b1g{li}"], LD)
    p.wait("tensor", GL, GL.n)
    p.wait("tensor", LD, LD.n)
    for nm_r, nm_b in (("qs_r", "s_qsb"), ("ks_r", "s_ksb")):
        p.mm(ps[3][:, 0:64], t["s_ones_r"][:], t[nm_r][:], True, True, inc=MM)
        p.wait("vector", MM, MM.n)
        p.op("vector", lambda e, nb=nm_b: e.tensor_copy(
            t[nb][:], ps[3][:, 0:64]), [(EV, 1)])
        p.wait("tensor", EV, EV.n)

    ev_slots = 0
    mm_v_prev = 0
    for tt in range(6):
        if tt > 0:
            p.wait("sync", MM, mm_v_prev)
        for hf in range(2):
            coff = (2 if hf == 0 else 1) + 512 * tt
            p.dma("sync",
                  t["xslab"][:, 2048 * hf:2048 * hf + 2048].rearrange(
                      "p (c f) -> p c f", f=512),
                  in_ext[512 * hf:512 * hf + 512, coff:coff + 512].rearrange(
                      "(c p) f -> p c f", p=128), LD)
        p.wait("tensor", LD, LD.n)
        if tt > 0:
            p.wait("tensor", EV, ev_slots)
        for grp in range(2):
            for tsub in range(4):
                bank, bcol = banks[4 * grp + tsub]
                for c in range(8):
                    xslab = bfA if c < 4 else bfB
                    scol = 512 * (c % 4)
                    p.mm(bank[:, bcol:bcol + 512],
                         xslab[:, scol + 128 * tsub:scol + 128 * tsub + 128],
                         t["fqkv_sb"][:, 1536 * c + 512 * grp:
                                      1536 * c + 512 * grp + 512],
                         start=(c == 0), stop=(c == 7),
                         inc=MM if c == 7 else None)
        for tsub in range(4):
            tok = 4 * tt + tsub
            rsl = tsub % 2
            if tt + tsub > 0:
                p.wait("sync", EV, EV.n)
            p.dma("sync", t["rot_sb"][:, 256 * rsl:256 * rsl + 256],
                  t["rotq"][128 * tok:128 * tok + 128, :], LD)
            for grp in range(2):
                bank, bcol = banks[4 * grp + tsub]
                qsb = t["s_qsb"] if grp == 0 else t["s_ksb"]
                c1o = 256 * rsl + 128 * grp
                c2o = c1o + 64
                p.wait("vector", MM, MM.n)
                p.wait("vector", LD, LD.n)
                p.op("vector", lambda e, b=bank, bc=bcol: e.tensor_copy(
                    evA[:, 0:512], b[:, bc:bc + 512]), [])
                p.op("vector", lambda e: e.tensor_mul(
                    evB[:, 0:512], evA[:, 0:512], evA[:, 0:512]), [(EV, 1)])
                p.wait("vector", EV, EV.n)
                snap = debug and tt == 2 and tsub == 0 and grp == 0 and li == 0
                if snap:
                    p.op("vector", lambda e: e.tensor_copy(
                        t["xb"][:, 0:512], evA[:, 0:512]), [])
                p.op("vector", lambda e: e.reduce_sum(
                    t["mcol"][:, 0:8].rearrange("p h -> p h ()"),
                    evB[:, 0:512].rearrange("p (h d) -> p h d", d=64),
                    axis=AX.X), [(EV, 1)])
                p.wait("vector", EV, EV.n)
                if snap:
                    p.op("vector", lambda e: e.tensor_copy(
                        t["xb"][:, 512:520], t["mcol"][:, 0:8]), [])
                p.op("vector", lambda e: e.tensor_single_scalar(
                    t["mcol"][:, 0:8], t["mcol"][:, 0:8], 1e-24, ALU.max),
                    [(EV, 1)])
                p.wait("scalar", EV, EV.n)
                p.op("scalar", lambda e: e.activation(
                    t["mcol"][:, 0:8], t["mcol"][:, 0:8], AF.Ln), [])
                p.op("scalar", lambda e: e.activation(
                    t["mcol"][:, 0:8], t["mcol"][:, 0:8], AF.Exp, scale=-0.5),
                    [(SC, 1)])
                p.wait("vector", SC, SC.n)
                if snap:
                    p.op("vector", lambda e: e.tensor_copy(
                        t["xb"][:, 520:528], t["mcol"][:, 0:8]), [])
                p.op("vector", lambda e: e.tensor_mul(
                    brep(evA[:, 0:512], [[1, 64], [64, 8]]),
                    brep(evA[:, 0:512], [[1, 64], [64, 8]]),
                    brep(t["mcol"][:, 0:8], [[0, 64], [1, 8]])), [])
                if snap:
                    p.op("vector", lambda e: e.tensor_copy(
                        t["xb"][:, 528:1040], evA[:, 0:512]), [])
                p.op("vector", lambda e, qsb=qsb: e.tensor_mul(
                    evA[:, 0:512].rearrange("p (h d) -> p h d", d=64),
                    evA[:, 0:512].rearrange("p (h d) -> p h d", d=64),
                    brep(qsb[:, 0:64], [[0, 8], [1, 64]])), [])
                if snap:
                    p.op("vector", lambda e: e.tensor_copy(
                        t["xb"][:, 1040:1552], evA[:, 0:512]), [])
                p.op("vector", lambda e, rsl=rsl, c1o=c1o: e.tensor_mul(
                    evB[:, 0:512].rearrange("p (h d) -> p h d", d=64),
                    evA[:, 0:512].rearrange("p (h d) -> p h d", d=64),
                    brep(t["rot_sb"][:, c1o:c1o + 64], [[0, 8], [1, 64]])), [])
                p.op("vector", lambda e, c2o=c2o: e.tensor_mul(
                    Tb[:, 0:512].rearrange("p (h d) -> p h d", d=64)[:, :, 0:32],
                    evA[:, 0:512].rearrange("p (h d) -> p h d", d=64)[:, :, 32:64],
                    brep(t["rot_sb"][:, c2o:c2o + 32], [[0, 8], [1, 32]])), [])
                p.op("vector", lambda e, c2o=c2o: e.tensor_mul(
                    Tb[:, 0:512].rearrange("p (h d) -> p h d", d=64)[:, :, 32:64],
                    evA[:, 0:512].rearrange("p (h d) -> p h d", d=64)[:, :, 0:32],
                    brep(t["rot_sb"][:, c2o + 32:c2o + 64], [[0, 8], [1, 32]])),
                    [])
                p.op("vector", lambda e: e.tensor_add(
                    evB[:, 0:512], evB[:, 0:512], Tb[:, 0:512]), [(EV, 1)])
                p.wait("tensor", EV, EV.n)
                for hh2 in range(4):
                    p.op("tensor", lambda e, b=bank, bc=bcol, hh2=hh2:
                         e.transpose(b[:, bc + 128 * hh2:bc + 128 * hh2 + 128],
                                     evB[:, 128 * hh2:128 * hh2 + 128],
                                     t["s_id"][:]),
                         [(MM, 1)] if hh2 == 3 else [])
                p.wait("vector", MM, MM.n)
                pslot = 2 * tsub + grp
                p.wait("vector", ST, p.pb_st.get(pslot, 0))
                p.op("vector", lambda e, b=bank, bc=bcol, psl=pslot:
                     e.tensor_copy(Pb[:, 512 * psl:512 * psl + 512],
                                   b[:, bc:bc + 512]), [(EV, 1)])
                p.wait("sync", EV, EV.n)
                for hh2 in range(4):
                    p.dma("sync",
                          t["qkv_d"][512 * grp + 128 * hh2:
                                     512 * grp + 128 * hh2 + 128,
                                     128 * tok:128 * tok + 128],
                          Pb[:, 512 * pslot + 128 * hh2:
                             512 * pslot + 128 * hh2 + 128], ST)
                p.pb_st[pslot] = ST.n
        ev_slots = EV.n
        p.wait("tensor", EV, EV.n)
        for tsub in range(4):
            bank, bcol = banks[tsub]
            for c in range(8):
                xslab = bfA if c < 4 else bfB
                scol = 512 * (c % 4)
                p.mm(bank[:, bcol:bcol + 512],
                     xslab[:, scol + 128 * tsub:scol + 128 * tsub + 128],
                     t["fqkv_sb"][:, 1536 * c + 1024:1536 * c + 1536],
                     start=(c == 0), stop=(c == 7),
                     inc=MM if c == 7 else None)
        mm_v_prev = MM.n
        p.wait("vector", MM, MM.n)
        for tsub in range(4):
            bank, bcol = banks[tsub]
            p.wait("vector", ST, ST.n)
            p.op("vector", lambda e: e.memset(
                ost[:, 0:520].rearrange("p (h d) -> p h d", d=65)
                [:, :, 64:65], 1.0), [(EV, 1)])
            p.op("vector", lambda e, b=bank, bc=bcol: e.tensor_copy(
                ost[:, 0:520].rearrange("p (h d) -> p h d", d=65)[:, :, 0:64],
                b[:, bc:bc + 512].rearrange("p (h d) -> p h d", d=64)),
                [(EV, 1)])
            p.wait("sync", EV, EV.n)
            tok = 4 * tt + tsub
            p.dma("sync", t["v_d"][128 * tok:128 * tok + 128, :],
                  ost[:, 0:520], ST)
        ev_slots = EV.n
    st_phaseA = ST.n
    if debug and li == 0:
        p.wait("gpsimd", ST, st_phaseA)
        p.dma("gpsimd", t["dbg_qkv"], t["qkv_d"], GL)
        p.dma("gpsimd", t["dbg_v2"], t["v_d"], GL)
        p.wait("gpsimd", EV, EV.n)
        p.dma("gpsimd", t["dbg_ph"], t["xb"], GL)
    if FSUB <= 1:
        return

    p.wait("sync", ST, st_phaseA)
    kb, qb2, vti, Pi, subT, Cbuf = (t[k] for k in
                                    ("kb", "qb2", "vti", "Pi", "subT", "Cbuf"))
    mrow = t["r3"][0:1, 0:512]
    ev_oD = 0
    qm_ev = 0
    mm_wi_end = 0
    for wi in range(1, 6):
        if wi > 1:
            p.wait("sync", MM, mm_wi_end)
        for fc in range(4):
            p.dma("sync", kb[:, 1024 * fc:1024 * fc + 1024],
                  t["qkv_d"][512 + 128 * fc:512 + 128 * fc + 128,
                             512 * (wi - 1):512 * (wi - 1) + 1024], LD)
            p.dma("sync", qb2[:, 512 * fc:512 * fc + 512],
                  t["qkv_d"][128 * fc:128 * fc + 128,
                             512 * wi:512 * wi + 512], LD)
        p.wait("tensor", LD, LD.n)
        for hp in range(4):
            for hh in range(2):
                base = 64 * hh
                for qs in range(4):
                    if qm_ev:
                        p.wait("tensor", EV, qm_ev)
                    p.mm(ps[2][:, 128 * qs:512],
                         qb2[base:base + 64,
                             512 * hp + 128 * qs:512 * hp + 128 * qs + 128],
                         kb[base:base + 64,
                            1024 * hp + 128 * qs:1024 * hp + 512], True, True)
                    p.mm(ps[2][:, 512:640 + 128 * qs],
                         qb2[base:base + 64,
                             512 * hp + 128 * qs:512 * hp + 128 * qs + 128],
                         kb[base:base + 64,
                            1024 * hp + 512:1024 * hp + 640 + 128 * qs],
                         True, True, inc=MM)
                    p.wait("vector", MM, MM.n)
                    p.op("vector", lambda e, qs=qs: e.tensor_add(
                        ps[2][:, 128 * qs:128 * qs + 128],
                        ps[2][:, 128 * qs:128 * qs + 128], t["s_trif"][:]), [])
                    p.op("vector", lambda e, qs=qs: e.tensor_add(
                        ps[2][:, 512 + 128 * qs:512 + 128 * qs + 128],
                        ps[2][:, 512 + 128 * qs:512 + 128 * qs + 128],
                        t["s_trip"][:]), [(EV, 1)])
                    if wi == 2:
                        p.op("vector", lambda e, qs=qs: e.tensor_add(
                            ps[2][:, 128 * qs:512], ps[2][:, 128 * qs:512],
                            t["s_mP0"][:, 512 * qs + 128 * qs:
                                       512 * qs + 512]), [(EV, 1)])
                    p.wait("vector", EV, EV.n)
                    p.op("vector", lambda e, qs=qs: e.reduce_max(
                        t["mcol"][:, qs:qs + 1],
                        ps[2][:, 128 * qs:640 + 128 * qs], axis=AX.X),
                        [(EV, 1)])
                    qm_ev = EV.n
                p.wait("tensor", EV, qm_ev)
                p.wait("tensor", EV, ev_oD)
                for qs in range(4):
                    p.op("tensor", lambda e, qs=qs: e.transpose(
                        ps[3][0:1, 512 + 128 * qs:512 + 128 * qs + 128],
                        t["mcol"][:, qs:qs + 1], t["s_id"][:]),
                        [(MM, 1)] if qs == 3 else [])
                p.wait("vector", MM, MM.n)
                p.op("vector", lambda e: e.tensor_copy(
                    mrow, ps[3][0:1, 512:1024]), [(EV, 1)])
                p.wait("tensor", EV, EV.n)
                p.mm(ps[3][:, 512:1024], t["s_mones_r"][:], mrow, True, True,
                     inc=MM)
                p.wait("vector", MM, MM.n)
                p.op("vector", lambda e, hh=hh: e.tensor_copy(
                    Cbuf[:, 512 * hh:512 * hh + 512], ps[3][:, 512:1024]),
                    [(EV, 1)])
                p.wait("tensor", EV, EV.n)
            sub_ev = {}
            pv_mm = {}
            for kc in range(8):
                Sps = ps[kc % 2]
                if kc >= 2:
                    p.wait("tensor", EV, sub_ev[kc - 2])
                if kc < 4:
                    q0, q1 = 0, 128 * (kc + 1)
                else:
                    q0, q1 = 128 * (kc - 4), 512
                for hh in range(2):
                    base = 64 * hh
                    p.mm(Sps[:, 512 * hh + q0:512 * hh + q1],
                         kb[base:base + 64,
                            1024 * hp + 128 * kc:1024 * hp + 128 * kc + 128],
                         qb2[base:base + 64, 512 * hp + q0:512 * hp + q1],
                         True, True, inc=MM if hh == 1 else None)
                p.wait("vector", MM, MM.n)
                for hh in range(2):
                    so = 1024 * (kc % 2) + 512 * hh
                    p.op("vector", lambda e, so=so, q0=q0, q1=q1, hh=hh,
                         Sps=Sps: e.tensor_add(
                             subT[:, so + q0:so + q1],
                             Sps[:, 512 * hh + q0:512 * hh + q1],
                             Cbuf[:, 512 * hh + q0:512 * hh + q1]), [])
                    cpos = 128 * kc if kc < 4 else 128 * (kc - 4)
                    tri = t["s_trip"] if kc < 4 else t["s_trif"]
                    p.op("vector", lambda e, so=so, cp=cpos, tri=tri:
                         e.tensor_add(subT[:, so + cp:so + cp + 128],
                                      subT[:, so + cp:so + cp + 128],
                                      tri[:]), [])
                    if wi == 2 and kc < 4:
                        p.op("vector", lambda e, so=so, kc=kc, q0=q0, q1=q1:
                             e.tensor_add(
                                 subT[:, so + q0:so + q1],
                                 subT[:, so + q0:so + q1],
                                 t["s_mP0"][:, 512 * kc + q0:512 * kc + q1]),
                             [])
                    if hh == 1:
                        p.op("vector", lambda e, so=so: e.tensor_copy(
                            subT[:, so:so + 1], subT[:, so:so + 1]),
                            [(EV, 1)])
                sub_ev[kc] = EV.n
                p.wait("scalar", EV, sub_ev[kc])
                if kc >= 2:
                    p.wait("scalar", MM, pv_mm[kc - 2])
                p.op("scalar", lambda e, kc=kc: e.activation(
                    Pi[:, 1024 * (kc % 2):1024 * (kc % 2) + 1024],
                    subT[:, 1024 * (kc % 2):1024 * (kc % 2) + 1024],
                    AF.Exp), [(SC, 1)])
                if kc >= 2:
                    p.wait("sync", MM, pv_mm[kc - 2])
                tokb = 512 * (wi - 1) + 128 * kc
                p.dma("sync", vti[:, 520 * (kc % 2):520 * (kc % 2) + 520],
                      t["v_d"][tokb:tokb + 128, :], LD)
                p.wait("tensor", SC, SC.n)
                p.wait("tensor", LD, LD.n)
                if kc == 0:
                    p.wait("tensor", EV, ev_oD)
                for hh in range(2):
                    h = 2 * hp + hh
                    p.mm(ps[3][0:65, 512 * hh + q0:512 * hh + q1],
                         vti[:, 520 * (kc % 2) + 65 * h:
                             520 * (kc % 2) + 65 * h + 65],
                         Pi[:, 1024 * (kc % 2) + 512 * hh + q0:
                            1024 * (kc % 2) + 512 * hh + q1],
                         start=(kc == 0), stop=(kc == 7),
                         inc=MM if hh == 1 else None)
                pv_mm[kc] = MM.n
            p.wait("vector", MM, MM.n)
            p.wait("vector", ST, ST.n)
            for hh in range(2):
                p.op("vector", lambda e, hh=hh: e.tensor_copy(
                    ost[0:64, 512 * hh:512 * hh + 512],
                    ps[3][0:64, 512 * hh:512 * hh + 512]), [])
                p.op("vector", lambda e, hh=hh: e.tensor_copy(
                    t["lst"][0:1, 512 * hh:512 * hh + 512],
                    ps[3][64:65, 512 * hh:512 * hh + 512]), [(EV, 1)])
            ev_oD = EV.n
            p.wait("sync", EV, EV.n)
            il_flat = t["il_row"].rearrange("a b -> (a b) ()").rearrange(
                "(x f) y -> x (f y)", x=1)
            for hh in range(2):
                h = 2 * hp + hh
                p.dma("sync", t["o_d"][65 * h:65 * h + 64,
                                       512 * (wi - 1):512 * (wi - 1) + 512],
                      ost[0:64, 512 * hh:512 * hh + 512], ST)
                i0 = 512 * (8 * (wi - 1) + h)
                p.dma("sync", il_flat[0:1, i0:i0 + 512],
                      t["lst"][0:1, 512 * hh:512 * hh + 512], ST)
        mm_wi_end = MM.n
    st_attn = ST.n
    if FSUB <= 2:
        if debug and li == 0:
            p.wait("gpsimd", ST, st_attn)
            p.dma("gpsimd", t["dbg_oi"], t["o_d"], GL)
            p.dma("gpsimd", t["dbg_il"], t["il_row"], GL)
        return

    p.wait("sync", ST, st_attn)
    p.dma("sync", t["li_sb"][:], t["il_row"], LD)
    p.wait("vector", LD, LD.n)
    p.op("vector", lambda e: e.reciprocal(t["li_sb"][:], t["li_sb"][:]),
         [(EV, 1)])
    p.wait("sync", EV, EV.n)
    p.dma("sync", t["il_row"], t["li_sb"][:], ST)
    p.wait("sync", ST, ST.n)
    il_flat = t["il_row"].rearrange("a b -> (a b) ()").rearrange(
        "(x f) y -> x (f y)", x=1)
    lr2 = t["lr2"]
    xa, xb = t["xa"], t["xb"]
    for wi in range(1, 6):
        ev_onorm = []
        for hp in range(4):
            oslot = hp % 2
            if hp >= 2:
                p.wait("sync", EV, ev_onorm[hp - 2])
            p.dma("sync", bfA[0:64, 512 * oslot:512 * oslot + 512],
                  t["o_d"][65 * 2 * hp:65 * 2 * hp + 64,
                           512 * (wi - 1):512 * (wi - 1) + 512], LD)
            p.dma("sync", bfA[64:128, 512 * oslot:512 * oslot + 512],
                  t["o_d"][65 * (2 * hp + 1):65 * (2 * hp + 1) + 64,
                           512 * (wi - 1):512 * (wi - 1) + 512], LD)
            i0 = 512 * (8 * (wi - 1) + 2 * hp)
            p.wait("sync", MM, MM.n)
            p.dma("sync", lr2[0:1, :], il_flat[0:1, i0:i0 + 512], LD)
            p.dma("sync", lr2[1:2, :], il_flat[0:1, i0 + 512:i0 + 1024], LD)
            p.wait("tensor", LD, LD.n)
            if hp >= 1 or wi > 1:
                p.wait("tensor", EV, EV.n)
            p.mm(ps[3][:, 0:512], t["s_e2"][:], lr2[:], True, True, inc=MM)
            p.wait("vector", MM, MM.n)
            p.op("vector", lambda e, hp=hp, osl=oslot: e.tensor_mul(
                Pb[:, 512 * hp:512 * hp + 512],
                bfA[:, 512 * osl:512 * osl + 512], ps[3][:, 0:512]),
                [(EV, 1)])
            ev_onorm.append(EV.n)
        p.wait("tensor", EV, EV.n)
        p.wait("vector", ST, ST.n)
        ev_x = []
        for ddc in range(8):
            dst = ps[ddc % 2][:, 512 * ((ddc // 2) % 2):
                              512 * ((ddc // 2) % 2) + 512]
            if ddc >= 4:
                p.wait("tensor", EV, ev_x[ddc - 4])
            for hp in range(4):
                p.mm(dst, t["fwo_sb"][:, 1024 * hp + 128 * ddc:
                                      1024 * hp + 128 * ddc + 128],
                     Pb[:, 512 * hp:512 * hp + 512],
                     start=(hp == 0), stop=(hp == 3),
                     inc=MM if hp == 3 else None)
            xt = xa if ddc % 2 == 0 else xb
            if wi == 1:
                p.dma("sync", xt[:, 0:1],
                      in_halo[128 * ddc:128 * ddc + 128, HALO - 1:HALO], LD,
                      slow=True)
            else:
                p.dma("sync", xt[:, 0:512],
                      in_full[128 * ddc:128 * ddc + 128,
                              512 * (wi - 2):512 * (wi - 2) + 512], LD)
            p.wait("vector", MM, MM.n)
            p.wait("vector", LD, LD.n)
            if wi == 1:
                p.op("vector", lambda e, d=ddc, dst=dst, xt=xt: e.tensor_add(
                    Tb[:, 512 * d:512 * d + 1], dst[:, 511:512], xt[:, 0:1]),
                    [(EV, 1)])
            else:
                p.op("vector", lambda e, d=ddc, dst=dst, xt=xt: e.tensor_add(
                    Tb[:, 512 * d:512 * d + 512], dst, xt[:, 0:512]),
                    [(EV, 1)])
            ev_x.append(EV.n)
            p.wait("sync", EV, EV.n)
            if wi == 1:
                p.dma("sync", t["xmid_d"][128 * ddc:128 * ddc + 128, 0:1],
                      Tb[:, 512 * ddc:512 * ddc + 1], ST, slow=True)
            else:
                j0 = 512 * wi - 1023
                p.dma("sync",
                      t["xmid_d"][128 * ddc:128 * ddc + 128, j0:j0 + 512],
                      Tb[:, 512 * ddc:512 * ddc + 512], ST)
    st_xmid = ST.n
    if debug and li == 0:
        p.wait("gpsimd", ST, st_xmid)
        p.dma("gpsimd", t["dbg_oi"], t["o_d"], GL)
        p.dma("gpsimd", t["dbg_xm"], t["xmid_d"], GL)
    if FSUB <= 3:
        return

    p.wait("sync", ST, st_xmid)
    _emit_mlp(p, li, out_full, debug)

    if li == 0 and FSUB >= 7:
        p.wait("gpsimd", ST, ST.n)
        p.op("gpsimd", lambda e: e.collective_compute(
            "AllGather", ALU.bypass, replica_groups=GROUPS4,
            ins=[t["ag_y1_in"]], outs=[t["ag_y1_out"]]), [(CC, 1)])
        p.wait("gpsimd", CC, CC.n)
        p.op("vector", lambda e: e.memset(t["evA"][:, 0:HALO], 0.0),
             [(EV, 1)])
        p.wait("gpsimd", EV, EV.n)
        _emit_halo_switch(p, "y1")
    if debug and li == 0:
        p.wait("gpsimd", ST, ST.n)
        p.dma("gpsimd", t["dbg_y1"], out_full, GL)


def _emit_mlp(p, li, out_full, debug):
    t, S, ps = p.t, p.S, p.ps
    LD, GL, MM, EV, SC, ST = (S[k] for k in
                              ("LD", "GL", "MM", "EV", "SC", "ST"))
    r0, r1, r2, r3 = t["r0"], t["r1"], t["r2"], t["r3"]
    gated, lnout = t["gated"], t["lnout"]
    evA, evB, Pb, bfA, Tb = t["evA"], t["evB"], t["Pb"], t["bfA"], t["Tb"]
    banks = p.banks
    for qt in range(2):
        if debug and li == 0 and qt == 1 and getattr(p, "mlp_gl_tap", 0):
            p.wait("vector", GL, p.mlp_gl_tap)
        p.wait("vector", ST, ST.n)
        ev_st1 = {}
        mm_st1 = {}
        for c in range(8):
            coff = (1 if c < 4 else 0) + 1024 * qt
            xt = evA if c % 2 == 0 else evB
            if c >= 2:
                p.wait("sync", EV, ev_st1[c - 2])
                p.wait("sync", MM, mm_st1[c - 2])
            p.dma("sync", xt[:, 0:1024],
                  t["xmid_d"][128 * c:128 * c + 128, coff:coff + 1024], LD)
            p.wait("vector", LD, LD.n)
            if c >= 2:
                p.wait("vector", MM, MM.n)
            sqb = Tb[:, 2048 + 1024 * (c % 2):2048 + 1024 * (c % 2) + 1024]
            p.op("vector", lambda e, xt=xt, sqb=sqb: e.tensor_mul(
                sqb, xt[:, 0:1024], xt[:, 0:1024]), [(EV, 1)])
            ev_st1[c] = EV.n
            p.wait("tensor", EV, EV.n)
            for half in range(2):
                p.mm(ps[2][0:1, 512 * half:512 * half + 512],
                     t["s_ones_cf"][:],
                     xt[:, 512 * half:512 * half + 512],
                     start=(c == 0), stop=(c == 7))
                p.mm(ps[3][0:1, 512 * half:512 * half + 512],
                     t["s_ones_cf"][:],
                     sqb[:, 512 * half:512 * half + 512],
                     start=(c == 0), stop=(c == 7),
                     inc=MM if half == 1 else None)
            mm_st1[c] = MM.n
        p.wait("vector", MM, MM.n)
        p.op("vector", lambda e: e.tensor_scalar_mul(
            r0[0:1, 0:1024], ps[2][0:1, 0:1024], 1.0 / 1024.0), [])
        p.op("vector", lambda e: e.tensor_scalar_mul(
            r1[0:1, 0:1024], ps[3][0:1, 0:1024], 1.0 / 1024.0), [])
        if debug and li == 0 and qt == 0:
            p.op("vector", lambda e: e.tensor_copy(
                t["lst"][0:1, 0:1024], r0[0:1, 0:1024]), [])
            p.op("vector", lambda e: e.tensor_copy(
                t["dbgrow"][0:1, 0:1024], r1[0:1, 0:1024]), [])
        p.op("vector", lambda e: e.tensor_mul(
            r2[0:1, 0:1024], r0[0:1, 0:1024], r0[0:1, 0:1024]), [])
        p.op("vector", lambda e: e.tensor_sub(
            r1[0:1, 0:1024], r1[0:1, 0:1024], r2[0:1, 0:1024]), [])
        p.op("vector", lambda e: e.tensor_scalar_add(
            r1[0:1, 0:1024], r1[0:1, 0:1024], 1e-5), [(EV, 1)])
        p.wait("scalar", EV, EV.n)
        p.op("scalar", lambda e: e.activation(
            r2[0:1, 0:1024], r1[0:1, 0:1024], AF.Ln), [])
        p.op("scalar", lambda e: e.activation(
            r2[0:1, 0:1024], r2[0:1, 0:1024], AF.Exp, scale=-0.5), [(SC, 1)])
        p.wait("vector", SC, SC.n)
        p.op("vector", lambda e: e.tensor_mul(
            r3[0:1, 0:1024], r0[0:1, 0:1024], r2[0:1, 0:1024]), [])
        p.op("vector", lambda e: e.tensor_scalar_mul(
            r3[0:1, 0:1024], r3[0:1, 0:1024], -1.0), [(EV, 1)])
        p.wait("tensor", EV, EV.n)
        for half in range(2):
            p.mm(ps[2][:, 512 * half:512 * half + 512], t["s_ones_r"][:],
                 r2[0:1, 512 * half:512 * half + 512], True, True)
            p.mm(ps[3][:, 512 * half:512 * half + 512], t["s_ones_r"][:],
                 r3[0:1, 512 * half:512 * half + 512], True, True,
                 inc=MM if half == 1 else None)
        p.wait("vector", MM, MM.n)
        for c in range(8):
            coff = (1 if c < 4 else 0) + 1024 * qt
            xt = evA if c % 2 == 0 else evB
            if c >= 2:
                p.wait("sync", EV, EV.n)
            p.dma("sync", xt[:, 0:1024],
                  t["xmid_d"][128 * c:128 * c + 128, coff:coff + 1024], LD)
            p.wait("vector", LD, LD.n)
            p.op("vector", lambda e, xt=xt: e.tensor_mul(
                xt[:, 0:1024], xt[:, 0:1024], ps[2][:, 0:1024]), [])
            p.op("vector", lambda e, xt=xt: e.tensor_add(
                xt[:, 0:1024], xt[:, 0:1024], ps[3][:, 0:1024]), [])
            p.op("vector", lambda e, xt=xt, c=c: e.tensor_scalar(
                lnout[:, 1024 * c:1024 * c + 1024], xt[:, 0:1024],
                t["s_lng"][:, c:c + 1], t["s_lnb"][:, c:c + 1],
                ALU.mult, ALU.add), [(EV, 1)])
        ev_ln = EV.n
        p.wait("tensor", EV, ev_ln)
        if FSUB <= 4:
            continue
        a_done = 0
        g_done = 0
        gate_done = 0
        for ft in range(22):
            ch = F1_CH[ft]
            for half in range(2):
                wslot = (2 * ft + half) % 4
                off = F1_OFF[ft] + HID * half
                p.wait("sync", EV, p.wst_ev)
                p.dma("sync",
                      t["wst"][:, 0:8 * ch].rearrange("p (c f) -> p c f",
                                                      f=ch),
                      t[f"fw1{li}"][:, off:off + ch].rearrange(
                          "(c p) f -> p c f", p=128), LD)
                if p.w1_slot_mm[wslot]:
                    p.wait("vector", MM, p.w1_slot_mm[wslot])
                p.wait("vector", LD, LD.n)
                p.op("vector", lambda e, ch=ch, wslot=wslot: e.tensor_copy(
                    t["w1t"][:, 1024 * wslot:1024 * wslot + 8 * ch],
                    t["wst"][:, 0:8 * ch]), [(EV, 1)])
                p.wst_ev = EV.n
                p.wait("tensor", EV, EV.n)
                if half == 0 and a_done:
                    p.wait("tensor", EV, a_done)
                if half == 1 and g_done:
                    p.wait("tensor", SC, g_done)
                bank = ps[half]
                for c in range(8):
                    for nn in range(2):
                        p.mm(bank[0:ch, 512 * nn:512 * nn + 512],
                             t["w1t"][:, 1024 * wslot + ch * c:
                                      1024 * wslot + ch * c + ch],
                             lnout[:, 1024 * c + 512 * nn:
                                   1024 * c + 512 * nn + 512],
                             start=(c == 0), stop=(c == 7),
                             inc=MM if (c == 7 and nn == 1) else None)
                p.w1_slot_mm[wslot] = MM.n
                if half == 0:
                    p.wait("vector", MM, MM.n)
                    p.op("vector", lambda e, ch=ch, ft=ft: e.tensor_scalar(
                        Tb[0:ch, 0:1024], ps[0][0:ch, 0:1024],
                        t["s_b1a"][0:ch, ft:ft + 1], None, ALU.add),
                        [(EV, 1)])
                    a_done = EV.n
                else:
                    p.wait("scalar", MM, MM.n)
                    if gate_done:
                        p.wait("scalar", EV, gate_done)
                    p.op("scalar", lambda e, ch=ch, ft=ft: e.activation(
                        Tb[0:ch, 1024:2048], ps[1][0:ch, 0:1024], AF.Gelu,
                        bias=t["s_b1g"][0:ch, ft:ft + 1]), [(SC, 1)])
                    g_done = SC.n
            p.wait("vector", SC, SC.n)
            p.op("vector", lambda e, ch=ch, ft=ft: e.tensor_mul(
                gated[0:ch, 1024 * ft:1024 * ft + 1024],
                Tb[0:ch, 0:1024], Tb[0:ch, 1024:2048]), [(EV, 1)])
            gate_done = EV.n
        if debug and li == 0 and qt == 0:
            p.wait("gpsimd", EV, gate_done)
            p.dma("gpsimd", t["dbg_ln"], lnout[:], GL)
            p.dma("gpsimd", t["dbg_gt"], gated[:], GL)
            p.dma("gpsimd", t["dbg_ph"][0:1, 0:1024], t["lst"][:], GL)
            p.dma("gpsimd", t["dbg_ph"][0:1, 1024:2048], t["dbgrow"][:], GL)
            p.mlp_gl_tap = GL.n
        if FSUB <= 5:
            continue
        nft = min(22, W2CAP)
        for sub in range(0 if W2ALT == 7 else min(2, W2SUB)):
            for ft in range(nft):
                ch = F1_CH[ft]
                wslot = ft % 2
                if W2LOAD:
                    p.wait("sync", EV, p.wst_ev)
                    p.dma("sync", t["wst"][0:ch, 0:1024],
                          t[f"fw2{li}"][F1_OFF[ft]:F1_OFF[ft] + ch, :], LD)
                    if p.w2_slot_mm[wslot]:
                        p.wait("vector", MM, p.w2_slot_mm[wslot])
                    p.wait("vector", LD, LD.n)
                    p.op("vector", lambda e, ch=ch, wslot=wslot:
                         e.tensor_copy(
                             t["w2u"][0:ch, 1024 * wslot:1024 * wslot + 1024],
                             t["wst"][0:ch, 0:1024]), [(EV, 1)])
                    p.wst_ev = EV.n
                    p.wait("tensor", EV, EV.n)
                if ft == 0 and sub == 0:
                    p.wait("tensor", EV, gate_done)
                    p.wait("tensor", SC, g_done)
                if ft == 0 and sub == 1:
                    p.wait("tensor", EV, EV.n)
                for dd in range(8):
                    bank, bcol = banks[dd]
                    lhs = (t["w2u"][0:ch, 1024 * wslot + 128 * dd:
                                    1024 * wslot + 128 * dd + 128]
                           if W2ALT not in (2, 3) else Pb[0:ch, 0:128])
                    rhs = (gated[0:ch, 1024 * ft + 512 * sub:
                                 1024 * ft + 512 * sub + 512]
                           if W2ALT not in (1, 3) else lnout[0:ch, 0:512])
                    if W2ALT == 8:
                        if 8 * ft + dd < NMM:
                            p.mm(bank[:, bcol:bcol + 512], Pb[0:128, 0:128],
                                 lnout[0:128, 0:512], start=True, stop=True)
                    elif W2ALT in (4, 6):
                        p.mm(bank[:, bcol:bcol + 512], Pb[0:128, 0:128],
                             lnout[0:128, 0:512], start=True, stop=True,
                             inc=MM if dd == 7 else None)
                    else:
                        p.mm(bank[:, bcol:bcol + 512], lhs, rhs,
                             start=(ft == 0), stop=(ft == nft - 1),
                             inc=MM if dd == 7 else None)
                p.w2_slot_mm[wslot] = MM.n
            if W2ALT not in (6, 8):
                p.wait("vector", MM, MM.n)
                p.wait("vector", ST, ST.n)
            if debug and li == 0 and qt == 0 and getattr(p, "stat_gl_tap", 0):
                p.wait("vector", GL, p.stat_gl_tap)
            ev_w2 = {}
            for dd in range(8 if W2EVAC else 0):
                bank, bcol = banks[dd]
                xt = evA if dd % 2 == 0 else evB
                ucol = 1024 * qt + 512 * sub
                if dd >= 2:
                    p.wait("sync", EV, ev_w2[dd - 2])
                p.dma("sync", xt[:, 1024:1536],
                      t["xmid_d"][128 * dd:128 * dd + 128,
                                  1 + ucol:1 + ucol + 512], LD)
                p.wait("vector", LD, LD.n)
                p.op("vector", lambda e, b=bank, bc=bcol, xt=xt, dd=dd:
                     e.scalar_tensor_tensor(
                         Tb[:, 512 * dd:512 * dd + 512], b[:, bc:bc + 512],
                         t["s_b2c"][:, dd:dd + 1], xt[:, 1024:1536],
                         ALU.add, ALU.add), [(EV, 1)])
                ev_w2[dd] = EV.n
                p.wait("sync", EV, EV.n)
                p.dma("sync", out_full[128 * dd:128 * dd + 128,
                                       ucol:ucol + 512],
                      Tb[:, 512 * dd:512 * dd + 512], ST)
                if li == 0:
                    if dd >= 4:
                        p.wait("vector", ST, p.st_ye[dd - 4])
                    p.op("vector", lambda e, dd=dd: e.tensor_copy(
                        bfA[:, 512 * (dd % 4):512 * (dd % 4) + 512],
                        Tb[:, 512 * dd:512 * dd + 512]), [(EV, 1)])
                    p.wait("sync", EV, EV.n)
                    p.dma("sync",
                          t["ye_d"][128 * dd:128 * dd + 128,
                                    HALO + ucol:HALO + ucol + 512],
                          bfA[:, 512 * (dd % 4):512 * (dd % 4) + 512], ST)
                    p.st_ye[dd] = ST.n
                    u0 = max(0, 1022 - ucol)
                    if u0 < 512:
                        p.dma("sync",
                              t["ag_y1_in"][128 * dd:128 * dd + 128,
                                            ucol + u0 - 1022:
                                            ucol + 512 - 1022],
                              Tb[:, 512 * dd + u0:512 * dd + 512], ST)
            if W2ALT not in (6, 8):
                p.wait("tensor", EV, EV.n)


def build_final(p, debug):
    t, S, ps = p.t, p.S, p.ps
    LD, MM, EV, SC, ST = (S[k] for k in ("LD", "MM", "EV", "SC", "ST"))
    xa, xb, Tb, Pb, bfA, bfB = (t[k] for k in
                                ("xa", "xb", "Tb", "Pb", "bfA", "bfB"))
    r2, r3 = t["r2"], t["r3"]
    for tt in range(4):
        p.wait("vector", ST, ST.n)
        ev_fc = {}
        for c in range(8):
            xt = xa if c % 2 == 0 else xb
            if c >= 2:
                p.wait("sync", EV, ev_fc[c - 2])
            p.dma("sync", xt[:, 0:512],
                  t["x1_full"][128 * c:128 * c + 128,
                               512 * tt:512 * tt + 512], LD)
            p.dma("sync", xt[:, 512:1024],
                  t["y2_full"][128 * c:128 * c + 128,
                               512 * tt:512 * tt + 512], LD)
            p.wait("vector", LD, LD.n)
            if c >= 2:
                p.wait("vector", MM, MM.n)
            p.op("vector", lambda e, xt=xt, c=c: e.tensor_add(
                Tb[:, 512 * c:512 * c + 512], xt[:, 0:512], xt[:, 512:1024]),
                [])
            p.op("vector", lambda e, c=c: e.tensor_copy(
                Pb[:, 512 * (c % 4):512 * (c % 4) + 512],
                Tb[:, 512 * c:512 * c + 512]), [])
            p.op("vector", lambda e, c=c: e.tensor_mul(
                bfA[:, 512 * (c % 4):512 * (c % 4) + 512],
                Pb[:, 512 * (c % 4):512 * (c % 4) + 512],
                Pb[:, 512 * (c % 4):512 * (c % 4) + 512]), [(EV, 1)])
            ev_fc[c] = EV.n
            p.wait("tensor", EV, EV.n)
            p.mm(ps[2][0:1, 0:512], t["s_ones_cb"][:],
                 Pb[:, 512 * (c % 4):512 * (c % 4) + 512],
                 start=(c == 0), stop=(c == 7))
            p.mm(ps[2][0:1, 512:1024], t["s_ones_cb"][:],
                 bfA[:, 512 * (c % 4):512 * (c % 4) + 512],
                 start=(c == 0), stop=(c == 7), inc=MM)
        _ln_rows_512(p)
        p.wait("tensor", EV, EV.n)
        p.mm(ps[2][:, 0:512], t["s_ones_r"][:], r2[0:1, 0:512], True, True)
        p.mm(ps[2][:, 512:1024], t["s_ones_r"][:], r3[0:1, 0:512], True, True,
             inc=MM)
        p.wait("vector", MM, MM.n)
        for c in range(8):
            p.op("vector", lambda e, c=c: e.tensor_mul(
                Tb[:, 512 * c:512 * c + 512], Tb[:, 512 * c:512 * c + 512],
                ps[2][:, 0:512]), [])
            p.op("vector", lambda e, c=c: e.tensor_add(
                Tb[:, 512 * c:512 * c + 512], Tb[:, 512 * c:512 * c + 512],
                ps[2][:, 512:1024]), [(EV, 1)])
            p.wait("sync", EV, EV.n)
            p.dma("sync", t["outT"][128 * c:128 * c + 128,
                                    512 * tt:512 * tt + 512],
                  Tb[:, 512 * c:512 * c + 512], ST)
        p.wait("tensor", EV, EV.n)
    p.wait("sync", ST, ST.n)


_CACHE = {}


def kernel(**inputs):
    maps = stage_inputs(inputs)
    key = ("nc", DEBUG, UPTO)
    if key not in _CACHE:
        _CACHE[key] = build(debug=DEBUG)
    ncobj = _CACHE[key]
    res = run_bass_kernel_spmd(ncobj, maps, core_ids=list(range(8)))
    out = np.empty((B2, N, D), np.float32)
    for r in range(8):
        B, Qq = r // 4, r % 4
        s0 = 2048 * Qq
        out[B, s0:s0 + TOK, :] = res.results[r]["outT"].T
    if DEBUG:
        kernel._last = res
    return out


# version 20
# speedup vs baseline: 1.1370x; 1.1370x over previous
import dataclasses
import math

import numpy as np
import concourse.bass as bass
import concourse.mybir as mybir
from concourse.bass_utils import run_bass_kernel_spmd

F32 = mybir.dt.float32
BF16 = mybir.dt.bfloat16
AF = mybir.ActivationFunctionType
ALU = mybir.AluOpType
ET = mybir.EngineType
AX = mybir.AxisListType

DEBUG = False
UPTO = 99
FSUB = 99
W2CAP = 22
W2SUB = 2
W2EVAC = 1
W2LOAD = 1
W2ALT = 0
NMM = 1
DMA_TWICE = 0
SKIPP3 = 0
PARANOID3 = False

B2, N, D = 2, 8192, 1024
DH = 64
TOK = 2048
HALO = 1026
QEXT = 3072
HID = 2730
FF1 = 5460
NEG = -1.0e30
LN32 = math.log(32.0)

GROUPS4 = [[0, 1, 2, 3], [4, 5, 6, 7]]
F1_CH = [128] * 21 + [42]
F1_OFF = [128 * i for i in range(22)]
KB = 1024


def brep(ap, dims):
    old = [list(x) for x in ap.ap]
    return dataclasses.replace(ap, ap=[old[0]] + [list(d) for d in dims])



def _rotary_tables(s0):
    p = np.arange(QEXT, dtype=np.float64)
    t = np.maximum(s0 - 1024 + p, 0.0)
    half = np.arange(0, DH, 2, dtype=np.float64)
    inv_freq = 1.0 / 10000.0 ** (half / DH)
    freqs = t[None, :] * inv_freq[:, None]
    freqs = np.concatenate([freqs, freqs], axis=0)
    base = (half + 0.4 * DH) / (1.4 * DH)
    power = (t - N // 2) / 256.0
    scale = base[:, None] ** power[None, :]
    scale = np.concatenate([scale, scale], axis=0)
    cos, sin = np.cos(freqs), np.sin(freqs)
    sgn = np.concatenate([-np.ones(32), np.ones(32)])[:, None]
    c1q = (8.0 * cos * scale).T
    c2q = (8.0 * sin * scale * sgn).T
    c1k = (cos / scale).T
    c2k = (sin / scale * sgn).T
    return np.ascontiguousarray(
        np.concatenate([c1q, c2q, c1k, c2k], axis=1)).astype(np.float32)


def stage_inputs(inp):
    x = np.asarray(inp["L0_attn_x"])
    gamma = np.asarray(inp["L0_attn_gamma"])
    w_qkv = np.asarray(inp["L0_attn_w_qkv"])
    w_out = np.asarray(inp["L0_attn_w_out"])
    fwqkv = np.asarray(inp["L0_ffn_wqkv"])
    fqs = np.asarray(inp["L0_ffn_q_scale"])
    fks = np.asarray(inp["L0_ffn_k_scale"])
    fwo = np.asarray(inp["L0_ffn_wo"])
    flng = np.asarray(inp["L0_ffn_ln_g"])
    flnb = np.asarray(inp["L0_ffn_ln_b"])
    fw1 = np.asarray(inp["L0_ffn_w1"])
    fb1 = np.asarray(inp["L0_ffn_b1"])
    fw2 = np.asarray(inp["L0_ffn_w2"])
    fb2 = np.asarray(inp["L0_ffn_b2"])

    woutT = np.ascontiguousarray(w_out.T)
    gamma_row = np.ascontiguousarray(gamma.reshape(1, D))

    def col128(v):
        return np.ascontiguousarray(v.reshape(8, 128).T)

    def colsF1(v, off):
        out = np.zeros((128, 22), v.dtype)
        for c in range(22):
            n = F1_CH[c]
            out[:n, c] = v[off + F1_OFF[c]: off + F1_OFF[c] + n]
        return np.ascontiguousarray(out)

    maps = []
    for r in range(8):
        B, Qq = r // 4, r % 4
        s0 = 2048 * Qq
        xT = np.ascontiguousarray(x[B, s0:s0 + TOK, :].T)
        hs = [4 * Qq + i for i in range(4)]
        rows = []
        for pi in range(2):
            for h in (hs[2 * pi], hs[2 * pi + 1]):
                rows.append(w_qkv[DH * h: DH * h + DH])
        for pi in range(2):
            for h in (hs[2 * pi], hs[2 * pi + 1]):
                rows.append(w_qkv[1024 + DH * h: 1024 + DH * h + DH])
        for h in hs:
            rows.append(w_qkv[2048 + DH * h: 2048 + DH * h + DH])
        wqkvT = np.ascontiguousarray(np.concatenate(rows, axis=0).T)

        mval = NEG if Qq == 0 else 0.0
        mP0 = np.full((512, 512), mval, np.float32)

        m = {
            "xT": xT.astype(np.float32),
            "wqkvT": wqkvT.astype(np.float32),
            "gamma_row": gamma_row.astype(np.float32),
            "woutT": woutT.astype(np.float32),
            "rotq": _rotary_tables(s0),
            "mP0": mP0,
        }
        for li in range(2):
            m[f"fwqkv{li}"] = np.ascontiguousarray(fwqkv[li]).astype(np.float32)
            m[f"fwo{li}"] = np.ascontiguousarray(fwo[li]).astype(np.float32)
            m[f"fw1{li}"] = np.ascontiguousarray(fw1[li]).astype(np.float32)
            m[f"fw2{li}"] = np.ascontiguousarray(fw2[li]).astype(np.float32)
            m[f"qs_row{li}"] = np.ascontiguousarray(
                fqs[li].reshape(1, DH)).astype(np.float32)
            m[f"ks_row{li}"] = np.ascontiguousarray(
                fks[li].reshape(1, DH)).astype(np.float32)
            m[f"lng{li}"] = col128(flng[li]).astype(np.float32)
            m[f"lnb{li}"] = col128(flnb[li]).astype(np.float32)
            m[f"b2c{li}"] = col128(fb2[li]).astype(np.float32)
            m[f"b1a{li}"] = colsF1(fb1[li], 0).astype(np.float32)
            m[f"b1g{li}"] = colsF1(fb1[li], HID).astype(np.float32)
        maps.append(m)
    return maps



class Ct:
    __slots__ = ("h", "n", "name")

    def __init__(self, h, name):
        self.h, self.n, self.name = h, 0, name


class RingCt:
    __slots__ = ("subs", "idx", "name")

    def __init__(self, subs, name):
        self.subs, self.idx, self.name = subs, 0, name

    @property
    def n(self):
        return tuple(c.n for c in self.subs)


class P:
    ENGS = ("sync", "tensor", "vector", "scalar", "gpsimd")

    DMA_DEPTH = {"sync": 4, "gpsimd": 4}

    def __init__(self, nc):
        self.nc = nc
        self.q = {e: [] for e in self.ENGS}
        self.sems = {}
        self.t = {}
        self.pb_st = {}
        self.w2_slot_mm = [0, 0]
        self.dma_fifo = {}
        self.st_ye = {}
        self.wst_ev = 0
        self.w1_slot_mm = [0, 0, 0, 0]

    def sem(self, name):
        if name not in self.sems:
            self.sems[name] = Ct(self.nc.alloc_semaphore(name), name)
        return self.sems[name]

    def wait(self, eng, ct, val=None):
        if isinstance(ct, RingCt):
            vals = ct.n if val is None else val
            if not isinstance(vals, tuple):
                return
            for c, v in zip(ct.subs, vals):
                self.wait(eng, c, v)
            return
        v = ct.n if val is None else val
        if v <= 0:
            return
        self.q[eng].append(lambda e, h=ct.h, v=v: e.wait_ge(h, v))

    def op(self, eng, fn, incs=()):
        snap = tuple(incs)

        def run(e, fn=fn, snap=snap):
            ins = fn(e)
            if ins is not None:
                for ct, amt in snap:
                    ins.then_inc(ct.h, amt)
        self.q[eng].append(run)
        for ct, amt in snap:
            ct.n += amt

    def raw(self, eng, fn):
        self.q[eng].append(fn)

    def dma(self, eng, dst, src, sem, slow=False):
        if isinstance(sem, RingCt):
            sub = sem.subs[sem.idx % len(sem.subs)]
            sem.idx += 1
            self.wait(eng, sub, sub.n)
            self.op(eng, lambda e, d=dst, s=src, sl=slow: e.dma_start(
                out=d, in_=s, allow_slow_non_contiguous=sl), [(sub, 16)])
            return
        self.op(eng, lambda e, d=dst, s=src, sl=slow: e.dma_start(
            out=d, in_=s, allow_slow_non_contiguous=sl), [(sem, 16)])

    def mm(self, out, lhsT, rhs, start, stop, inc=None):
        incs = [(inc, 1)] if inc is not None else []
        self.op("tensor",
                lambda e, o=out, l=lhsT, r=rhs, s=start, st=stop:
                e.matmul(o, l, r, start=s, stop=st), incs)

    def finalize(self):
        nc = self.nc
        with nc.Block() as block:
            @block.sync
            def _(e):
                for fn in self.q["sync"]:
                    fn(e)

            @block.tensor
            def _(e):
                for fn in self.q["tensor"]:
                    fn(e)

            @block.vector
            def _(e):
                for fn in self.q["vector"]:
                    fn(e)

            @block.scalar
            def _(e):
                for fn in self.q["scalar"]:
                    fn(e)

            @block.gpsimd
            def _(e):
                for fn in self.q["gpsimd"]:
                    fn(e)



def build(debug=False):
    nc = bass.Bass("TRN2", target_bir_lowering=False, debug=False)
    p = P(nc)
    t = p.t

    def ein(name, shape, dt=F32):
        t[name] = nc.dram_tensor(name, list(shape), dt, kind="ExternalInput").ap()

    def eout(name, shape, dt=F32):
        t[name] = nc.dram_tensor(name, list(shape), dt, kind="ExternalOutput").ap()

    def idram(name, shape, dt):
        t[name] = nc.dram_tensor(name, list(shape), dt, kind="Internal").ap()

    ein("xT", (D, TOK))
    ein("wqkvT", (D, 768))
    ein("gamma_row", (1, D))
    ein("woutT", (D, D))
    ein("rotq", (QEXT, 256))
    ein("mP0", (512, 512))
    for li in range(2):
        ein(f"fwqkv{li}", (D, 1536))
        ein(f"fwo{li}", (512, D))
        ein(f"fw1{li}", (D, FF1))
        ein(f"fw2{li}", (HID, D))
        ein(f"qs_row{li}", (1, DH))
        ein(f"ks_row{li}", (1, DH))
        ein(f"lng{li}", (128, 8))
        ein(f"lnb{li}", (128, 8))
        ein(f"b2c{li}", (128, 8))
        ein(f"b1a{li}", (128, 22))
        ein(f"b1g{li}", (128, 22))
    eout("outT", (D, TOK))

    idram("ag_xn_in", (D, TOK), BF16)
    idram("ag_xn_out", (4 * D, TOK), BF16)
    idram("q_d", (256, N), BF16)
    idram("ag_o_in", (260, N), BF16)
    idram("ag_o_out", (1040, N), BF16)
    idram("o_mine", (1040, TOK), BF16)
    idram("ag_l_in", (4, N), F32)
    idram("ag_l_out", (16, N), F32)
    idram("l_d", (16, TOK), F32)
    idram("linv_d", (16, TOK), F32)
    idram("x1_full", (D, TOK), F32)
    idram("y1_full", (D, TOK), F32)
    idram("y2_full", (D, TOK), F32)
    for nm in ("x1", "y1"):
        idram(f"ag_{nm}_in", (D, HALO), F32)
        idram(f"ag_{nm}_out", (4 * D, HALO), F32)
        idram(f"{nm}_halo", (D, HALO), F32)
    idram("xe_d", (D, 2 + QEXT), BF16)
    idram("ye_d", (D, 2 + QEXT), BF16)
    idram("qkv_d", (1024, QEXT), BF16)
    idram("v_d", (QEXT, 520), BF16)
    idram("o_d", (520, 2560), BF16)
    idram("il_row", (128, 160), F32)
    idram("xmid_d", (D, 2049), F32)

    if debug:
        eout("dbg_xn", (4 * D, TOK))
        eout("dbg_q", (256, N))
        eout("dbg_k", (256, N))
        eout("dbg_v", (N, 260))
        eout("dbg_o", (1040, TOK))
        eout("dbg_oin", (260, N))
        eout("dbg_lin", (4, N))
        eout("dbg_onm", (128, 4096))
        eout("dbg_t1", (128, 4096))
        eout("dbg_linv", (16, TOK))
        eout("dbg_x1", (D, TOK))
        eout("dbg_ph", (128, 2048))
        eout("dbg_il", (128, 160))
        eout("dbg_ln", (128, 8192))
        eout("dbg_gt", (128, 22528))
        eout("dbg_pb", (128, 2048))
        eout("dbg_ba", (128, 2048))
        eout("dbg_xe", (D, 2 + QEXT))
        eout("dbg_qkv", (1024, QEXT))
        eout("dbg_v2", (QEXT, 520))
        eout("dbg_oi", (520, 2560))
        eout("dbg_xm", (D, 2049))
        eout("dbg_y1", (D, TOK))

    idx = np.arange(128)
    tri_pf = np.where(idx[:, None] >= idx[None, :], 0.0, NEG).astype(np.float32)
    tri_fp = np.where(idx[None, :] >= idx[:, None], 0.0, NEG).astype(np.float32)
    e2 = np.zeros((2, 128), np.float32)
    e2[0, :64] = 1.0
    e2[1, 64:] = 1.0
    t["c_tri_pf"] = nc.inline_tensor(tri_pf, "c_tri_pf").ap()
    t["c_tri_fp"] = nc.inline_tensor(tri_fp, "c_tri_fp").ap()
    t["c_ident"] = nc.inline_tensor(np.eye(128, dtype=np.float32), "c_ident").ap()
    t["c_ones_r"] = nc.inline_tensor(np.ones((1, 128), np.float32),
                                     "c_ones_r").ap()
    t["c_mones_r"] = nc.inline_tensor(-np.ones((1, 128), np.float32),
                                      "c_mones_r").ap()
    t["c_ones_c"] = nc.inline_tensor(np.ones((128, 1), np.float32),
                                     "c_ones_c").ap()
    t["c_e2"] = nc.inline_tensor(e2, "c_e2").ap()
    t["c_m30"] = nc.inline_tensor(np.full((128, 1), -30.0, np.float32),
                                  "c_m30").ap()

    def sb(name, shape, dt, off):
        h = nc.alloc_sbuf_tensor_at(name, list(shape), dt,
                                    offset=int((off + 16.0) * KB))
        t[name] = h.ap()
        return t[name]

    sb("s_trip", (128, 128), F32, 0.0)
    sb("s_trif", (128, 128), F32, 0.5)
    sb("s_id", (128, 128), F32, 1.0)
    sb("s_ones_r", (1, 128), F32, 1.5)
    sb("s_mones_r", (1, 128), F32, 2.0)
    sb("s_ones_cf", (128, 1), F32, 2.5)
    sb("s_ones_cb", (128, 1), BF16, 3.0)
    sb("s_e2", (2, 128), F32, 3.5)
    sb("s_m30", (128, 1), F32, 5.75)
    sb("s_lng", (128, 8), F32, 4.0)
    sb("s_lnb", (128, 8), F32, 4.25)
    sb("s_b2c", (128, 8), F32, 4.5)
    sb("s_qsb", (128, 64), F32, 4.75)
    sb("s_ksb", (128, 64), F32, 5.0)
    sb("s_b1a", (128, 22), F32, 5.25)
    sb("s_b1g", (128, 22), F32, 5.5)
    sb("lr2", (2, 512), F32, 6.0)
    sb("lst", (1, 1024), F32, 200.0)
    sb("dbgrow", (1, 1024), F32, 186.0)
    sb("li_sb", (128, 160), F32, 8.0)
    sb("s_gp1", (1, D), F32, 8.75)
    sb("qs_r", (1, 64), F32, 12.75)
    sb("ks_r", (1, 64), F32, 13.0)
    sb("mcol", (128, 8), F32, 13.25)
    sb("r0", (1, 2048), F32, 14.0)
    sb("r1", (1, 2048), F32, 22.0)
    sb("r2", (1, 2048), F32, 30.0)
    sb("r3", (1, 2048), F32, 38.0)
    sb("Tb", (128, 4096), F32, 46.0)
    sb("Pb", (128, 4096), BF16, 62.0)
    sb("ost", (128, 1024), BF16, 70.0)
    sb("evA", (128, 2048), F32, 72.0)
    sb("evB", (128, 2048), F32, 80.0)
    sb("bfA", (128, 2048), BF16, 88.0)
    sb("bfB", (128, 2048), BF16, 92.0)
    sb("xslab", (128, 4096), BF16, 88.0)
    sb("xa", (128, 2048), F32, 96.0)
    sb("xb", (128, 2048), F32, 104.0)
    sb("l_sb", (16, 2048), F32, 112.0)
    sb("KT", (128, 2 * N), BF16, 120.0)
    sb("VR", (128, 64 * 260), BF16, 152.0)
    sb("wq_sb", (128, 8 * 768), BF16, 186.0)
    sb("qbuf", (128, 2 * 512), BF16, 198.0)
    sb("wo_sb", (128, 8 * 1024), BF16, 120.0)
    sb("kb", (128, 4 * 1024), BF16, 120.0)
    sb("qb2", (128, 4 * 512), BF16, 128.0)
    sb("s_mP0", (128, 4 * 512), F32, 132.0)
    sb("vti", (128, 2 * 520), BF16, 140.0)
    sb("Cbuf", (128, 2 * 512), F32, 142.5)
    sb("subT", (128, 2 * 1024), F32, 146.5)
    sb("Pi", (128, 2 * 1024), BF16, 154.5)
    sb("rot_sb", (128, 2 * 256), F32, 158.5)
    sb("fqkv_sb", (128, 8 * 1536), BF16, 164.0)
    sb("fwo_sb", (128, 4 * 1024), BF16, 188.0)
    sb("gated", (128, 22 * 1024), BF16, 120.0)
    sb("w1t", (128, 4 * 1024), BF16, 164.0)
    sb("w2u", (128, 2 * 1024), BF16, 180.0)
    sb("wst", (128, 2048), F32, 172.0)
    sb("lnout", (128, 8 * 1024), BF16, 96.0)

    ps = []
    for i in range(4):
        h = nc.alloc_psum_tensor(f"ps{i}", [128, 1024], F32)
        ps.append(h.ap())
    p.ps = ps
    p.banks = [(ps[i], 512 * hh) for i in range(4) for hh in range(2)]

    for nm in ("mm", "ev", "sc", "cc", "sw"):
        p.sem(nm)
    rings = {}
    for nm, ring in (("ld", 1), ("st", 4), ("gl", 1)):
        subs = [p.sem(f"{nm}{i}") for i in range(ring)]
        rings[nm] = RingCt(subs, nm)
    p.S = {"MM": p.sems["mm"], "EV": p.sems["ev"], "SC": p.sems["sc"],
           "CC": p.sems["cc"], "SW": p.sems["sw"],
           "LD": rings["ld"], "ST": rings["st"], "GL": rings["gl"]}

    build_init(p, debug)
    if UPTO >= 1:
        build_p0_rms(p, debug)
    if UPTO >= 2:
        build_p2_qkv(p, debug)
    if UPTO >= 3 and not SKIPP3:
        build_p3_attn(p, debug)
    elif UPTO >= 3:
        p.st_p3 = p.S["ST"].n
    if UPTO >= 4:
        build_p5_x1(p, debug)
    if UPTO >= 5:
        build_ffn_layer(p, 0, debug)
    if UPTO >= 6:
        build_ffn_layer(p, 1, debug)
    if UPTO >= 7:
        build_final(p, debug)
    for eng in ("sync", "gpsimd"):
        p.wait(eng, p.S["ST"])
        p.wait(eng, p.S["LD"])
        p.wait(eng, p.S["GL"])
        p.wait(eng, p.S["SW"])
    p.finalize()
    return nc


def build_init(p, debug):
    t, S = p.t, p.S
    LD, GL, EV = S["LD"], S["GL"], S["EV"]
    for nm, src in (("s_trip", "c_tri_pf"), ("s_trif", "c_tri_fp"),
                    ("s_id", "c_ident"), ("s_ones_r", "c_ones_r"),
                    ("s_mones_r", "c_mones_r"), ("s_ones_cf", "c_ones_c"),
                    ("s_e2", "c_e2"), ("s_m30", "c_m30"),
                    ("s_gp1", "gamma_row")):
        p.dma("sync", t[nm][:], t[src], LD)
    p.dma("gpsimd", t["s_ones_cb"][:], t["c_ones_c"], GL)
    p.dma("gpsimd", t["wq_sb"][:].rearrange("p (c f) -> p c f", f=768),
          t["wqkvT"].rearrange("(c p) f -> p c f", p=128), GL)
    p.wait("vector", LD)
    p.op("vector", lambda e: e.tensor_scalar(
        t["s_gp1"][:], t["s_gp1"][:], 32.0, 32.0, ALU.mult, ALU.add),
        [(EV, 1)])
    p.op("vector", lambda e: e.memset(
        t["VR"][:, :].rearrange("p (b h d) -> p b h d", h=4, d=65)
        [:, :, :, 64:65], 1.0), [(EV, 1)])
    p.op("vector", lambda e: e.memset(t["Tb"][:], 0.0), [(EV, 1)])
    p.ev_init = EV.n
    p.gl_init = GL.n


def build_p0_rms(p, debug):
    t, S, ps = p.t, p.S, p.ps
    LD, MM, EV, SC, ST, CC = (S[k] for k in ("LD", "MM", "EV", "SC", "ST", "CC"))
    xa, xb, evA, evB, bfA = t["xa"], t["xb"], t["evA"], t["evB"], t["bfA"]
    r0, r1 = t["r0"], t["r1"]
    ss_banks = [(ps[2], 0), (ps[2], 512), (ps[3], 0), (ps[3], 512)]

    sq_ev = []
    for c in range(8):
        xt = xa if c % 2 == 0 else xb
        sq = evA if c % 2 == 0 else evB
        if c >= 2:
            p.wait("sync", EV, sq_ev[c - 2])
        p.dma("sync", xt[:], t["xT"][128 * c:128 * c + 128, :], LD)
        p.wait("vector", LD, LD.n)
        if c >= 2:
            p.wait("vector", MM, MM.n)
        p.op("vector", lambda e, xt=xt, sq=sq: e.tensor_mul(sq[:], xt[:], xt[:]),
             [(EV, 1)])
        sq_ev.append(EV.n)
        p.wait("tensor", EV, EV.n)
        for j in range(4):
            bank, col = ss_banks[j]
            p.mm(bank[0:1, col:col + 512], t["s_ones_cf"][:],
                 sq[:, 512 * j:512 * j + 512], start=(c == 0), stop=(c == 7),
                 inc=MM)
    mm_ss = MM.n
    p.wait("vector", MM, mm_ss)
    for j in range(4):
        bank, col = ss_banks[j]
        p.op("vector", lambda e, b=bank, col=col, j=j: e.tensor_copy(
            r0[0:1, 512 * j:512 * j + 512], b[0:1, col:col + 512]), [])
    p.op("vector", lambda e: e.tensor_single_scalar(
        r0[:], r0[:], 1e-24, ALU.max), [(EV, 1)])
    p.wait("scalar", EV, EV.n)
    p.op("scalar", lambda e: e.activation(r1[:], r0[:], AF.Ln), [])
    p.op("scalar", lambda e: e.activation(
        r1[:], r1[:], AF.Exp, scale=-0.5), [(SC, 1)])

    p.wait("tensor", SC, SC.n)
    st_list = []
    ev_list = []
    for c in range(8):
        xt = xa if c % 2 == 0 else xb
        if c == 0:
            p.wait("sync", MM, mm_ss)
        if c >= 2:
            p.wait("sync", EV, ev_list[4 * (c - 2) + 3])
        p.dma("sync", xt[:], t["xT"][128 * c:128 * c + 128, :], LD)
        p.wait("tensor", LD, LD.n)
        for j in range(4):
            i = 4 * c + j
            bank = ps[0] if j % 2 == 0 else ps[1]
            if i >= 2:
                p.wait("tensor", EV, ev_list[i - 2])
            p.mm(bank[:, 0:512], t["s_gp1"][0:1, 128 * c:128 * c + 128],
                 r1[0:1, 512 * j:512 * j + 512], start=True, stop=True, inc=MM)
            p.wait("vector", MM, MM.n)
            if i >= 2:
                p.wait("vector", ST, st_list[i - 2])
            p.op("vector", lambda e, xt=xt, j=j, bank=bank: e.tensor_mul(
                bfA[:, 512 * (j % 2):512 * (j % 2) + 512],
                xt[:, 512 * j:512 * j + 512], bank[:, 0:512]), [(EV, 1)])
            ev_list.append(EV.n)
            p.wait("sync", EV, EV.n)
            p.dma("sync",
                  t["ag_xn_in"][128 * c:128 * c + 128, 512 * j:512 * j + 512],
                  bfA[:, 512 * (j % 2):512 * (j % 2) + 512], ST)
            st_list.append(ST.n)

    p.wait("gpsimd", ST, ST.n)
    p.op("gpsimd", lambda e: e.collective_compute(
        "AllGather", ALU.bypass, replica_groups=GROUPS4,
        ins=[t["ag_xn_in"]], outs=[t["ag_xn_out"]]), [(CC, 1)])
    p.cc_xn = CC.n


def build_p2_qkv(p, debug):
    t, S, ps = p.t, p.S, p.ps
    LD, GL, MM, EV, ST, CC = (S[k] for k in ("LD", "GL", "MM", "EV", "ST", "CC"))
    bfA, bfB = t["bfA"], t["bfB"]
    qk_dst = [(ps[0], 0), (ps[0], 512), (ps[1], 0), (ps[1], 512)]
    p.wait("sync", CC, p.cc_xn)
    p.wait("tensor", GL, p.gl_init)
    p.wait("tensor", EV, p.ev_init)

    mm_slot = []
    ev_tt = 0
    for tt in range(16):
        q4, col0 = tt // 4, 512 * (tt % 4)
        for c in range(8):
            li = tt * 8 + c
            slot = c % 4
            xnb = bfA if slot < 2 else bfB
            scol = 512 * (slot % 2)
            if li >= 4:
                p.wait("sync", MM, mm_slot[li - 4])
            p.dma("sync", xnb[:, scol:scol + 512],
                  t["ag_xn_out"][1024 * q4 + 128 * c:1024 * q4 + 128 * c + 128,
                                 col0:col0 + 512], LD)
            p.wait("tensor", LD, LD.n)
            if c == 0 and tt > 0:
                p.wait("tensor", EV, ev_tt)
            for f in range(4):
                bank, bcol = qk_dst[f]
                p.mm(bank[:, bcol:bcol + 512],
                     t["wq_sb"][:, 768 * c + 128 * f:768 * c + 128 * f + 128],
                     xnb[:, scol:scol + 512], start=(c == 0), stop=(c == 7))
            for tsub in range(4):
                p.mm(ps[2][:, 256 * tsub:256 * tsub + 256],
                     xnb[:, scol + 128 * tsub:scol + 128 * tsub + 128],
                     t["wq_sb"][:, 768 * c + 512:768 * c + 768],
                     start=(c == 0 and tsub % 2 == 0), stop=(c == 7),
                     inc=MM if tsub == 3 else None)
            mm_slot.append(MM.n)
        p.wait("vector", MM, MM.n)
        qst = t["Pb"]
        p.wait("vector", ST, ST.n)
        for f in range(2):
            bank, bcol = qk_dst[f]
            p.op("vector", lambda e, b=bank, bc=bcol, f=f: e.tensor_copy(
                qst[:, 512 * f:512 * f + 512], b[:, bc:bc + 512]), [(EV, 1)])
        for f in range(2, 4):
            bank, bcol = qk_dst[f]
            p.op("vector", lambda e, b=bank, bc=bcol, f=f, q4=q4, col0=col0:
                 e.tensor_copy(
                     t["KT"][:, N * (f - 2) + 2048 * q4 + col0:
                             N * (f - 2) + 2048 * q4 + col0 + 512],
                     b[:, bc:bc + 512]), [(EV, 1)])
        for tsub in range(4):
            tok = 4 * tt + tsub
            p.op("vector", lambda e, ts=tsub, tok=tok: e.tensor_copy(
                t["VR"][:, 260 * tok:260 * tok + 260].rearrange(
                    "p (h d) -> p h d", d=65)[:, :, 0:64],
                ps[2][:, 256 * ts:256 * ts + 256].rearrange(
                    "p (h d) -> p h d", d=64)), [(EV, 1)])
        ev_tt = EV.n
        p.wait("sync", EV, EV.n)
        for f in range(2):
            p.dma("sync", t["q_d"][128 * f:128 * f + 128,
                                   2048 * q4 + col0:2048 * q4 + col0 + 512],
                  qst[:, 512 * f:512 * f + 512], ST)
    p.st_p2 = ST.n
    p.ev_p2 = EV.n

    if debug:
        p.wait("gpsimd", ST, ST.n)
        p.wait("gpsimd", EV, EV.n)
        p.dma("gpsimd", t["dbg_xn"], t["ag_xn_out"], GL)
        p.dma("gpsimd", t["dbg_q"], t["q_d"], GL)
        for f in range(2):
            p.dma("gpsimd", t["dbg_k"][128 * f:128 * f + 128, :],
                  t["KT"][:, N * f:N * f + N], GL)
        p.dma("gpsimd", t["dbg_v"].rearrange("(b p) c -> p b c", p=128),
              t["VR"][:, :].rearrange("p (b c) -> p b c", c=260), GL)


def build_p3_attn(p, debug):
    t, S, ps = p.t, p.S, p.ps
    LD, MM, EV, SC, ST = (S[k] for k in ("LD", "MM", "EV", "SC", "ST"))
    KT, VR, qbuf, Tb, Pb, ost = (t[k] for k in
                                 ("KT", "VR", "qbuf", "Tb", "Pb", "ost"))
    p.wait("scalar", EV, p.ev_init)
    p.wait("sync", ST, p.st_p2)
    p.wait("scalar", ST, p.st_p2)
    p.wait("tensor", EV, p.ev_p2)
    smm_last = []
    ev_o_prev = 0
    for un in range(32):
        qt, hp = un // 2, un % 2
        qslot = un % 2
        if un >= 2:
            p.wait("sync", MM, smm_last[un - 2])
        p.dma("sync", qbuf[:, 512 * qslot:512 * qslot + 512],
              t["q_d"][128 * hp:128 * hp + 128, 512 * qt:512 * qt + 512], LD)
        p.wait("tensor", LD, LD.n)
        nkc = 4 * qt + 4
        tanh_sc = {}
        pv_mm = {}
        dsub_ev = {}
        for kc in range(nkc):
            j = kc - 4 * qt
            n0 = 128 * j if j >= 0 else 0
            Sps = ps[kc % 2]
            if kc >= 2:
                p.wait("tensor", SC, tanh_sc[kc - 2])
            p.mm(Sps[:, n0:512],
                 KT[0:64, N * hp + 128 * kc:N * hp + 128 * kc + 128],
                 qbuf[0:64, 512 * qslot + n0:512 * qslot + 512], True, True)
            p.mm(Sps[:, 512 + n0:1024],
                 KT[64:128, N * hp + 128 * kc:N * hp + 128 * kc + 128],
                 qbuf[64:128, 512 * qslot + n0:512 * qslot + 512], True, True,
                 inc=MM)
            tslot = kc % 4
            p.wait("scalar", MM, MM.n)
            p.op("scalar", lambda e, ts=tslot, Sps=Sps: e.activation(
                Tb[:, 1024 * ts:1024 * ts + 1024], Sps[:, 0:1024], AF.Tanh,
                scale=1.0 / 400.0), [(SC, 1)])
            tanh_sc[kc] = SC.n
            if j >= 0:
                p.wait("vector", SC, SC.n)
                for head in range(2):
                    p.op("vector", lambda e, ts=tslot, h=head, n0=n0:
                         e.tensor_add(
                             Tb[:, 1024 * ts + 512 * h + n0:
                                1024 * ts + 512 * h + n0 + 128],
                             Tb[:, 1024 * ts + 512 * h + n0:
                                1024 * ts + 512 * h + n0 + 128],
                             t["s_trif"][:]), [(EV, 1)])
                dsub_ev[kc] = EV.n
            if kc % 2 == 1 or kc == nkc - 1:
                pair = kc // 2
                pb = pair % 2
                base_kc = 2 * pair
                width = 1024 * (kc - base_kc + 1)
                for kk in range(base_kc, kc + 1):
                    if kk in dsub_ev:
                        p.wait("scalar", EV, dsub_ev[kk])
                if pair >= 2:
                    p.wait("scalar", MM, pv_mm[pair - 2])
                p.op("scalar", lambda e, pb=pb, b=base_kc, w=width:
                     e.activation(Pb[:, 2048 * pb:2048 * pb + w],
                                  Tb[:, 1024 * (b % 4):1024 * (b % 4) + w],
                                  AF.Exp, scale=50.0, bias=t["s_m30"][:]),
                     [(SC, 1)])
                p.wait("tensor", SC, SC.n)
                if pair == 0:
                    p.wait("tensor", EV, ev_o_prev)
                for kk in range(base_kc, kc + 1):
                    jj = kk - 4 * qt
                    nn0 = 128 * jj if jj >= 0 else 0
                    for head in range(2):
                        hloc = 2 * hp + head
                        p.mm(ps[2][0:65, 512 * head + nn0:512 * head + 512],
                             VR[:, 260 * kk + 65 * hloc:
                                260 * kk + 65 * hloc + 65],
                             Pb[:, 2048 * pb + 1024 * (kk - base_kc) +
                                512 * head + nn0:
                                2048 * pb + 1024 * (kk - base_kc) +
                                512 * head + 512],
                             start=(kk == 0), stop=(kk == nkc - 1),
                             inc=MM if (kk == kc and head == 1) else None)
                pv_mm[pair] = MM.n
            if PARANOID3:
                p.wait("tensor", EV, EV.n)
                p.wait("tensor", SC, SC.n)
                p.wait("scalar", EV, EV.n)
                p.wait("scalar", MM, MM.n)
                p.wait("vector", MM, MM.n)
                p.wait("vector", SC, SC.n)
                p.wait("sync", MM, MM.n)
        smm_last.append(MM.n)
        p.wait("vector", MM, MM.n)
        p.wait("vector", ST, ST.n)
        for head in range(2):
            p.op("vector", lambda e, h=head: e.tensor_copy(
                ost[0:65, 512 * h:512 * h + 512],
                ps[2][0:65, 512 * h:512 * h + 512]), [])
            p.op("vector", lambda e, h=head: e.tensor_copy(
                t["lst"][0:1, 512 * h:512 * h + 512],
                ps[2][64:65, 512 * h:512 * h + 512]), [(EV, 1)])
        ev_o_prev = EV.n
        p.wait("sync", EV, EV.n)
        for head in range(2):
            p.dma("sync", t["ag_o_in"][65 * (2 * hp + head):
                                       65 * (2 * hp + head) + 65,
                                       512 * qt:512 * qt + 512],
                  ost[0:65, 512 * head:512 * head + 512], ST)
            p.dma("sync", t["ag_l_in"][2 * hp + head:2 * hp + head + 1,
                                       512 * qt:512 * qt + 512],
                  t["lst"][0:1, 512 * head:512 * head + 512], ST)
    p.st_p3 = ST.n


def build_p5_x1(p, debug):
    t, S, ps = p.t, p.S, p.ps
    LD, GL, MM, EV, SC, ST, CC, SW = (S[k] for k in
                                      ("LD", "GL", "MM", "EV", "SC", "ST",
                                       "CC", "SW"))
    nc = p.nc
    p.wait("gpsimd", ST, p.st_p3)
    if debug:
        p.dma("gpsimd", t["dbg_oin"], t["ag_o_in"], GL)
        p.dma("gpsimd", t["dbg_lin"], t["ag_l_in"], GL)
        p.wait("gpsimd", GL, GL.n)
    p.op("gpsimd", lambda e: e.collective_compute(
        "AllGather", ALU.bypass, replica_groups=GROUPS4,
        ins=[t["ag_o_in"]], outs=[t["ag_o_out"]]), [(CC, 1)])
    p.op("gpsimd", lambda e: e.collective_compute(
        "AllGather", ALU.bypass, replica_groups=GROUPS4,
        ins=[t["ag_l_in"]], outs=[t["ag_l_out"]]), [(CC, 1)])
    p.wait("gpsimd", CC, CC.n)

    sw_base = SW.n

    def sw_omine(e):
        pid = nc.partition_id(engines=[ET.Pool])
        for arm in e.Switch(pid, 8):
            qq = arm % 4
            e.dma_start(out=t["o_mine"],
                        in_=t["ag_o_out"][:, 2048 * qq:2048 * qq + 2048]
                        ).then_inc(SW.h, 16)
            e.dma_start(out=t["l_d"],
                        in_=t["ag_l_out"][:, 2048 * qq:2048 * qq + 2048]
                        ).then_inc(SW.h, 16)
    p.raw("gpsimd", sw_omine)
    SW.n += 32

    p.wait("sync", SW, SW.n)
    p.dma("sync", t["l_sb"][:], t["l_d"], LD)
    p.wait("vector", LD, LD.n)
    p.op("vector", lambda e: e.reciprocal(t["l_sb"][:], t["l_sb"][:]), [(EV, 1)])
    p.wait("sync", EV, EV.n)
    p.dma("sync", t["linv_d"], t["l_sb"][:], ST)
    st_linv = ST.n
    p.dma("gpsimd", t["wo_sb"][:].rearrange("p (c f) -> p c f", f=1024),
          t["woutT"].rearrange("(c p) f -> p c f", p=128), GL)
    p.wait("tensor", GL, GL.n)
    p.wait("sync", ST, st_linv)

    bfA, bfB, Pb, Tb = t["bfA"], t["bfB"], t["Pb"], t["Tb"]
    xa, xb = t["xa"], t["xb"]
    r0, r1, r2, r3 = t["r0"], t["r1"], t["r2"], t["r3"]
    lr2 = t["lr2"]

    for tt in range(4):
        ev_norm = []
        for fc in range(8):
            oslot = fc % 2
            if fc >= 2:
                p.wait("sync", EV, ev_norm[fc - 2])
            p.dma("sync", bfA[0:64, 512 * oslot:512 * oslot + 512],
                  t["o_mine"][65 * 2 * fc:65 * 2 * fc + 64,
                              512 * tt:512 * tt + 512], LD)
            p.dma("sync", bfA[64:128, 512 * oslot:512 * oslot + 512],
                  t["o_mine"][65 * (2 * fc + 1):65 * (2 * fc + 1) + 64,
                              512 * tt:512 * tt + 512], LD)
            p.wait("sync", MM, MM.n)
            p.dma("sync", lr2[:], t["linv_d"][2 * fc:2 * fc + 2,
                                              512 * tt:512 * tt + 512], LD)
            p.wait("tensor", LD, LD.n)
            if fc >= 1 or tt >= 1:
                p.wait("tensor", EV, EV.n)
            p.mm(ps[3][:, 0:512], t["s_e2"][:], lr2[:], True, True, inc=MM)
            p.wait("vector", MM, MM.n)
            p.wait("vector", LD, LD.n)
            p.op("vector", lambda e, fc=fc, osl=oslot: e.tensor_mul(
                Pb[:, 512 * fc:512 * fc + 512],
                bfA[:, 512 * osl:512 * osl + 512], ps[3][:, 0:512]), [(EV, 1)])
            ev_norm.append(EV.n)
        p.wait("tensor", EV, EV.n)
        p.wait("vector", ST, ST.n)
        ev_t1 = []
        for ddc in range(8):
            dst = ps[ddc % 2][:, 512 * ((ddc // 2) % 2):
                              512 * ((ddc // 2) % 2) + 512]
            if ddc >= 4:
                p.wait("tensor", EV, ev_t1[ddc - 4])
            for fc in range(8):
                p.mm(dst, t["wo_sb"][:, 1024 * fc + 128 * ddc:
                                     1024 * fc + 128 * ddc + 128],
                     Pb[:, 512 * fc:512 * fc + 512],
                     start=(fc == 0), stop=(fc == 7),
                     inc=MM if fc == 7 else None)
            xt = xa if ddc % 2 == 0 else xb
            if ddc >= 2:
                p.wait("sync", EV, ev_t1[ddc - 2])
            p.dma("sync", xt[:, 0:512],
                  t["xT"][128 * ddc:128 * ddc + 128, 512 * tt:512 * tt + 512],
                  LD)
            p.wait("vector", MM, MM.n)
            p.wait("vector", LD, LD.n)
            p.op("vector", lambda e, d=ddc, dst=dst, xt=xt: e.tensor_add(
                Tb[:, 512 * d:512 * d + 512], dst, xt[:, 0:512]), [])
            p.op("vector", lambda e, d=ddc: e.tensor_copy(
                bfB[:, 512 * (d % 4):512 * (d % 4) + 512],
                Tb[:, 512 * d:512 * d + 512]), [])
            p.op("vector", lambda e, d=ddc: e.tensor_mul(
                bfA[:, 512 * (d % 4):512 * (d % 4) + 512],
                bfB[:, 512 * (d % 4):512 * (d % 4) + 512],
                bfB[:, 512 * (d % 4):512 * (d % 4) + 512]), [(EV, 1)])
            ev_t1.append(EV.n)
            p.wait("tensor", EV, EV.n)
            p.mm(ps[2][0:1, 0:512], t["s_ones_cb"][:],
                 bfB[:, 512 * (ddc % 4):512 * (ddc % 4) + 512],
                 start=(ddc == 0), stop=(ddc == 7))
            p.mm(ps[2][0:1, 512:1024], t["s_ones_cb"][:],
                 bfA[:, 512 * (ddc % 4):512 * (ddc % 4) + 512],
                 start=(ddc == 0), stop=(ddc == 7), inc=MM)
        if debug and tt == 0:
            p.wait("gpsimd", EV, EV.n)
            p.dma("gpsimd", t["dbg_onm"], t["Pb"][:, :], GL)
            p.dma("gpsimd", t["dbg_t1"], t["Tb"][:, :], GL)
            p.dma("gpsimd", t["dbg_linv"], t["l_sb"][:], GL)
            gl_tap = GL.n
        _ln_rows_512(p)
        p.wait("tensor", EV, EV.n)
        p.mm(ps[2][:, 0:512], t["s_ones_r"][:], r2[0:1, 0:512], True, True)
        p.mm(ps[2][:, 512:1024], t["s_ones_r"][:], r3[0:1, 0:512], True, True,
             inc=MM)
        p.wait("vector", MM, MM.n)
        if debug and tt == 0:
            p.wait("vector", GL, gl_tap)
        st_xe = {}
        for ddc in range(8):
            p.op("vector", lambda e, d=ddc: e.tensor_mul(
                Tb[:, 512 * d:512 * d + 512], Tb[:, 512 * d:512 * d + 512],
                ps[2][:, 0:512]), [])
            p.op("vector", lambda e, d=ddc: e.tensor_add(
                Tb[:, 512 * d:512 * d + 512], Tb[:, 512 * d:512 * d + 512],
                ps[2][:, 512:1024]), [])
            if ddc >= 4:
                p.wait("vector", ST, st_xe[ddc - 4])
            p.op("vector", lambda e, d=ddc: e.tensor_copy(
                bfB[:, 512 * (d % 4):512 * (d % 4) + 512],
                Tb[:, 512 * d:512 * d + 512]), [(EV, 1)])
            p.wait("sync", EV, EV.n)
            p.dma("sync", t["x1_full"][128 * ddc:128 * ddc + 128,
                                       512 * tt:512 * tt + 512],
                  Tb[:, 512 * ddc:512 * ddc + 512], ST)
            p.dma("sync", t["xe_d"][128 * ddc:128 * ddc + 128,
                                    HALO + 512 * tt:HALO + 512 * tt + 512],
                  bfB[:, 512 * (ddc % 4):512 * (ddc % 4) + 512], ST)
            st_xe[ddc] = ST.n
            u0 = max(0, 1022 - 512 * tt)
            if u0 < 512:
                p.dma("sync",
                      t["ag_x1_in"][128 * ddc:128 * ddc + 128,
                                    512 * tt + u0 - 1022:
                                    512 * tt + 512 - 1022],
                      Tb[:, 512 * ddc + u0:512 * ddc + 512], ST)
        p.wait("tensor", EV, EV.n)

    p.wait("gpsimd", ST, ST.n)
    p.op("gpsimd", lambda e: e.collective_compute(
        "AllGather", ALU.bypass, replica_groups=GROUPS4,
        ins=[t["ag_x1_in"]], outs=[t["ag_x1_out"]]), [(CC, 1)])
    p.wait("gpsimd", CC, CC.n)
    p.op("vector", lambda e: e.memset(t["evA"][:, 0:HALO], 0.0), [(EV, 1)])
    p.wait("gpsimd", EV, EV.n)
    _emit_halo_switch(p, "x1")
    if debug:
        p.wait("gpsimd", ST, ST.n)
        p.wait("gpsimd", SW, SW.n)
        p.dma("gpsimd", t["dbg_o"], t["o_mine"], GL)
        p.dma("gpsimd", t["dbg_x1"], t["x1_full"], GL)


def _ln_rows_512(p):
    t, S, ps = p.t, p.S, p.ps
    MM, EV, SC = S["MM"], S["EV"], S["SC"]
    r0, r1, r2, r3 = t["r0"], t["r1"], t["r2"], t["r3"]
    p.wait("vector", MM, MM.n)
    p.op("vector", lambda e: e.tensor_scalar_mul(
        r0[0:1, 0:512], ps[2][0:1, 0:512], 1.0 / 1024.0), [])
    p.op("vector", lambda e: e.tensor_scalar_mul(
        r1[0:1, 0:512], ps[2][0:1, 512:1024], 1.0 / 1024.0), [])
    p.op("vector", lambda e: e.tensor_mul(
        r2[0:1, 0:512], r0[0:1, 0:512], r0[0:1, 0:512]), [])
    p.op("vector", lambda e: e.tensor_sub(
        r1[0:1, 0:512], r1[0:1, 0:512], r2[0:1, 0:512]), [])
    p.op("vector", lambda e: e.tensor_scalar_add(
        r1[0:1, 0:512], r1[0:1, 0:512], 1e-5), [(EV, 1)])
    p.wait("scalar", EV, EV.n)
    p.op("scalar", lambda e: e.activation(
        r2[0:1, 0:512], r1[0:1, 0:512], AF.Ln), [])
    p.op("scalar", lambda e: e.activation(
        r2[0:1, 0:512], r2[0:1, 0:512], AF.Exp, scale=-0.5), [(SC, 1)])
    p.wait("vector", SC, SC.n)
    p.op("vector", lambda e: e.tensor_mul(
        r3[0:1, 0:512], r0[0:1, 0:512], r2[0:1, 0:512]), [])
    p.op("vector", lambda e: e.tensor_scalar_mul(
        r3[0:1, 0:512], r3[0:1, 0:512], -1.0), [(EV, 1)])


def _emit_halo_switch(p, nm):
    t, S = p.t, p.S
    SW = S["SW"]
    nc = p.nc
    ext = t["xe_d"] if nm == "x1" else t["ye_d"]

    base = SW.n

    def fn(e):
        pid = nc.partition_id(engines=[ET.Pool])
        for arm in e.Switch(pid, 8):
            qq = arm % 4
            k = 0
            for c in range(8):
                if qq == 0:
                    src = t["evA"][:, 0:HALO]
                else:
                    src = t[f"ag_{nm}_out"][1024 * (qq - 1) + 128 * c:
                                            1024 * (qq - 1) + 128 * c + 128, :]
                e.dma_start(out=t[f"{nm}_halo"][128 * c:128 * c + 128, :],
                            in_=src).then_inc(SW.h, 16)
                e.dma_start(out=ext[128 * c:128 * c + 128, 0:HALO],
                            in_=src).then_inc(SW.h, 16)
    p.raw("gpsimd", fn)
    SW.n += 16 * 16


def build_ffn_layer(p, li, debug):
    t, S, ps = p.t, p.S, p.ps
    LD, GL, MM, EV, SC, ST, CC, SW = (S[k] for k in
                                      ("LD", "GL", "MM", "EV", "SC", "ST",
                                       "CC", "SW"))
    in_ext = t["xe_d"] if li == 0 else t["ye_d"]
    in_halo = t["x1_halo"] if li == 0 else t["y1_halo"]
    in_full = t["x1_full"] if li == 0 else t["y1_full"]
    out_full = t["y1_full"] if li == 0 else t["y2_full"]
    bfA, bfB, ost = t["bfA"], t["bfB"], t["ost"]
    evA, evB, Tb, Pb = t["evA"], t["evB"], t["Tb"], t["Pb"]
    banks = p.banks

    p.wait("gpsimd", SW, SW.n)
    p.wait("sync", SW, SW.n)
    p.wait("sync", ST, ST.n)
    if debug and li == 0:
        p.wait("gpsimd", ST, ST.n)
        p.dma("gpsimd", t["dbg_xe"], in_ext, GL)
    p.dma("gpsimd", t["fqkv_sb"][:].rearrange("p (c f) -> p c f", f=1536),
          t[f"fwqkv{li}"].rearrange("(c p) f -> p c f", p=128), GL)
    p.dma("gpsimd", t["fwo_sb"][:].rearrange("p (c f) -> p c f", f=1024),
          t[f"fwo{li}"].rearrange("(c p) f -> p c f", p=128), GL)
    p.dma("sync", t["qs_r"][:], t[f"qs_row{li}"], LD)
    p.dma("sync", t["ks_r"][:], t[f"ks_row{li}"], LD)
    p.dma("sync", t["s_mP0"][:].rearrange("p (a q) -> p a q", q=512),
          t["mP0"].rearrange("(a p) q -> p a q", p=128), LD)
    p.dma("sync", t["s_lng"][:], t[f"lng{li}"], LD)
    p.dma("sync", t["s_lnb"][:], t[f"lnb{li}"], LD)
    p.dma("sync", t["s_b2c"][:], t[f"b2c{li}"], LD)
    p.dma("sync", t["s_b1a"][:], t[f"b1a{li}"], LD)
    p.dma("sync", t["s_b1g"][:], t[f"b1g{li}"], LD)
    p.wait("tensor", GL, GL.n)
    p.wait("tensor", LD, LD.n)
    for nm_r, nm_b in (("qs_r", "s_qsb"), ("ks_r", "s_ksb")):
        p.mm(ps[3][:, 0:64], t["s_ones_r"][:], t[nm_r][:], True, True, inc=MM)
        p.wait("vector", MM, MM.n)
        p.op("vector", lambda e, nb=nm_b: e.tensor_copy(
            t[nb][:], ps[3][:, 0:64]), [(EV, 1)])
        p.wait("tensor", EV, EV.n)

    ev_slots = 0
    mm_v_prev = 0
    for tt in range(6):
        if tt > 0:
            p.wait("sync", MM, mm_v_prev)
        for hf in range(2):
            coff = (2 if hf == 0 else 1) + 512 * tt
            p.dma("sync",
                  t["xslab"][:, 2048 * hf:2048 * hf + 2048].rearrange(
                      "p (c f) -> p c f", f=512),
                  in_ext[512 * hf:512 * hf + 512, coff:coff + 512].rearrange(
                      "(c p) f -> p c f", p=128), LD)
        p.wait("tensor", LD, LD.n)
        if tt > 0:
            p.wait("tensor", EV, ev_slots)
        for grp in range(2):
            for tsub in range(4):
                bank, bcol = banks[4 * grp + tsub]
                for c in range(8):
                    xslab = bfA if c < 4 else bfB
                    scol = 512 * (c % 4)
                    p.mm(bank[:, bcol:bcol + 512],
                         xslab[:, scol + 128 * tsub:scol + 128 * tsub + 128],
                         t["fqkv_sb"][:, 1536 * c + 512 * grp:
                                      1536 * c + 512 * grp + 512],
                         start=(c == 0), stop=(c == 7),
                         inc=MM if c == 7 else None)
        for tsub in range(4):
            tok = 4 * tt + tsub
            rsl = tsub % 2
            if tt + tsub > 0:
                p.wait("sync", EV, EV.n)
            p.dma("sync", t["rot_sb"][:, 256 * rsl:256 * rsl + 256],
                  t["rotq"][128 * tok:128 * tok + 128, :], LD)
            for grp in range(2):
                bank, bcol = banks[4 * grp + tsub]
                qsb = t["s_qsb"] if grp == 0 else t["s_ksb"]
                c1o = 256 * rsl + 128 * grp
                c2o = c1o + 64
                p.wait("vector", MM, MM.n)
                p.wait("vector", LD, LD.n)
                p.op("vector", lambda e, b=bank, bc=bcol: e.tensor_copy(
                    evA[:, 0:512], b[:, bc:bc + 512]), [])
                p.op("vector", lambda e: e.tensor_mul(
                    evB[:, 0:512], evA[:, 0:512], evA[:, 0:512]), [(EV, 1)])
                p.wait("vector", EV, EV.n)
                snap = debug and tt == 2 and tsub == 0 and grp == 0 and li == 0
                if snap:
                    p.op("vector", lambda e: e.tensor_copy(
                        t["xb"][:, 0:512], evA[:, 0:512]), [])
                p.op("vector", lambda e: e.reduce_sum(
                    t["mcol"][:, 0:8].rearrange("p h -> p h ()"),
                    evB[:, 0:512].rearrange("p (h d) -> p h d", d=64),
                    axis=AX.X), [(EV, 1)])
                p.wait("vector", EV, EV.n)
                if snap:
                    p.op("vector", lambda e: e.tensor_copy(
                        t["xb"][:, 512:520], t["mcol"][:, 0:8]), [])
                p.op("vector", lambda e: e.tensor_single_scalar(
                    t["mcol"][:, 0:8], t["mcol"][:, 0:8], 1e-24, ALU.max),
                    [(EV, 1)])
                p.wait("scalar", EV, EV.n)
                p.op("scalar", lambda e: e.activation(
                    t["mcol"][:, 0:8], t["mcol"][:, 0:8], AF.Ln), [])
                p.op("scalar", lambda e: e.activation(
                    t["mcol"][:, 0:8], t["mcol"][:, 0:8], AF.Exp, scale=-0.5),
                    [(SC, 1)])
                p.wait("vector", SC, SC.n)
                if snap:
                    p.op("vector", lambda e: e.tensor_copy(
                        t["xb"][:, 520:528], t["mcol"][:, 0:8]), [])
                p.op("vector", lambda e: e.tensor_mul(
                    brep(evA[:, 0:512], [[1, 64], [64, 8]]),
                    brep(evA[:, 0:512], [[1, 64], [64, 8]]),
                    brep(t["mcol"][:, 0:8], [[0, 64], [1, 8]])), [])
                if snap:
                    p.op("vector", lambda e: e.tensor_copy(
                        t["xb"][:, 528:1040], evA[:, 0:512]), [])
                p.op("vector", lambda e, qsb=qsb: e.tensor_mul(
                    evA[:, 0:512].rearrange("p (h d) -> p h d", d=64),
                    evA[:, 0:512].rearrange("p (h d) -> p h d", d=64),
                    brep(qsb[:, 0:64], [[0, 8], [1, 64]])), [])
                if snap:
                    p.op("vector", lambda e: e.tensor_copy(
                        t["xb"][:, 1040:1552], evA[:, 0:512]), [])
                p.op("vector", lambda e, rsl=rsl, c1o=c1o: e.tensor_mul(
                    evB[:, 0:512].rearrange("p (h d) -> p h d", d=64),
                    evA[:, 0:512].rearrange("p (h d) -> p h d", d=64),
                    brep(t["rot_sb"][:, c1o:c1o + 64], [[0, 8], [1, 64]])), [])
                p.op("vector", lambda e, c2o=c2o: e.tensor_mul(
                    Tb[:, 0:512].rearrange("p (h d) -> p h d", d=64)[:, :, 0:32],
                    evA[:, 0:512].rearrange("p (h d) -> p h d", d=64)[:, :, 32:64],
                    brep(t["rot_sb"][:, c2o:c2o + 32], [[0, 8], [1, 32]])), [])
                p.op("vector", lambda e, c2o=c2o: e.tensor_mul(
                    Tb[:, 0:512].rearrange("p (h d) -> p h d", d=64)[:, :, 32:64],
                    evA[:, 0:512].rearrange("p (h d) -> p h d", d=64)[:, :, 0:32],
                    brep(t["rot_sb"][:, c2o + 32:c2o + 64], [[0, 8], [1, 32]])),
                    [])
                p.op("vector", lambda e: e.tensor_add(
                    evB[:, 0:512], evB[:, 0:512], Tb[:, 0:512]), [(EV, 1)])
                p.wait("tensor", EV, EV.n)
                for hh2 in range(4):
                    p.op("tensor", lambda e, b=bank, bc=bcol, hh2=hh2:
                         e.transpose(b[:, bc + 128 * hh2:bc + 128 * hh2 + 128],
                                     evB[:, 128 * hh2:128 * hh2 + 128],
                                     t["s_id"][:]),
                         [(MM, 1)] if hh2 == 3 else [])
                p.wait("vector", MM, MM.n)
                pslot = 2 * tsub + grp
                p.wait("vector", ST, p.pb_st.get(pslot, 0))
                p.op("vector", lambda e, b=bank, bc=bcol, psl=pslot:
                     e.tensor_copy(Pb[:, 512 * psl:512 * psl + 512],
                                   b[:, bc:bc + 512]), [(EV, 1)])
                p.wait("sync", EV, EV.n)
                for hh2 in range(4):
                    p.dma("sync",
                          t["qkv_d"][512 * grp + 128 * hh2:
                                     512 * grp + 128 * hh2 + 128,
                                     128 * tok:128 * tok + 128],
                          Pb[:, 512 * pslot + 128 * hh2:
                             512 * pslot + 128 * hh2 + 128], ST)
                p.pb_st[pslot] = ST.n
        ev_slots = EV.n
        p.wait("tensor", EV, EV.n)
        for tsub in range(4):
            bank, bcol = banks[tsub]
            for c in range(8):
                xslab = bfA if c < 4 else bfB
                scol = 512 * (c % 4)
                p.mm(bank[:, bcol:bcol + 512],
                     xslab[:, scol + 128 * tsub:scol + 128 * tsub + 128],
                     t["fqkv_sb"][:, 1536 * c + 1024:1536 * c + 1536],
                     start=(c == 0), stop=(c == 7),
                     inc=MM if c == 7 else None)
        mm_v_prev = MM.n
        p.wait("vector", MM, MM.n)
        for tsub in range(4):
            bank, bcol = banks[tsub]
            p.wait("vector", ST, ST.n)
            p.op("vector", lambda e: e.memset(
                ost[:, 0:520].rearrange("p (h d) -> p h d", d=65)
                [:, :, 64:65], 1.0), [(EV, 1)])
            p.op("vector", lambda e, b=bank, bc=bcol: e.tensor_copy(
                ost[:, 0:520].rearrange("p (h d) -> p h d", d=65)[:, :, 0:64],
                b[:, bc:bc + 512].rearrange("p (h d) -> p h d", d=64)),
                [(EV, 1)])
            p.wait("sync", EV, EV.n)
            tok = 4 * tt + tsub
            p.dma("sync", t["v_d"][128 * tok:128 * tok + 128, :],
                  ost[:, 0:520], ST)
        ev_slots = EV.n
    st_phaseA = ST.n
    if debug and li == 0:
        p.wait("gpsimd", ST, st_phaseA)
        p.dma("gpsimd", t["dbg_qkv"], t["qkv_d"], GL)
        p.dma("gpsimd", t["dbg_v2"], t["v_d"], GL)
        p.wait("gpsimd", EV, EV.n)
        p.dma("gpsimd", t["dbg_ph"], t["xb"], GL)
    if FSUB <= 1:
        return

    p.wait("sync", ST, st_phaseA)
    kb, qb2, vti, Pi, subT, Cbuf = (t[k] for k in
                                    ("kb", "qb2", "vti", "Pi", "subT", "Cbuf"))
    mrow = t["r3"][0:1, 0:512]
    ev_oD = 0
    qm_ev = 0
    mm_wi_end = 0
    for wi in range(1, 6):
        if wi > 1:
            p.wait("sync", MM, mm_wi_end)
        for fc in range(4):
            p.dma("sync", kb[:, 1024 * fc:1024 * fc + 1024],
                  t["qkv_d"][512 + 128 * fc:512 + 128 * fc + 128,
                             512 * (wi - 1):512 * (wi - 1) + 1024], LD)
            p.dma("sync", qb2[:, 512 * fc:512 * fc + 512],
                  t["qkv_d"][128 * fc:128 * fc + 128,
                             512 * wi:512 * wi + 512], LD)
        p.wait("tensor", LD, LD.n)
        for hp in range(4):
            for hh in range(2):
                base = 64 * hh
                for qs in range(4):
                    if qm_ev:
                        p.wait("tensor", EV, qm_ev)
                    p.mm(ps[2][:, 128 * qs:512],
                         qb2[base:base + 64,
                             512 * hp + 128 * qs:512 * hp + 128 * qs + 128],
                         kb[base:base + 64,
                            1024 * hp + 128 * qs:1024 * hp + 512], True, True)
                    p.mm(ps[2][:, 512:640 + 128 * qs],
                         qb2[base:base + 64,
                             512 * hp + 128 * qs:512 * hp + 128 * qs + 128],
                         kb[base:base + 64,
                            1024 * hp + 512:1024 * hp + 640 + 128 * qs],
                         True, True, inc=MM)
                    p.wait("vector", MM, MM.n)
                    p.op("vector", lambda e, qs=qs: e.tensor_add(
                        ps[2][:, 128 * qs:128 * qs + 128],
                        ps[2][:, 128 * qs:128 * qs + 128], t["s_trif"][:]), [])
                    p.op("vector", lambda e, qs=qs: e.tensor_add(
                        ps[2][:, 512 + 128 * qs:512 + 128 * qs + 128],
                        ps[2][:, 512 + 128 * qs:512 + 128 * qs + 128],
                        t["s_trip"][:]), [(EV, 1)])
                    if wi == 2:
                        p.op("vector", lambda e, qs=qs: e.tensor_add(
                            ps[2][:, 128 * qs:512], ps[2][:, 128 * qs:512],
                            t["s_mP0"][:, 512 * qs + 128 * qs:
                                       512 * qs + 512]), [(EV, 1)])
                    p.wait("vector", EV, EV.n)
                    p.op("vector", lambda e, qs=qs: e.reduce_max(
                        t["mcol"][:, qs:qs + 1],
                        ps[2][:, 128 * qs:640 + 128 * qs], axis=AX.X),
                        [(EV, 1)])
                    qm_ev = EV.n
                p.wait("tensor", EV, qm_ev)
                p.wait("tensor", EV, ev_oD)
                for qs in range(4):
                    p.op("tensor", lambda e, qs=qs: e.transpose(
                        ps[3][0:1, 512 + 128 * qs:512 + 128 * qs + 128],
                        t["mcol"][:, qs:qs + 1], t["s_id"][:]),
                        [(MM, 1)] if qs == 3 else [])
                p.wait("vector", MM, MM.n)
                p.op("vector", lambda e: e.tensor_copy(
                    mrow, ps[3][0:1, 512:1024]), [(EV, 1)])
                p.wait("tensor", EV, EV.n)
                p.mm(ps[3][:, 512:1024], t["s_mones_r"][:], mrow, True, True,
                     inc=MM)
                p.wait("vector", MM, MM.n)
                p.op("vector", lambda e, hh=hh: e.tensor_copy(
                    Cbuf[:, 512 * hh:512 * hh + 512], ps[3][:, 512:1024]),
                    [(EV, 1)])
                p.wait("tensor", EV, EV.n)
            sub_ev = {}
            pv_mm = {}
            for kc in range(8):
                Sps = ps[kc % 2]
                if kc >= 2:
                    p.wait("tensor", EV, sub_ev[kc - 2])
                if kc < 4:
                    q0, q1 = 0, 128 * (kc + 1)
                else:
                    q0, q1 = 128 * (kc - 4), 512
                for hh in range(2):
                    base = 64 * hh
                    p.mm(Sps[:, 512 * hh + q0:512 * hh + q1],
                         kb[base:base + 64,
                            1024 * hp + 128 * kc:1024 * hp + 128 * kc + 128],
                         qb2[base:base + 64, 512 * hp + q0:512 * hp + q1],
                         True, True, inc=MM if hh == 1 else None)
                p.wait("vector", MM, MM.n)
                for hh in range(2):
                    so = 1024 * (kc % 2) + 512 * hh
                    p.op("vector", lambda e, so=so, q0=q0, q1=q1, hh=hh,
                         Sps=Sps: e.tensor_add(
                             subT[:, so + q0:so + q1],
                             Sps[:, 512 * hh + q0:512 * hh + q1],
                             Cbuf[:, 512 * hh + q0:512 * hh + q1]), [])
                    cpos = 128 * kc if kc < 4 else 128 * (kc - 4)
                    tri = t["s_trip"] if kc < 4 else t["s_trif"]
                    p.op("vector", lambda e, so=so, cp=cpos, tri=tri:
                         e.tensor_add(subT[:, so + cp:so + cp + 128],
                                      subT[:, so + cp:so + cp + 128],
                                      tri[:]), [])
                    if wi == 2 and kc < 4:
                        p.op("vector", lambda e, so=so, kc=kc, q0=q0, q1=q1:
                             e.tensor_add(
                                 subT[:, so + q0:so + q1],
                                 subT[:, so + q0:so + q1],
                                 t["s_mP0"][:, 512 * kc + q0:512 * kc + q1]),
                             [])
                    if hh == 1:
                        p.op("vector", lambda e, so=so: e.tensor_copy(
                            subT[:, so:so + 1], subT[:, so:so + 1]),
                            [(EV, 1)])
                sub_ev[kc] = EV.n
                p.wait("scalar", EV, sub_ev[kc])
                if kc >= 2:
                    p.wait("scalar", MM, pv_mm[kc - 2])
                p.op("scalar", lambda e, kc=kc: e.activation(
                    Pi[:, 1024 * (kc % 2):1024 * (kc % 2) + 1024],
                    subT[:, 1024 * (kc % 2):1024 * (kc % 2) + 1024],
                    AF.Exp), [(SC, 1)])
                if kc >= 2:
                    p.wait("sync", MM, pv_mm[kc - 2])
                tokb = 512 * (wi - 1) + 128 * kc
                p.dma("sync", vti[:, 520 * (kc % 2):520 * (kc % 2) + 520],
                      t["v_d"][tokb:tokb + 128, :], LD)
                p.wait("tensor", SC, SC.n)
                p.wait("tensor", LD, LD.n)
                if kc == 0:
                    p.wait("tensor", EV, ev_oD)
                for hh in range(2):
                    h = 2 * hp + hh
                    p.mm(ps[3][0:65, 512 * hh + q0:512 * hh + q1],
                         vti[:, 520 * (kc % 2) + 65 * h:
                             520 * (kc % 2) + 65 * h + 65],
                         Pi[:, 1024 * (kc % 2) + 512 * hh + q0:
                            1024 * (kc % 2) + 512 * hh + q1],
                         start=(kc == 0), stop=(kc == 7),
                         inc=MM if hh == 1 else None)
                pv_mm[kc] = MM.n
            p.wait("vector", MM, MM.n)
            p.wait("vector", ST, ST.n)
            for hh in range(2):
                p.op("vector", lambda e, hh=hh: e.tensor_copy(
                    ost[0:64, 512 * hh:512 * hh + 512],
                    ps[3][0:64, 512 * hh:512 * hh + 512]), [])
                p.op("vector", lambda e, hh=hh: e.tensor_copy(
                    t["lst"][0:1, 512 * hh:512 * hh + 512],
                    ps[3][64:65, 512 * hh:512 * hh + 512]), [(EV, 1)])
            ev_oD = EV.n
            p.wait("sync", EV, EV.n)
            il_flat = t["il_row"].rearrange("a b -> (a b) ()").rearrange(
                "(x f) y -> x (f y)", x=1)
            for hh in range(2):
                h = 2 * hp + hh
                p.dma("sync", t["o_d"][65 * h:65 * h + 64,
                                       512 * (wi - 1):512 * (wi - 1) + 512],
                      ost[0:64, 512 * hh:512 * hh + 512], ST)
                i0 = 512 * (8 * (wi - 1) + h)
                p.dma("sync", il_flat[0:1, i0:i0 + 512],
                      t["lst"][0:1, 512 * hh:512 * hh + 512], ST)
        mm_wi_end = MM.n
    st_attn = ST.n
    if FSUB <= 2:
        if debug and li == 0:
            p.wait("gpsimd", ST, st_attn)
            p.dma("gpsimd", t["dbg_oi"], t["o_d"], GL)
            p.dma("gpsimd", t["dbg_il"], t["il_row"], GL)
        return

    p.wait("sync", ST, st_attn)
    p.dma("sync", t["li_sb"][:], t["il_row"], LD)
    p.wait("vector", LD, LD.n)
    p.op("vector", lambda e: e.reciprocal(t["li_sb"][:], t["li_sb"][:]),
         [(EV, 1)])
    p.wait("sync", EV, EV.n)
    p.dma("sync", t["il_row"], t["li_sb"][:], ST)
    p.wait("sync", ST, ST.n)
    il_flat = t["il_row"].rearrange("a b -> (a b) ()").rearrange(
        "(x f) y -> x (f y)", x=1)
    lr2 = t["lr2"]
    xa, xb = t["xa"], t["xb"]
    for wi in range(1, 6):
        ev_onorm = []
        for hp in range(4):
            oslot = hp % 2
            if hp >= 2:
                p.wait("sync", EV, ev_onorm[hp - 2])
            p.dma("sync", bfA[0:64, 512 * oslot:512 * oslot + 512],
                  t["o_d"][65 * 2 * hp:65 * 2 * hp + 64,
                           512 * (wi - 1):512 * (wi - 1) + 512], LD)
            p.dma("sync", bfA[64:128, 512 * oslot:512 * oslot + 512],
                  t["o_d"][65 * (2 * hp + 1):65 * (2 * hp + 1) + 64,
                           512 * (wi - 1):512 * (wi - 1) + 512], LD)
            i0 = 512 * (8 * (wi - 1) + 2 * hp)
            p.wait("sync", MM, MM.n)
            p.dma("sync", lr2[0:1, :], il_flat[0:1, i0:i0 + 512], LD)
            p.dma("sync", lr2[1:2, :], il_flat[0:1, i0 + 512:i0 + 1024], LD)
            p.wait("tensor", LD, LD.n)
            if hp >= 1 or wi > 1:
                p.wait("tensor", EV, EV.n)
            p.mm(ps[3][:, 0:512], t["s_e2"][:], lr2[:], True, True, inc=MM)
            p.wait("vector", MM, MM.n)
            p.op("vector", lambda e, hp=hp, osl=oslot: e.tensor_mul(
                Pb[:, 512 * hp:512 * hp + 512],
                bfA[:, 512 * osl:512 * osl + 512], ps[3][:, 0:512]),
                [(EV, 1)])
            ev_onorm.append(EV.n)
        p.wait("tensor", EV, EV.n)
        p.wait("vector", ST, ST.n)
        ev_x = []
        for ddc in range(8):
            dst = ps[ddc % 2][:, 512 * ((ddc // 2) % 2):
                              512 * ((ddc // 2) % 2) + 512]
            if ddc >= 4:
                p.wait("tensor", EV, ev_x[ddc - 4])
            for hp in range(4):
                p.mm(dst, t["fwo_sb"][:, 1024 * hp + 128 * ddc:
                                      1024 * hp + 128 * ddc + 128],
                     Pb[:, 512 * hp:512 * hp + 512],
                     start=(hp == 0), stop=(hp == 3),
                     inc=MM if hp == 3 else None)
            xt = xa if ddc % 2 == 0 else xb
            if wi == 1:
                p.dma("sync", xt[:, 0:1],
                      in_halo[128 * ddc:128 * ddc + 128, HALO - 1:HALO], LD,
                      slow=True)
            else:
                p.dma("sync", xt[:, 0:512],
                      in_full[128 * ddc:128 * ddc + 128,
                              512 * (wi - 2):512 * (wi - 2) + 512], LD)
            p.wait("vector", MM, MM.n)
            p.wait("vector", LD, LD.n)
            if wi == 1:
                p.op("vector", lambda e, d=ddc, dst=dst, xt=xt: e.tensor_add(
                    Tb[:, 512 * d:512 * d + 1], dst[:, 511:512], xt[:, 0:1]),
                    [(EV, 1)])
            else:
                p.op("vector", lambda e, d=ddc, dst=dst, xt=xt: e.tensor_add(
                    Tb[:, 512 * d:512 * d + 512], dst, xt[:, 0:512]),
                    [(EV, 1)])
            ev_x.append(EV.n)
            p.wait("sync", EV, EV.n)
            if wi == 1:
                p.dma("sync", t["xmid_d"][128 * ddc:128 * ddc + 128, 0:1],
                      Tb[:, 512 * ddc:512 * ddc + 1], ST, slow=True)
            else:
                j0 = 512 * wi - 1023
                p.dma("sync",
                      t["xmid_d"][128 * ddc:128 * ddc + 128, j0:j0 + 512],
                      Tb[:, 512 * ddc:512 * ddc + 512], ST)
    st_xmid = ST.n
    if debug and li == 0:
        p.wait("gpsimd", ST, st_xmid)
        p.dma("gpsimd", t["dbg_oi"], t["o_d"], GL)
        p.dma("gpsimd", t["dbg_xm"], t["xmid_d"], GL)
    if FSUB <= 3:
        return

    p.wait("sync", ST, st_xmid)
    p.wait("gpsimd", ST, st_xmid)
    _emit_mlp(p, li, out_full, debug)

    if li == 0 and FSUB >= 7:
        p.wait("gpsimd", ST, ST.n)
        p.op("gpsimd", lambda e: e.collective_compute(
            "AllGather", ALU.bypass, replica_groups=GROUPS4,
            ins=[t["ag_y1_in"]], outs=[t["ag_y1_out"]]), [(CC, 1)])
        p.wait("gpsimd", CC, CC.n)
        p.op("vector", lambda e: e.memset(t["evA"][:, 0:HALO], 0.0),
             [(EV, 1)])
        p.wait("gpsimd", EV, EV.n)
        _emit_halo_switch(p, "y1")
    if debug and li == 0:
        p.wait("gpsimd", ST, ST.n)
        p.dma("gpsimd", t["dbg_y1"], out_full, GL)


def _emit_mlp(p, li, out_full, debug):
    t, S, ps = p.t, p.S, p.ps
    LD, GL, MM, EV, SC, ST = (S[k] for k in
                              ("LD", "GL", "MM", "EV", "SC", "ST"))
    r0, r1, r2, r3 = t["r0"], t["r1"], t["r2"], t["r3"]
    gated, lnout = t["gated"], t["lnout"]
    evA, evB, Pb, bfA, Tb = t["evA"], t["evB"], t["Pb"], t["bfA"], t["Tb"]
    banks = p.banks
    for qt in range(2):
        if debug and li == 0 and qt == 1 and getattr(p, "mlp_gl_tap", 0):
            p.wait("vector", GL, p.mlp_gl_tap)
        p.wait("vector", ST, ST.n)
        ev_st1 = {}
        mm_st1 = {}
        for c in range(8):
            coff = (1 if c < 4 else 0) + 1024 * qt
            xt = evA if c % 2 == 0 else evB
            if c >= 2:
                p.wait("gpsimd", EV, ev_st1[c - 2])
                p.wait("gpsimd", MM, mm_st1[c - 2])
            p.dma("gpsimd", xt[:, 0:1024],
                  t["xmid_d"][128 * c:128 * c + 128, coff:coff + 1024], GL)
            p.wait("vector", GL, GL.n)
            if c >= 2:
                p.wait("vector", MM, MM.n)
            sqb = Tb[:, 2048 + 1024 * (c % 2):2048 + 1024 * (c % 2) + 1024]
            p.op("vector", lambda e, xt=xt, sqb=sqb: e.tensor_mul(
                sqb, xt[:, 0:1024], xt[:, 0:1024]), [(EV, 1)])
            ev_st1[c] = EV.n
            p.wait("tensor", EV, EV.n)
            for half in range(2):
                p.mm(ps[2][0:1, 512 * half:512 * half + 512],
                     t["s_ones_cf"][:],
                     xt[:, 512 * half:512 * half + 512],
                     start=(c == 0), stop=(c == 7))
                p.mm(ps[3][0:1, 512 * half:512 * half + 512],
                     t["s_ones_cf"][:],
                     sqb[:, 512 * half:512 * half + 512],
                     start=(c == 0), stop=(c == 7),
                     inc=MM if half == 1 else None)
            mm_st1[c] = MM.n
        p.wait("vector", MM, MM.n)
        p.op("vector", lambda e: e.tensor_scalar_mul(
            r0[0:1, 0:1024], ps[2][0:1, 0:1024], 1.0 / 1024.0), [])
        p.op("vector", lambda e: e.tensor_scalar_mul(
            r1[0:1, 0:1024], ps[3][0:1, 0:1024], 1.0 / 1024.0), [])
        if debug and li == 0 and qt == 0:
            p.op("vector", lambda e: e.tensor_copy(
                t["lst"][0:1, 0:1024], r0[0:1, 0:1024]), [])
            p.op("vector", lambda e: e.tensor_copy(
                t["dbgrow"][0:1, 0:1024], r1[0:1, 0:1024]), [])
        p.op("vector", lambda e: e.tensor_mul(
            r2[0:1, 0:1024], r0[0:1, 0:1024], r0[0:1, 0:1024]), [])
        p.op("vector", lambda e: e.tensor_sub(
            r1[0:1, 0:1024], r1[0:1, 0:1024], r2[0:1, 0:1024]), [])
        p.op("vector", lambda e: e.tensor_scalar_add(
            r1[0:1, 0:1024], r1[0:1, 0:1024], 1e-5), [(EV, 1)])
        p.wait("scalar", EV, EV.n)
        p.op("scalar", lambda e: e.activation(
            r2[0:1, 0:1024], r1[0:1, 0:1024], AF.Ln), [])
        p.op("scalar", lambda e: e.activation(
            r2[0:1, 0:1024], r2[0:1, 0:1024], AF.Exp, scale=-0.5), [(SC, 1)])
        p.wait("vector", SC, SC.n)
        p.op("vector", lambda e: e.tensor_mul(
            r3[0:1, 0:1024], r0[0:1, 0:1024], r2[0:1, 0:1024]), [])
        p.op("vector", lambda e: e.tensor_scalar_mul(
            r3[0:1, 0:1024], r3[0:1, 0:1024], -1.0), [(EV, 1)])
        p.wait("tensor", EV, EV.n)
        for half in range(2):
            p.mm(ps[2][:, 512 * half:512 * half + 512], t["s_ones_r"][:],
                 r2[0:1, 512 * half:512 * half + 512], True, True)
            p.mm(ps[3][:, 512 * half:512 * half + 512], t["s_ones_r"][:],
                 r3[0:1, 512 * half:512 * half + 512], True, True,
                 inc=MM if half == 1 else None)
        p.wait("vector", MM, MM.n)
        for c in range(8):
            coff = (1 if c < 4 else 0) + 1024 * qt
            xt = evA if c % 2 == 0 else evB
            if c >= 2:
                p.wait("gpsimd", EV, EV.n)
            p.dma("gpsimd", xt[:, 0:1024],
                  t["xmid_d"][128 * c:128 * c + 128, coff:coff + 1024], GL)
            p.wait("vector", GL, GL.n)
            p.op("vector", lambda e, xt=xt: e.tensor_mul(
                xt[:, 0:1024], xt[:, 0:1024], ps[2][:, 0:1024]), [])
            p.op("vector", lambda e, xt=xt: e.tensor_add(
                xt[:, 0:1024], xt[:, 0:1024], ps[3][:, 0:1024]), [])
            p.op("vector", lambda e, xt=xt, c=c: e.tensor_scalar(
                lnout[:, 1024 * c:1024 * c + 1024], xt[:, 0:1024],
                t["s_lng"][:, c:c + 1], t["s_lnb"][:, c:c + 1],
                ALU.mult, ALU.add), [(EV, 1)])
        ev_ln = EV.n
        p.wait("tensor", EV, ev_ln)
        if FSUB <= 4:
            continue
        a_done = 0
        g_done = 0
        gate_done = 0
        for ft in range(22):
            ch = F1_CH[ft]
            for half in range(2):
                wslot = (2 * ft + half) % 4
                off = F1_OFF[ft] + HID * half
                p.wait("sync", EV, p.wst_ev)
                p.dma("sync",
                      t["wst"][:, 0:8 * ch].rearrange("p (c f) -> p c f",
                                                      f=ch),
                      t[f"fw1{li}"][:, off:off + ch].rearrange(
                          "(c p) f -> p c f", p=128), LD)
                if p.w1_slot_mm[wslot]:
                    p.wait("vector", MM, p.w1_slot_mm[wslot])
                p.wait("vector", LD, LD.n)
                p.op("vector", lambda e, ch=ch, wslot=wslot: e.tensor_copy(
                    t["w1t"][:, 1024 * wslot:1024 * wslot + 8 * ch],
                    t["wst"][:, 0:8 * ch]), [(EV, 1)])
                p.wst_ev = EV.n
                p.wait("tensor", EV, EV.n)
                if half == 0 and a_done:
                    p.wait("tensor", EV, a_done)
                if half == 1 and g_done:
                    p.wait("tensor", SC, g_done)
                bank = ps[half]
                for c in range(8):
                    for nn in range(2):
                        p.mm(bank[0:ch, 512 * nn:512 * nn + 512],
                             t["w1t"][:, 1024 * wslot + ch * c:
                                      1024 * wslot + ch * c + ch],
                             lnout[:, 1024 * c + 512 * nn:
                                   1024 * c + 512 * nn + 512],
                             start=(c == 0), stop=(c == 7),
                             inc=MM if (c == 7 and nn == 1) else None)
                p.w1_slot_mm[wslot] = MM.n
                if half == 0:
                    p.wait("vector", MM, MM.n)
                    p.op("vector", lambda e, ch=ch, ft=ft: e.tensor_scalar(
                        Tb[0:ch, 0:1024], ps[0][0:ch, 0:1024],
                        t["s_b1a"][0:ch, ft:ft + 1], None, ALU.add),
                        [(EV, 1)])
                    a_done = EV.n
                else:
                    p.wait("scalar", MM, MM.n)
                    if gate_done:
                        p.wait("scalar", EV, gate_done)
                    p.op("scalar", lambda e, ch=ch, ft=ft: e.activation(
                        Tb[0:ch, 1024:2048], ps[1][0:ch, 0:1024], AF.Gelu,
                        bias=t["s_b1g"][0:ch, ft:ft + 1]), [(SC, 1)])
                    g_done = SC.n
            p.wait("vector", SC, SC.n)
            p.op("vector", lambda e, ch=ch, ft=ft: e.tensor_mul(
                gated[0:ch, 1024 * ft:1024 * ft + 1024],
                Tb[0:ch, 0:1024], Tb[0:ch, 1024:2048]), [(EV, 1)])
            gate_done = EV.n
        if debug and li == 0 and qt == 0:
            p.wait("gpsimd", EV, gate_done)
            p.dma("gpsimd", t["dbg_ln"], lnout[:], GL)
            p.dma("gpsimd", t["dbg_gt"], gated[:], GL)
            p.dma("gpsimd", t["dbg_ph"][0:1, 0:1024], t["lst"][:], GL)
            p.dma("gpsimd", t["dbg_ph"][0:1, 1024:2048], t["dbgrow"][:], GL)
            p.mlp_gl_tap = GL.n
        if FSUB <= 5:
            continue
        nft = min(22, W2CAP)
        for sub in range(0 if W2ALT == 7 else min(2, W2SUB)):
            for ft in range(nft):
                ch = F1_CH[ft]
                wslot = ft % 2
                if W2LOAD:
                    p.wait("sync", EV, p.wst_ev)
                    p.dma("sync", t["wst"][0:ch, 0:1024],
                          t[f"fw2{li}"][F1_OFF[ft]:F1_OFF[ft] + ch, :], LD)
                    if p.w2_slot_mm[wslot]:
                        p.wait("vector", MM, p.w2_slot_mm[wslot])
                    p.wait("vector", LD, LD.n)
                    p.op("vector", lambda e, ch=ch, wslot=wslot:
                         e.tensor_copy(
                             t["w2u"][0:ch, 1024 * wslot:1024 * wslot + 1024],
                             t["wst"][0:ch, 0:1024]), [(EV, 1)])
                    p.wst_ev = EV.n
                    p.wait("tensor", EV, EV.n)
                if ft == 0 and sub == 0:
                    p.wait("tensor", EV, gate_done)
                    p.wait("tensor", SC, g_done)
                if ft == 0 and sub == 1:
                    p.wait("tensor", EV, EV.n)
                for dd in range(8):
                    bank, bcol = banks[dd]
                    lhs = (t["w2u"][0:ch, 1024 * wslot + 128 * dd:
                                    1024 * wslot + 128 * dd + 128]
                           if W2ALT not in (2, 3) else Pb[0:ch, 0:128])
                    rhs = (gated[0:ch, 1024 * ft + 512 * sub:
                                 1024 * ft + 512 * sub + 512]
                           if W2ALT not in (1, 3) else lnout[0:ch, 0:512])
                    if W2ALT == 8:
                        if 8 * ft + dd < NMM:
                            p.mm(bank[:, bcol:bcol + 512], Pb[0:128, 0:128],
                                 lnout[0:128, 0:512], start=True, stop=True)
                    elif W2ALT in (4, 6):
                        p.mm(bank[:, bcol:bcol + 512], Pb[0:128, 0:128],
                             lnout[0:128, 0:512], start=True, stop=True,
                             inc=MM if dd == 7 else None)
                    else:
                        p.mm(bank[:, bcol:bcol + 512], lhs, rhs,
                             start=(ft == 0), stop=(ft == nft - 1),
                             inc=MM if dd == 7 else None)
                p.w2_slot_mm[wslot] = MM.n
            if W2ALT not in (6, 8):
                p.wait("vector", MM, MM.n)
                p.wait("vector", ST, ST.n)
            if debug and li == 0 and qt == 0 and getattr(p, "stat_gl_tap", 0):
                p.wait("vector", GL, p.stat_gl_tap)
            ev_w2 = {}
            for dd in range(8 if W2EVAC else 0):
                bank, bcol = banks[dd]
                xt = evA if dd % 2 == 0 else evB
                ucol = 1024 * qt + 512 * sub
                if dd >= 2:
                    p.wait("gpsimd", EV, ev_w2[dd - 2])
                p.dma("gpsimd", xt[:, 1024:1536],
                      t["xmid_d"][128 * dd:128 * dd + 128,
                                  1 + ucol:1 + ucol + 512], GL)
                p.wait("vector", GL, GL.n)
                p.op("vector", lambda e, b=bank, bc=bcol, xt=xt, dd=dd:
                     e.scalar_tensor_tensor(
                         Tb[:, 512 * dd:512 * dd + 512], b[:, bc:bc + 512],
                         t["s_b2c"][:, dd:dd + 1], xt[:, 1024:1536],
                         ALU.add, ALU.add), [(EV, 1)])
                ev_w2[dd] = EV.n
                p.wait("sync", EV, EV.n)
                p.dma("sync", out_full[128 * dd:128 * dd + 128,
                                       ucol:ucol + 512],
                      Tb[:, 512 * dd:512 * dd + 512], ST)
                if li == 0:
                    if dd >= 4:
                        p.wait("vector", ST, p.st_ye[dd - 4])
                    p.op("vector", lambda e, dd=dd: e.tensor_copy(
                        bfA[:, 512 * (dd % 4):512 * (dd % 4) + 512],
                        Tb[:, 512 * dd:512 * dd + 512]), [(EV, 1)])
                    p.wait("sync", EV, EV.n)
                    p.dma("sync",
                          t["ye_d"][128 * dd:128 * dd + 128,
                                    HALO + ucol:HALO + ucol + 512],
                          bfA[:, 512 * (dd % 4):512 * (dd % 4) + 512], ST)
                    p.st_ye[dd] = ST.n
                    u0 = max(0, 1022 - ucol)
                    if u0 < 512:
                        p.dma("sync",
                              t["ag_y1_in"][128 * dd:128 * dd + 128,
                                            ucol + u0 - 1022:
                                            ucol + 512 - 1022],
                              Tb[:, 512 * dd + u0:512 * dd + 512], ST)
            if W2ALT not in (6, 8):
                p.wait("tensor", EV, EV.n)


def build_final(p, debug):
    t, S, ps = p.t, p.S, p.ps
    LD, MM, EV, SC, ST = (S[k] for k in ("LD", "MM", "EV", "SC", "ST"))
    xa, xb, Tb, Pb, bfA, bfB = (t[k] for k in
                                ("xa", "xb", "Tb", "Pb", "bfA", "bfB"))
    r2, r3 = t["r2"], t["r3"]
    for tt in range(4):
        p.wait("vector", ST, ST.n)
        ev_fc = {}
        for c in range(8):
            xt = xa if c % 2 == 0 else xb
            if c >= 2:
                p.wait("sync", EV, ev_fc[c - 2])
            p.dma("sync", xt[:, 0:512],
                  t["x1_full"][128 * c:128 * c + 128,
                               512 * tt:512 * tt + 512], LD)
            p.dma("sync", xt[:, 512:1024],
                  t["y2_full"][128 * c:128 * c + 128,
                               512 * tt:512 * tt + 512], LD)
            p.wait("vector", LD, LD.n)
            if c >= 2:
                p.wait("vector", MM, MM.n)
            p.op("vector", lambda e, xt=xt, c=c: e.tensor_add(
                Tb[:, 512 * c:512 * c + 512], xt[:, 0:512], xt[:, 512:1024]),
                [])
            p.op("vector", lambda e, c=c: e.tensor_copy(
                Pb[:, 512 * (c % 4):512 * (c % 4) + 512],
                Tb[:, 512 * c:512 * c + 512]), [])
            p.op("vector", lambda e, c=c: e.tensor_mul(
                bfA[:, 512 * (c % 4):512 * (c % 4) + 512],
                Pb[:, 512 * (c % 4):512 * (c % 4) + 512],
                Pb[:, 512 * (c % 4):512 * (c % 4) + 512]), [(EV, 1)])
            ev_fc[c] = EV.n
            p.wait("tensor", EV, EV.n)
            p.mm(ps[2][0:1, 0:512], t["s_ones_cb"][:],
                 Pb[:, 512 * (c % 4):512 * (c % 4) + 512],
                 start=(c == 0), stop=(c == 7))
            p.mm(ps[2][0:1, 512:1024], t["s_ones_cb"][:],
                 bfA[:, 512 * (c % 4):512 * (c % 4) + 512],
                 start=(c == 0), stop=(c == 7), inc=MM)
        _ln_rows_512(p)
        p.wait("tensor", EV, EV.n)
        p.mm(ps[2][:, 0:512], t["s_ones_r"][:], r2[0:1, 0:512], True, True)
        p.mm(ps[2][:, 512:1024], t["s_ones_r"][:], r3[0:1, 0:512], True, True,
             inc=MM)
        p.wait("vector", MM, MM.n)
        for c in range(8):
            p.op("vector", lambda e, c=c: e.tensor_mul(
                Tb[:, 512 * c:512 * c + 512], Tb[:, 512 * c:512 * c + 512],
                ps[2][:, 0:512]), [])
            p.op("vector", lambda e, c=c: e.tensor_add(
                Tb[:, 512 * c:512 * c + 512], Tb[:, 512 * c:512 * c + 512],
                ps[2][:, 512:1024]), [(EV, 1)])
            p.wait("sync", EV, EV.n)
            p.dma("sync", t["outT"][128 * c:128 * c + 128,
                                    512 * tt:512 * tt + 512],
                  Tb[:, 512 * c:512 * c + 512], ST)
        p.wait("tensor", EV, EV.n)
    p.wait("sync", ST, ST.n)


_CACHE = {}


def kernel(**inputs):
    maps = stage_inputs(inputs)
    key = ("nc", DEBUG, UPTO)
    if key not in _CACHE:
        _CACHE[key] = build(debug=DEBUG)
    ncobj = _CACHE[key]
    res = run_bass_kernel_spmd(ncobj, maps, core_ids=list(range(8)))
    out = np.empty((B2, N, D), np.float32)
    for r in range(8):
        B, Qq = r // 4, r % 4
        s0 = 2048 * Qq
        out[B, s0:s0 + TOK, :] = res.results[r]["outT"].T
    if DEBUG:
        kernel._last = res
    return out


# version 21
# speedup vs baseline: 1.1556x; 1.0163x over previous
import dataclasses
import math

import numpy as np
import concourse.bass as bass
import concourse.mybir as mybir
from concourse.bass_utils import run_bass_kernel_spmd

F32 = mybir.dt.float32
BF16 = mybir.dt.bfloat16
AF = mybir.ActivationFunctionType
ALU = mybir.AluOpType
ET = mybir.EngineType
AX = mybir.AxisListType

DEBUG = False
UPTO = 99
FSUB = 99
W2CAP = 22
W2SUB = 2
W2EVAC = 1
W2LOAD = 1
W2ALT = 0
NMM = 1
DMA_TWICE = 0
SKIPP3 = 0
PARANOID3 = False

B2, N, D = 2, 8192, 1024
DH = 64
TOK = 2048
HALO = 1026
QEXT = 3072
HID = 2730
FF1 = 5460
NEG = -1.0e30
LN32 = math.log(32.0)

GROUPS4 = [[0, 1, 2, 3], [4, 5, 6, 7]]
F1_CH = [128] * 21 + [42]
F1_OFF = [128 * i for i in range(22)]
KB = 1024


def brep(ap, dims):
    old = [list(x) for x in ap.ap]
    return dataclasses.replace(ap, ap=[old[0]] + [list(d) for d in dims])



def _rotary_tables(s0):
    p = np.arange(QEXT, dtype=np.float64)
    t = np.maximum(s0 - 1024 + p, 0.0)
    half = np.arange(0, DH, 2, dtype=np.float64)
    inv_freq = 1.0 / 10000.0 ** (half / DH)
    freqs = t[None, :] * inv_freq[:, None]
    freqs = np.concatenate([freqs, freqs], axis=0)
    base = (half + 0.4 * DH) / (1.4 * DH)
    power = (t - N // 2) / 256.0
    scale = base[:, None] ** power[None, :]
    scale = np.concatenate([scale, scale], axis=0)
    cos, sin = np.cos(freqs), np.sin(freqs)
    sgn = np.concatenate([-np.ones(32), np.ones(32)])[:, None]
    c1q = (8.0 * cos * scale).T
    c2q = (8.0 * sin * scale * sgn).T
    c1k = (cos / scale).T
    c2k = (sin / scale * sgn).T
    return np.ascontiguousarray(
        np.concatenate([c1q, c2q, c1k, c2k], axis=1)).astype(np.float32)


def stage_inputs(inp):
    x = np.asarray(inp["L0_attn_x"])
    gamma = np.asarray(inp["L0_attn_gamma"])
    w_qkv = np.asarray(inp["L0_attn_w_qkv"])
    w_out = np.asarray(inp["L0_attn_w_out"])
    fwqkv = np.asarray(inp["L0_ffn_wqkv"])
    fqs = np.asarray(inp["L0_ffn_q_scale"])
    fks = np.asarray(inp["L0_ffn_k_scale"])
    fwo = np.asarray(inp["L0_ffn_wo"])
    flng = np.asarray(inp["L0_ffn_ln_g"])
    flnb = np.asarray(inp["L0_ffn_ln_b"])
    fw1 = np.asarray(inp["L0_ffn_w1"])
    fb1 = np.asarray(inp["L0_ffn_b1"])
    fw2 = np.asarray(inp["L0_ffn_w2"])
    fb2 = np.asarray(inp["L0_ffn_b2"])

    woutT = np.ascontiguousarray(w_out.T)
    gamma_row = np.ascontiguousarray(gamma.reshape(1, D))

    def col128(v):
        return np.ascontiguousarray(v.reshape(8, 128).T)

    def colsF1(v, off):
        out = np.zeros((128, 22), v.dtype)
        for c in range(22):
            n = F1_CH[c]
            out[:n, c] = v[off + F1_OFF[c]: off + F1_OFF[c] + n]
        return np.ascontiguousarray(out)

    maps = []
    for r in range(8):
        B, Qq = r // 4, r % 4
        s0 = 2048 * Qq
        xT = np.ascontiguousarray(x[B, s0:s0 + TOK, :].T)
        hs = [4 * Qq + i for i in range(4)]
        rows = []
        for pi in range(2):
            for h in (hs[2 * pi], hs[2 * pi + 1]):
                rows.append(w_qkv[DH * h: DH * h + DH])
        for pi in range(2):
            for h in (hs[2 * pi], hs[2 * pi + 1]):
                rows.append(w_qkv[1024 + DH * h: 1024 + DH * h + DH])
        for h in hs:
            rows.append(w_qkv[2048 + DH * h: 2048 + DH * h + DH])
        wqkvT = np.ascontiguousarray(np.concatenate(rows, axis=0).T)

        mval = NEG if Qq == 0 else 0.0
        mP0 = np.full((512, 512), mval, np.float32)

        m = {
            "xT": xT.astype(np.float32),
            "wqkvT": wqkvT.astype(np.float32),
            "gamma_row": gamma_row.astype(np.float32),
            "woutT": woutT.astype(np.float32),
            "rotq": _rotary_tables(s0),
            "mP0": mP0,
        }
        for li in range(2):
            m[f"fwqkv{li}"] = np.ascontiguousarray(fwqkv[li]).astype(np.float32)
            m[f"fwo{li}"] = np.ascontiguousarray(fwo[li]).astype(np.float32)
            m[f"fw1{li}"] = np.ascontiguousarray(fw1[li]).astype(np.float32)
            m[f"fw2{li}"] = np.ascontiguousarray(fw2[li]).astype(np.float32)
            m[f"qs_row{li}"] = np.ascontiguousarray(
                fqs[li].reshape(1, DH)).astype(np.float32)
            m[f"ks_row{li}"] = np.ascontiguousarray(
                fks[li].reshape(1, DH)).astype(np.float32)
            m[f"lng{li}"] = col128(flng[li]).astype(np.float32)
            m[f"lnb{li}"] = col128(flnb[li]).astype(np.float32)
            m[f"b2c{li}"] = col128(fb2[li]).astype(np.float32)
            m[f"b1a{li}"] = colsF1(fb1[li], 0).astype(np.float32)
            m[f"b1g{li}"] = colsF1(fb1[li], HID).astype(np.float32)
        maps.append(m)
    return maps



class Ct:
    __slots__ = ("h", "n", "name")

    def __init__(self, h, name):
        self.h, self.n, self.name = h, 0, name


class RingCt:
    __slots__ = ("subs", "idx", "name")

    def __init__(self, subs, name):
        self.subs, self.idx, self.name = subs, 0, name

    @property
    def n(self):
        return tuple(c.n for c in self.subs)


class P:
    ENGS = ("sync", "tensor", "vector", "scalar", "gpsimd")

    DMA_DEPTH = {"sync": 4, "gpsimd": 4}

    def __init__(self, nc):
        self.nc = nc
        self.q = {e: [] for e in self.ENGS}
        self.sems = {}
        self.t = {}
        self.pb_st = {}
        self.w2_slot_mm = [0, 0]
        self.dma_fifo = {}
        self.st_ye = {}
        self.wst_ev = 0
        self.w1_slot_mm = [0, 0, 0, 0]

    def sem(self, name):
        if name not in self.sems:
            self.sems[name] = Ct(self.nc.alloc_semaphore(name), name)
        return self.sems[name]

    def wait(self, eng, ct, val=None):
        if isinstance(ct, RingCt):
            vals = ct.n if val is None else val
            if not isinstance(vals, tuple):
                return
            for c, v in zip(ct.subs, vals):
                self.wait(eng, c, v)
            return
        v = ct.n if val is None else val
        if v <= 0:
            return
        self.q[eng].append(lambda e, h=ct.h, v=v: e.wait_ge(h, v))

    def op(self, eng, fn, incs=()):
        snap = tuple(incs)

        def run(e, fn=fn, snap=snap):
            ins = fn(e)
            if ins is not None:
                for ct, amt in snap:
                    ins.then_inc(ct.h, amt)
        self.q[eng].append(run)
        for ct, amt in snap:
            ct.n += amt

    def raw(self, eng, fn):
        self.q[eng].append(fn)

    def dma(self, eng, dst, src, sem, slow=False):
        if isinstance(sem, RingCt):
            sub = sem.subs[sem.idx % len(sem.subs)]
            sem.idx += 1
            self.wait(eng, sub, sub.n)
            self.op(eng, lambda e, d=dst, s=src, sl=slow: e.dma_start(
                out=d, in_=s, allow_slow_non_contiguous=sl), [(sub, 16)])
            return
        self.op(eng, lambda e, d=dst, s=src, sl=slow: e.dma_start(
            out=d, in_=s, allow_slow_non_contiguous=sl), [(sem, 16)])

    def mm(self, out, lhsT, rhs, start, stop, inc=None):
        incs = [(inc, 1)] if inc is not None else []
        self.op("tensor",
                lambda e, o=out, l=lhsT, r=rhs, s=start, st=stop:
                e.matmul(o, l, r, start=s, stop=st), incs)

    def finalize(self):
        nc = self.nc
        with nc.Block() as block:
            @block.sync
            def _(e):
                for fn in self.q["sync"]:
                    fn(e)

            @block.tensor
            def _(e):
                for fn in self.q["tensor"]:
                    fn(e)

            @block.vector
            def _(e):
                for fn in self.q["vector"]:
                    fn(e)

            @block.scalar
            def _(e):
                for fn in self.q["scalar"]:
                    fn(e)

            @block.gpsimd
            def _(e):
                for fn in self.q["gpsimd"]:
                    fn(e)



def build(debug=False):
    nc = bass.Bass("TRN2", target_bir_lowering=False, debug=False)
    p = P(nc)
    t = p.t

    def ein(name, shape, dt=F32):
        t[name] = nc.dram_tensor(name, list(shape), dt, kind="ExternalInput").ap()

    def eout(name, shape, dt=F32):
        t[name] = nc.dram_tensor(name, list(shape), dt, kind="ExternalOutput").ap()

    def idram(name, shape, dt):
        t[name] = nc.dram_tensor(name, list(shape), dt, kind="Internal").ap()

    ein("xT", (D, TOK))
    ein("wqkvT", (D, 768))
    ein("gamma_row", (1, D))
    ein("woutT", (D, D))
    ein("rotq", (QEXT, 256))
    ein("mP0", (512, 512))
    for li in range(2):
        ein(f"fwqkv{li}", (D, 1536))
        ein(f"fwo{li}", (512, D))
        ein(f"fw1{li}", (D, FF1))
        ein(f"fw2{li}", (HID, D))
        ein(f"qs_row{li}", (1, DH))
        ein(f"ks_row{li}", (1, DH))
        ein(f"lng{li}", (128, 8))
        ein(f"lnb{li}", (128, 8))
        ein(f"b2c{li}", (128, 8))
        ein(f"b1a{li}", (128, 22))
        ein(f"b1g{li}", (128, 22))
    eout("outT", (D, TOK))

    idram("ag_xn_in", (D, TOK), BF16)
    idram("ag_xn_out", (4 * D, TOK), BF16)
    idram("q_d", (256, N), BF16)
    idram("ag_o_in", (260, N), BF16)
    idram("ag_o_out", (1040, N), BF16)
    idram("o_mine", (1040, TOK), BF16)
    idram("ag_l_in", (4, N), F32)
    idram("ag_l_out", (16, N), F32)
    idram("l_d", (16, TOK), F32)
    idram("linv_d", (16, TOK), F32)
    idram("x1_full", (D, TOK), F32)
    idram("y1_full", (D, TOK), F32)
    idram("y2_full", (D, TOK), F32)
    for nm in ("x1", "y1"):
        idram(f"ag_{nm}_in", (D, HALO), F32)
        idram(f"ag_{nm}_out", (4 * D, HALO), F32)
        idram(f"{nm}_halo", (D, HALO), F32)
    idram("xe_d", (D, 2 + QEXT), BF16)
    idram("ye_d", (D, 2 + QEXT), BF16)
    idram("qkv_d", (1024, QEXT), BF16)
    idram("v_d", (QEXT, 520), BF16)
    idram("o_d", (520, 2560), BF16)
    idram("il_row", (128, 160), F32)
    idram("xmid_d", (D, 2049), F32)

    if debug:
        eout("dbg_xn", (4 * D, TOK))
        eout("dbg_q", (256, N))
        eout("dbg_k", (256, N))
        eout("dbg_v", (N, 260))
        eout("dbg_o", (1040, TOK))
        eout("dbg_oin", (260, N))
        eout("dbg_lin", (4, N))
        eout("dbg_onm", (128, 4096))
        eout("dbg_t1", (128, 4096))
        eout("dbg_linv", (16, TOK))
        eout("dbg_x1", (D, TOK))
        eout("dbg_ph", (128, 2048))
        eout("dbg_il", (128, 160))
        eout("dbg_ln", (128, 8192))
        eout("dbg_gt", (128, 22528))
        eout("dbg_pb", (128, 2048))
        eout("dbg_ba", (128, 2048))
        eout("dbg_xe", (D, 2 + QEXT))
        eout("dbg_qkv", (1024, QEXT))
        eout("dbg_v2", (QEXT, 520))
        eout("dbg_oi", (520, 2560))
        eout("dbg_xm", (D, 2049))
        eout("dbg_y1", (D, TOK))

    idx = np.arange(128)
    tri_pf = np.where(idx[:, None] >= idx[None, :], 0.0, NEG).astype(np.float32)
    tri_fp = np.where(idx[None, :] >= idx[:, None], 0.0, NEG).astype(np.float32)
    e2 = np.zeros((2, 128), np.float32)
    e2[0, :64] = 1.0
    e2[1, 64:] = 1.0
    t["c_tri_pf"] = nc.inline_tensor(tri_pf, "c_tri_pf").ap()
    t["c_tri_fp"] = nc.inline_tensor(tri_fp, "c_tri_fp").ap()
    t["c_ident"] = nc.inline_tensor(np.eye(128, dtype=np.float32), "c_ident").ap()
    t["c_ones_r"] = nc.inline_tensor(np.ones((1, 128), np.float32),
                                     "c_ones_r").ap()
    t["c_mones_r"] = nc.inline_tensor(-np.ones((1, 128), np.float32),
                                      "c_mones_r").ap()
    t["c_ones_c"] = nc.inline_tensor(np.ones((128, 1), np.float32),
                                     "c_ones_c").ap()
    t["c_e2"] = nc.inline_tensor(e2, "c_e2").ap()
    t["c_m30"] = nc.inline_tensor(np.full((128, 1), -30.0, np.float32),
                                  "c_m30").ap()

    def sb(name, shape, dt, off):
        h = nc.alloc_sbuf_tensor_at(name, list(shape), dt,
                                    offset=int((off + 16.0) * KB))
        t[name] = h.ap()
        return t[name]

    sb("s_trip", (128, 128), F32, 0.0)
    sb("s_trif", (128, 128), F32, 0.5)
    sb("s_id", (128, 128), F32, 1.0)
    sb("s_ones_r", (1, 128), F32, 1.5)
    sb("s_mones_r", (1, 128), F32, 2.0)
    sb("s_ones_cf", (128, 1), F32, 2.5)
    sb("s_ones_cb", (128, 1), BF16, 3.0)
    sb("s_e2", (2, 128), F32, 3.5)
    sb("s_m30", (128, 1), F32, 5.75)
    sb("s_lng", (128, 8), F32, 4.0)
    sb("s_lnb", (128, 8), F32, 4.25)
    sb("s_b2c", (128, 8), F32, 4.5)
    sb("s_qsb", (128, 64), F32, 4.75)
    sb("s_ksb", (128, 64), F32, 5.0)
    sb("s_b1a", (128, 22), F32, 5.25)
    sb("s_b1g", (128, 22), F32, 5.5)
    sb("lr2", (2, 512), F32, 6.0)
    sb("lst", (1, 1024), F32, 200.0)
    sb("dbgrow", (1, 1024), F32, 186.0)
    sb("li_sb", (128, 160), F32, 8.0)
    sb("s_gp1", (1, D), F32, 8.75)
    sb("qs_r", (1, 64), F32, 12.75)
    sb("ks_r", (1, 64), F32, 13.0)
    sb("mcol", (128, 8), F32, 13.25)
    sb("r0", (1, 2048), F32, 14.0)
    sb("r1", (1, 2048), F32, 22.0)
    sb("r2", (1, 2048), F32, 30.0)
    sb("r3", (1, 2048), F32, 38.0)
    sb("Tb", (128, 4096), F32, 46.0)
    sb("Pb", (128, 4096), BF16, 62.0)
    sb("ost", (128, 1024), BF16, 70.0)
    sb("evA", (128, 2048), F32, 72.0)
    sb("evB", (128, 2048), F32, 80.0)
    sb("bfA", (128, 2048), BF16, 88.0)
    sb("bfB", (128, 2048), BF16, 92.0)
    sb("xslab", (128, 4096), BF16, 88.0)
    sb("xa", (128, 2048), F32, 96.0)
    sb("xb", (128, 2048), F32, 104.0)
    sb("l_sb", (16, 2048), F32, 112.0)
    sb("KT", (128, 2 * N), BF16, 120.0)
    sb("VR", (128, 64 * 260), BF16, 152.0)
    sb("wq_sb", (128, 8 * 768), BF16, 186.0)
    sb("qbuf", (128, 2 * 512), BF16, 198.0)
    sb("wo_sb", (128, 8 * 1024), BF16, 120.0)
    sb("kb", (128, 4 * 1024), BF16, 120.0)
    sb("qb2", (128, 4 * 512), BF16, 128.0)
    sb("s_mP0", (128, 4 * 512), F32, 132.0)
    sb("vti", (128, 2 * 520), BF16, 140.0)
    sb("Cbuf", (128, 2 * 512), F32, 142.5)
    sb("subT", (128, 2 * 1024), F32, 146.5)
    sb("Pi", (128, 2 * 1024), BF16, 154.5)
    sb("rot_sb", (128, 2 * 256), F32, 158.5)
    sb("fqkv_sb", (128, 8 * 1536), BF16, 164.0)
    sb("fwo_sb", (128, 4 * 1024), BF16, 188.0)
    sb("gated", (128, 22 * 1024), BF16, 120.0)
    sb("w1t", (128, 4 * 1024), BF16, 164.0)
    sb("w2u", (128, 2 * 1024), BF16, 180.0)
    sb("wst", (128, 2048), F32, 172.0)
    sb("lnout", (128, 8 * 1024), BF16, 96.0)

    ps = []
    for i in range(4):
        h = nc.alloc_psum_tensor(f"ps{i}", [128, 1024], F32)
        ps.append(h.ap())
    p.ps = ps
    p.banks = [(ps[i], 512 * hh) for i in range(4) for hh in range(2)]

    for nm in ("mm", "ev", "sc", "cc", "sw"):
        p.sem(nm)
    rings = {}
    for nm, ring in (("ld", 1), ("st", 4), ("gl", 1)):
        subs = [p.sem(f"{nm}{i}") for i in range(ring)]
        rings[nm] = RingCt(subs, nm)
    p.S = {"MM": p.sems["mm"], "EV": p.sems["ev"], "SC": p.sems["sc"],
           "CC": p.sems["cc"], "SW": p.sems["sw"],
           "LD": rings["ld"], "ST": rings["st"], "GL": rings["gl"]}

    build_init(p, debug)
    if UPTO >= 1:
        build_p0_rms(p, debug)
    if UPTO >= 2:
        build_p2_qkv(p, debug)
    if UPTO >= 3 and not SKIPP3:
        build_p3_attn(p, debug)
    elif UPTO >= 3:
        p.st_p3 = p.S["ST"].n
    if UPTO >= 4:
        build_p5_x1(p, debug)
    if UPTO >= 5:
        build_ffn_layer(p, 0, debug)
    if UPTO >= 6:
        build_ffn_layer(p, 1, debug)
    if UPTO >= 7:
        build_final(p, debug)
    for eng in ("sync", "gpsimd"):
        p.wait(eng, p.S["ST"])
        p.wait(eng, p.S["LD"])
        p.wait(eng, p.S["GL"])
        p.wait(eng, p.S["SW"])
    p.finalize()
    return nc


def build_init(p, debug):
    t, S = p.t, p.S
    LD, GL, EV = S["LD"], S["GL"], S["EV"]
    for nm, src in (("s_trip", "c_tri_pf"), ("s_trif", "c_tri_fp"),
                    ("s_id", "c_ident"), ("s_ones_r", "c_ones_r"),
                    ("s_mones_r", "c_mones_r"), ("s_ones_cf", "c_ones_c"),
                    ("s_e2", "c_e2"), ("s_m30", "c_m30"),
                    ("s_gp1", "gamma_row")):
        p.dma("sync", t[nm][:], t[src], LD)
    p.dma("gpsimd", t["s_ones_cb"][:], t["c_ones_c"], GL)
    p.dma("gpsimd", t["wq_sb"][:].rearrange("p (c f) -> p c f", f=768),
          t["wqkvT"].rearrange("(c p) f -> p c f", p=128), GL)
    p.wait("vector", LD)
    p.op("vector", lambda e: e.tensor_scalar(
        t["s_gp1"][:], t["s_gp1"][:], 32.0, 32.0, ALU.mult, ALU.add),
        [(EV, 1)])
    p.op("vector", lambda e: e.memset(
        t["VR"][:, :].rearrange("p (b h d) -> p b h d", h=4, d=65)
        [:, :, :, 64:65], 1.0), [(EV, 1)])
    p.op("vector", lambda e: e.memset(t["Tb"][:], 0.0), [(EV, 1)])
    p.ev_init = EV.n
    p.gl_init = GL.n


def build_p0_rms(p, debug):
    t, S, ps = p.t, p.S, p.ps
    LD, MM, EV, SC, ST, CC = (S[k] for k in ("LD", "MM", "EV", "SC", "ST", "CC"))
    xa, xb, evA, evB, bfA = t["xa"], t["xb"], t["evA"], t["evB"], t["bfA"]
    r0, r1 = t["r0"], t["r1"]
    ss_banks = [(ps[2], 0), (ps[2], 512), (ps[3], 0), (ps[3], 512)]

    sq_ev = []
    for c in range(8):
        xt = xa if c % 2 == 0 else xb
        sq = evA if c % 2 == 0 else evB
        if c >= 2:
            p.wait("sync", EV, sq_ev[c - 2])
        p.dma("sync", xt[:], t["xT"][128 * c:128 * c + 128, :], LD)
        p.wait("vector", LD, LD.n)
        if c >= 2:
            p.wait("vector", MM, MM.n)
        p.op("vector", lambda e, xt=xt, sq=sq: e.tensor_mul(sq[:], xt[:], xt[:]),
             [(EV, 1)])
        sq_ev.append(EV.n)
        p.wait("tensor", EV, EV.n)
        for j in range(4):
            bank, col = ss_banks[j]
            p.mm(bank[0:1, col:col + 512], t["s_ones_cf"][:],
                 sq[:, 512 * j:512 * j + 512], start=(c == 0), stop=(c == 7),
                 inc=MM)
    mm_ss = MM.n
    p.wait("vector", MM, mm_ss)
    for j in range(4):
        bank, col = ss_banks[j]
        p.op("vector", lambda e, b=bank, col=col, j=j: e.tensor_copy(
            r0[0:1, 512 * j:512 * j + 512], b[0:1, col:col + 512]), [])
    p.op("vector", lambda e: e.tensor_single_scalar(
        r0[:], r0[:], 1e-24, ALU.max), [(EV, 1)])
    p.wait("scalar", EV, EV.n)
    p.op("scalar", lambda e: e.activation(r1[:], r0[:], AF.Ln), [])
    p.op("scalar", lambda e: e.activation(
        r1[:], r1[:], AF.Exp, scale=-0.5), [(SC, 1)])

    p.wait("tensor", SC, SC.n)
    st_list = []
    ev_list = []
    for c in range(8):
        xt = xa if c % 2 == 0 else xb
        if c == 0:
            p.wait("sync", MM, mm_ss)
        if c >= 2:
            p.wait("sync", EV, ev_list[4 * (c - 2) + 3])
        p.dma("sync", xt[:], t["xT"][128 * c:128 * c + 128, :], LD)
        p.wait("tensor", LD, LD.n)
        for j in range(4):
            i = 4 * c + j
            bank = ps[0] if j % 2 == 0 else ps[1]
            if i >= 2:
                p.wait("tensor", EV, ev_list[i - 2])
            p.mm(bank[:, 0:512], t["s_gp1"][0:1, 128 * c:128 * c + 128],
                 r1[0:1, 512 * j:512 * j + 512], start=True, stop=True, inc=MM)
            p.wait("vector", MM, MM.n)
            if i >= 2:
                p.wait("vector", ST, st_list[i - 2])
            p.op("vector", lambda e, xt=xt, j=j, bank=bank: e.tensor_mul(
                bfA[:, 512 * (j % 2):512 * (j % 2) + 512],
                xt[:, 512 * j:512 * j + 512], bank[:, 0:512]), [(EV, 1)])
            ev_list.append(EV.n)
            p.wait("sync", EV, EV.n)
            p.dma("sync",
                  t["ag_xn_in"][128 * c:128 * c + 128, 512 * j:512 * j + 512],
                  bfA[:, 512 * (j % 2):512 * (j % 2) + 512], ST)
            st_list.append(ST.n)

    p.wait("gpsimd", ST, ST.n)
    p.op("gpsimd", lambda e: e.collective_compute(
        "AllGather", ALU.bypass, replica_groups=GROUPS4,
        ins=[t["ag_xn_in"]], outs=[t["ag_xn_out"]]), [(CC, 1)])
    p.cc_xn = CC.n


def build_p2_qkv(p, debug):
    t, S, ps = p.t, p.S, p.ps
    LD, GL, MM, EV, ST, CC = (S[k] for k in ("LD", "GL", "MM", "EV", "ST", "CC"))
    bfA, bfB = t["bfA"], t["bfB"]
    qk_dst = [(ps[0], 0), (ps[0], 512), (ps[1], 0), (ps[1], 512)]
    p.wait("sync", CC, p.cc_xn)
    p.wait("tensor", GL, p.gl_init)
    p.wait("tensor", EV, p.ev_init)

    mm_slot = []
    ev_tt = 0
    for tt in range(16):
        q4, col0 = tt // 4, 512 * (tt % 4)
        for c in range(8):
            li = tt * 8 + c
            slot = c % 4
            xnb = bfA if slot < 2 else bfB
            scol = 512 * (slot % 2)
            if li >= 4:
                p.wait("sync", MM, mm_slot[li - 4])
            p.dma("sync", xnb[:, scol:scol + 512],
                  t["ag_xn_out"][1024 * q4 + 128 * c:1024 * q4 + 128 * c + 128,
                                 col0:col0 + 512], LD)
            p.wait("tensor", LD, LD.n)
            if c == 0 and tt > 0:
                p.wait("tensor", EV, ev_tt)
            for f in range(4):
                bank, bcol = qk_dst[f]
                p.mm(bank[:, bcol:bcol + 512],
                     t["wq_sb"][:, 768 * c + 128 * f:768 * c + 128 * f + 128],
                     xnb[:, scol:scol + 512], start=(c == 0), stop=(c == 7))
            for tsub in range(4):
                p.mm(ps[2][:, 256 * tsub:256 * tsub + 256],
                     xnb[:, scol + 128 * tsub:scol + 128 * tsub + 128],
                     t["wq_sb"][:, 768 * c + 512:768 * c + 768],
                     start=(c == 0 and tsub % 2 == 0), stop=(c == 7),
                     inc=MM if tsub == 3 else None)
            mm_slot.append(MM.n)
        p.wait("vector", MM, MM.n)
        qst = t["Pb"]
        p.wait("vector", ST, ST.n)
        for f in range(2):
            bank, bcol = qk_dst[f]
            p.op("vector", lambda e, b=bank, bc=bcol, f=f: e.tensor_copy(
                qst[:, 512 * f:512 * f + 512], b[:, bc:bc + 512]), [(EV, 1)])
        for f in range(2, 4):
            bank, bcol = qk_dst[f]
            p.op("vector", lambda e, b=bank, bc=bcol, f=f, q4=q4, col0=col0:
                 e.tensor_copy(
                     t["KT"][:, N * (f - 2) + 2048 * q4 + col0:
                             N * (f - 2) + 2048 * q4 + col0 + 512],
                     b[:, bc:bc + 512]), [(EV, 1)])
        for tsub in range(4):
            tok = 4 * tt + tsub
            p.op("vector", lambda e, ts=tsub, tok=tok: e.tensor_copy(
                t["VR"][:, 260 * tok:260 * tok + 260].rearrange(
                    "p (h d) -> p h d", d=65)[:, :, 0:64],
                ps[2][:, 256 * ts:256 * ts + 256].rearrange(
                    "p (h d) -> p h d", d=64)), [(EV, 1)])
        ev_tt = EV.n
        p.wait("sync", EV, EV.n)
        for f in range(2):
            p.dma("sync", t["q_d"][128 * f:128 * f + 128,
                                   2048 * q4 + col0:2048 * q4 + col0 + 512],
                  qst[:, 512 * f:512 * f + 512], ST)
    p.st_p2 = ST.n
    p.ev_p2 = EV.n

    if debug:
        p.wait("gpsimd", ST, ST.n)
        p.wait("gpsimd", EV, EV.n)
        p.dma("gpsimd", t["dbg_xn"], t["ag_xn_out"], GL)
        p.dma("gpsimd", t["dbg_q"], t["q_d"], GL)
        for f in range(2):
            p.dma("gpsimd", t["dbg_k"][128 * f:128 * f + 128, :],
                  t["KT"][:, N * f:N * f + N], GL)
        p.dma("gpsimd", t["dbg_v"].rearrange("(b p) c -> p b c", p=128),
              t["VR"][:, :].rearrange("p (b c) -> p b c", c=260), GL)


def build_p3_attn(p, debug):
    t, S, ps = p.t, p.S, p.ps
    LD, MM, EV, SC, ST = (S[k] for k in ("LD", "MM", "EV", "SC", "ST"))
    KT, VR, qbuf, Tb, Pb, ost = (t[k] for k in
                                 ("KT", "VR", "qbuf", "Tb", "Pb", "ost"))
    p.wait("scalar", EV, p.ev_init)
    p.wait("sync", ST, p.st_p2)
    p.wait("scalar", ST, p.st_p2)
    p.wait("tensor", EV, p.ev_p2)
    smm_last = []
    ev_o_prev = 0
    for un in range(32):
        qt, hp = un // 2, un % 2
        qslot = un % 2
        if un >= 2:
            p.wait("sync", MM, smm_last[un - 2])
        p.dma("sync", qbuf[:, 512 * qslot:512 * qslot + 512],
              t["q_d"][128 * hp:128 * hp + 128, 512 * qt:512 * qt + 512], LD)
        p.wait("tensor", LD, LD.n)
        nkc = 4 * qt + 4
        tanh_sc = {}
        pv_mm = {}
        dsub_ev = {}
        for kc in range(nkc):
            j = kc - 4 * qt
            n0 = 128 * j if j >= 0 else 0
            Sps = ps[kc % 2]
            if kc >= 2:
                p.wait("tensor", SC, tanh_sc[kc - 2])
            p.mm(Sps[:, n0:512],
                 KT[0:64, N * hp + 128 * kc:N * hp + 128 * kc + 128],
                 qbuf[0:64, 512 * qslot + n0:512 * qslot + 512], True, True)
            p.mm(Sps[:, 512 + n0:1024],
                 KT[64:128, N * hp + 128 * kc:N * hp + 128 * kc + 128],
                 qbuf[64:128, 512 * qslot + n0:512 * qslot + 512], True, True,
                 inc=MM)
            tslot = kc % 4
            p.wait("scalar", MM, MM.n)
            p.op("scalar", lambda e, ts=tslot, Sps=Sps: e.activation(
                Tb[:, 1024 * ts:1024 * ts + 1024], Sps[:, 0:1024], AF.Tanh,
                scale=1.0 / 400.0), [(SC, 1)])
            tanh_sc[kc] = SC.n
            if j >= 0:
                p.wait("vector", SC, SC.n)
                for head in range(2):
                    p.op("vector", lambda e, ts=tslot, h=head, n0=n0:
                         e.tensor_add(
                             Tb[:, 1024 * ts + 512 * h + n0:
                                1024 * ts + 512 * h + n0 + 128],
                             Tb[:, 1024 * ts + 512 * h + n0:
                                1024 * ts + 512 * h + n0 + 128],
                             t["s_trif"][:]), [(EV, 1)])
                dsub_ev[kc] = EV.n
            if kc % 2 == 1 or kc == nkc - 1:
                pair = kc // 2
                pb = pair % 2
                base_kc = 2 * pair
                width = 1024 * (kc - base_kc + 1)
                for kk in range(base_kc, kc + 1):
                    if kk in dsub_ev:
                        p.wait("scalar", EV, dsub_ev[kk])
                if pair >= 2:
                    p.wait("scalar", MM, pv_mm[pair - 2])
                p.op("scalar", lambda e, pb=pb, b=base_kc, w=width:
                     e.activation(Pb[:, 2048 * pb:2048 * pb + w],
                                  Tb[:, 1024 * (b % 4):1024 * (b % 4) + w],
                                  AF.Exp, scale=50.0, bias=t["s_m30"][:]),
                     [(SC, 1)])
                p.wait("tensor", SC, SC.n)
                if pair == 0:
                    p.wait("tensor", EV, ev_o_prev)
                for kk in range(base_kc, kc + 1):
                    jj = kk - 4 * qt
                    nn0 = 128 * jj if jj >= 0 else 0
                    for head in range(2):
                        hloc = 2 * hp + head
                        p.mm(ps[2][0:65, 512 * head + nn0:512 * head + 512],
                             VR[:, 260 * kk + 65 * hloc:
                                260 * kk + 65 * hloc + 65],
                             Pb[:, 2048 * pb + 1024 * (kk - base_kc) +
                                512 * head + nn0:
                                2048 * pb + 1024 * (kk - base_kc) +
                                512 * head + 512],
                             start=(kk == 0), stop=(kk == nkc - 1),
                             inc=MM if (kk == kc and head == 1) else None)
                pv_mm[pair] = MM.n
            if PARANOID3:
                p.wait("tensor", EV, EV.n)
                p.wait("tensor", SC, SC.n)
                p.wait("scalar", EV, EV.n)
                p.wait("scalar", MM, MM.n)
                p.wait("vector", MM, MM.n)
                p.wait("vector", SC, SC.n)
                p.wait("sync", MM, MM.n)
        smm_last.append(MM.n)
        p.wait("vector", MM, MM.n)
        p.wait("vector", ST, ST.n)
        for head in range(2):
            p.op("vector", lambda e, h=head: e.tensor_copy(
                ost[0:65, 512 * h:512 * h + 512],
                ps[2][0:65, 512 * h:512 * h + 512]), [])
            p.op("vector", lambda e, h=head: e.tensor_copy(
                t["lst"][0:1, 512 * h:512 * h + 512],
                ps[2][64:65, 512 * h:512 * h + 512]), [(EV, 1)])
        ev_o_prev = EV.n
        p.wait("sync", EV, EV.n)
        for head in range(2):
            p.dma("sync", t["ag_o_in"][65 * (2 * hp + head):
                                       65 * (2 * hp + head) + 65,
                                       512 * qt:512 * qt + 512],
                  ost[0:65, 512 * head:512 * head + 512], ST)
            p.dma("sync", t["ag_l_in"][2 * hp + head:2 * hp + head + 1,
                                       512 * qt:512 * qt + 512],
                  t["lst"][0:1, 512 * head:512 * head + 512], ST)
    p.st_p3 = ST.n


def build_p5_x1(p, debug):
    t, S, ps = p.t, p.S, p.ps
    LD, GL, MM, EV, SC, ST, CC, SW = (S[k] for k in
                                      ("LD", "GL", "MM", "EV", "SC", "ST",
                                       "CC", "SW"))
    nc = p.nc
    p.wait("gpsimd", ST, p.st_p3)
    if debug:
        p.dma("gpsimd", t["dbg_oin"], t["ag_o_in"], GL)
        p.dma("gpsimd", t["dbg_lin"], t["ag_l_in"], GL)
        p.wait("gpsimd", GL, GL.n)
    p.op("gpsimd", lambda e: e.collective_compute(
        "AllGather", ALU.bypass, replica_groups=GROUPS4,
        ins=[t["ag_o_in"]], outs=[t["ag_o_out"]]), [(CC, 1)])
    p.op("gpsimd", lambda e: e.collective_compute(
        "AllGather", ALU.bypass, replica_groups=GROUPS4,
        ins=[t["ag_l_in"]], outs=[t["ag_l_out"]]), [(CC, 1)])
    p.wait("gpsimd", CC, CC.n)

    sw_base = SW.n

    def sw_omine(e):
        pid = nc.partition_id(engines=[ET.Pool])
        for arm in e.Switch(pid, 8):
            qq = arm % 4
            e.dma_start(out=t["o_mine"],
                        in_=t["ag_o_out"][:, 2048 * qq:2048 * qq + 2048]
                        ).then_inc(SW.h, 16)
            e.dma_start(out=t["l_d"],
                        in_=t["ag_l_out"][:, 2048 * qq:2048 * qq + 2048]
                        ).then_inc(SW.h, 16)
    p.raw("gpsimd", sw_omine)
    SW.n += 32

    p.wait("sync", SW, SW.n)
    p.dma("sync", t["l_sb"][:], t["l_d"], LD)
    p.wait("vector", LD, LD.n)
    p.op("vector", lambda e: e.reciprocal(t["l_sb"][:], t["l_sb"][:]), [(EV, 1)])
    p.wait("sync", EV, EV.n)
    p.dma("sync", t["linv_d"], t["l_sb"][:], ST)
    st_linv = ST.n
    p.dma("gpsimd", t["wo_sb"][:].rearrange("p (c f) -> p c f", f=1024),
          t["woutT"].rearrange("(c p) f -> p c f", p=128), GL)
    p.wait("tensor", GL, GL.n)
    p.wait("sync", ST, st_linv)

    bfA, bfB, Pb, Tb = t["bfA"], t["bfB"], t["Pb"], t["Tb"]
    xa, xb = t["xa"], t["xb"]
    r0, r1, r2, r3 = t["r0"], t["r1"], t["r2"], t["r3"]
    lr2 = t["lr2"]

    for tt in range(4):
        ev_norm = []
        for fc in range(8):
            oslot = fc % 2
            if fc >= 2:
                p.wait("sync", EV, ev_norm[fc - 2])
            p.dma("sync", bfA[0:64, 512 * oslot:512 * oslot + 512],
                  t["o_mine"][65 * 2 * fc:65 * 2 * fc + 64,
                              512 * tt:512 * tt + 512], LD)
            p.dma("sync", bfA[64:128, 512 * oslot:512 * oslot + 512],
                  t["o_mine"][65 * (2 * fc + 1):65 * (2 * fc + 1) + 64,
                              512 * tt:512 * tt + 512], LD)
            p.wait("sync", MM, MM.n)
            p.dma("sync", lr2[:], t["linv_d"][2 * fc:2 * fc + 2,
                                              512 * tt:512 * tt + 512], LD)
            p.wait("tensor", LD, LD.n)
            if fc >= 1 or tt >= 1:
                p.wait("tensor", EV, EV.n)
            p.mm(ps[3][:, 0:512], t["s_e2"][:], lr2[:], True, True, inc=MM)
            p.wait("vector", MM, MM.n)
            p.wait("vector", LD, LD.n)
            p.op("vector", lambda e, fc=fc, osl=oslot: e.tensor_mul(
                Pb[:, 512 * fc:512 * fc + 512],
                bfA[:, 512 * osl:512 * osl + 512], ps[3][:, 0:512]), [(EV, 1)])
            ev_norm.append(EV.n)
        p.wait("tensor", EV, EV.n)
        p.wait("vector", ST, ST.n)
        ev_t1 = []
        for ddc in range(8):
            dst = ps[ddc % 2][:, 512 * ((ddc // 2) % 2):
                              512 * ((ddc // 2) % 2) + 512]
            if ddc >= 4:
                p.wait("tensor", EV, ev_t1[ddc - 4])
            for fc in range(8):
                p.mm(dst, t["wo_sb"][:, 1024 * fc + 128 * ddc:
                                     1024 * fc + 128 * ddc + 128],
                     Pb[:, 512 * fc:512 * fc + 512],
                     start=(fc == 0), stop=(fc == 7),
                     inc=MM if fc == 7 else None)
            xt = xa if ddc % 2 == 0 else xb
            if ddc >= 2:
                p.wait("gpsimd", EV, ev_t1[ddc - 2])
            p.dma("gpsimd", xt[:, 0:512],
                  t["xT"][128 * ddc:128 * ddc + 128, 512 * tt:512 * tt + 512],
                  GL)
            p.wait("vector", MM, MM.n)
            p.wait("vector", LD, LD.n)
            p.wait("vector", GL, GL.n)
            p.op("vector", lambda e, d=ddc, dst=dst, xt=xt: e.tensor_add(
                Tb[:, 512 * d:512 * d + 512], dst, xt[:, 0:512]), [])
            p.op("vector", lambda e, d=ddc: e.tensor_copy(
                bfB[:, 512 * (d % 4):512 * (d % 4) + 512],
                Tb[:, 512 * d:512 * d + 512]), [])
            p.op("vector", lambda e, d=ddc: e.tensor_mul(
                bfA[:, 512 * (d % 4):512 * (d % 4) + 512],
                bfB[:, 512 * (d % 4):512 * (d % 4) + 512],
                bfB[:, 512 * (d % 4):512 * (d % 4) + 512]), [(EV, 1)])
            ev_t1.append(EV.n)
            p.wait("tensor", EV, EV.n)
            p.mm(ps[2][0:1, 0:512], t["s_ones_cb"][:],
                 bfB[:, 512 * (ddc % 4):512 * (ddc % 4) + 512],
                 start=(ddc == 0), stop=(ddc == 7))
            p.mm(ps[2][0:1, 512:1024], t["s_ones_cb"][:],
                 bfA[:, 512 * (ddc % 4):512 * (ddc % 4) + 512],
                 start=(ddc == 0), stop=(ddc == 7), inc=MM)
        if debug and tt == 0:
            p.wait("gpsimd", EV, EV.n)
            p.dma("gpsimd", t["dbg_onm"], t["Pb"][:, :], GL)
            p.dma("gpsimd", t["dbg_t1"], t["Tb"][:, :], GL)
            p.dma("gpsimd", t["dbg_linv"], t["l_sb"][:], GL)
            gl_tap = GL.n
        _ln_rows_512(p)
        p.wait("tensor", EV, EV.n)
        p.mm(ps[2][:, 0:512], t["s_ones_r"][:], r2[0:1, 0:512], True, True)
        p.mm(ps[2][:, 512:1024], t["s_ones_r"][:], r3[0:1, 0:512], True, True,
             inc=MM)
        p.wait("vector", MM, MM.n)
        if debug and tt == 0:
            p.wait("vector", GL, gl_tap)
        st_xe = {}
        for ddc in range(8):
            p.op("vector", lambda e, d=ddc: e.tensor_mul(
                Tb[:, 512 * d:512 * d + 512], Tb[:, 512 * d:512 * d + 512],
                ps[2][:, 0:512]), [])
            p.op("vector", lambda e, d=ddc: e.tensor_add(
                Tb[:, 512 * d:512 * d + 512], Tb[:, 512 * d:512 * d + 512],
                ps[2][:, 512:1024]), [])
            if ddc >= 4:
                p.wait("vector", ST, st_xe[ddc - 4])
            p.op("vector", lambda e, d=ddc: e.tensor_copy(
                bfB[:, 512 * (d % 4):512 * (d % 4) + 512],
                Tb[:, 512 * d:512 * d + 512]), [(EV, 1)])
            p.wait("sync", EV, EV.n)
            p.dma("sync", t["x1_full"][128 * ddc:128 * ddc + 128,
                                       512 * tt:512 * tt + 512],
                  Tb[:, 512 * ddc:512 * ddc + 512], ST)
            p.dma("sync", t["xe_d"][128 * ddc:128 * ddc + 128,
                                    HALO + 512 * tt:HALO + 512 * tt + 512],
                  bfB[:, 512 * (ddc % 4):512 * (ddc % 4) + 512], ST)
            st_xe[ddc] = ST.n
            u0 = max(0, 1022 - 512 * tt)
            if u0 < 512:
                p.dma("sync",
                      t["ag_x1_in"][128 * ddc:128 * ddc + 128,
                                    512 * tt + u0 - 1022:
                                    512 * tt + 512 - 1022],
                      Tb[:, 512 * ddc + u0:512 * ddc + 512], ST)
        p.wait("tensor", EV, EV.n)

    p.wait("gpsimd", ST, ST.n)
    p.op("gpsimd", lambda e: e.collective_compute(
        "AllGather", ALU.bypass, replica_groups=GROUPS4,
        ins=[t["ag_x1_in"]], outs=[t["ag_x1_out"]]), [(CC, 1)])
    p.wait("gpsimd", CC, CC.n)
    p.op("vector", lambda e: e.memset(t["evA"][:, 0:HALO], 0.0), [(EV, 1)])
    p.wait("gpsimd", EV, EV.n)
    _emit_halo_switch(p, "x1")
    if debug:
        p.wait("gpsimd", ST, ST.n)
        p.wait("gpsimd", SW, SW.n)
        p.dma("gpsimd", t["dbg_o"], t["o_mine"], GL)
        p.dma("gpsimd", t["dbg_x1"], t["x1_full"], GL)


def _ln_rows_512(p):
    t, S, ps = p.t, p.S, p.ps
    MM, EV, SC = S["MM"], S["EV"], S["SC"]
    r0, r1, r2, r3 = t["r0"], t["r1"], t["r2"], t["r3"]
    p.wait("vector", MM, MM.n)
    p.op("vector", lambda e: e.tensor_scalar_mul(
        r0[0:1, 0:512], ps[2][0:1, 0:512], 1.0 / 1024.0), [])
    p.op("vector", lambda e: e.tensor_scalar_mul(
        r1[0:1, 0:512], ps[2][0:1, 512:1024], 1.0 / 1024.0), [])
    p.op("vector", lambda e: e.tensor_mul(
        r2[0:1, 0:512], r0[0:1, 0:512], r0[0:1, 0:512]), [])
    p.op("vector", lambda e: e.tensor_sub(
        r1[0:1, 0:512], r1[0:1, 0:512], r2[0:1, 0:512]), [])
    p.op("vector", lambda e: e.tensor_scalar_add(
        r1[0:1, 0:512], r1[0:1, 0:512], 1e-5), [(EV, 1)])
    p.wait("scalar", EV, EV.n)
    p.op("scalar", lambda e: e.activation(
        r2[0:1, 0:512], r1[0:1, 0:512], AF.Ln), [])
    p.op("scalar", lambda e: e.activation(
        r2[0:1, 0:512], r2[0:1, 0:512], AF.Exp, scale=-0.5), [(SC, 1)])
    p.wait("vector", SC, SC.n)
    p.op("vector", lambda e: e.tensor_mul(
        r3[0:1, 0:512], r0[0:1, 0:512], r2[0:1, 0:512]), [])
    p.op("vector", lambda e: e.tensor_scalar_mul(
        r3[0:1, 0:512], r3[0:1, 0:512], -1.0), [(EV, 1)])


def _emit_halo_switch(p, nm):
    t, S = p.t, p.S
    SW = S["SW"]
    nc = p.nc
    ext = t["xe_d"] if nm == "x1" else t["ye_d"]

    base = SW.n

    def fn(e):
        pid = nc.partition_id(engines=[ET.Pool])
        for arm in e.Switch(pid, 8):
            qq = arm % 4
            k = 0
            for c in range(8):
                if qq == 0:
                    src = t["evA"][:, 0:HALO]
                else:
                    src = t[f"ag_{nm}_out"][1024 * (qq - 1) + 128 * c:
                                            1024 * (qq - 1) + 128 * c + 128, :]
                e.dma_start(out=t[f"{nm}_halo"][128 * c:128 * c + 128, :],
                            in_=src).then_inc(SW.h, 16)
                e.dma_start(out=ext[128 * c:128 * c + 128, 0:HALO],
                            in_=src).then_inc(SW.h, 16)
    p.raw("gpsimd", fn)
    SW.n += 16 * 16


def build_ffn_layer(p, li, debug):
    t, S, ps = p.t, p.S, p.ps
    LD, GL, MM, EV, SC, ST, CC, SW = (S[k] for k in
                                      ("LD", "GL", "MM", "EV", "SC", "ST",
                                       "CC", "SW"))
    in_ext = t["xe_d"] if li == 0 else t["ye_d"]
    in_halo = t["x1_halo"] if li == 0 else t["y1_halo"]
    in_full = t["x1_full"] if li == 0 else t["y1_full"]
    out_full = t["y1_full"] if li == 0 else t["y2_full"]
    bfA, bfB, ost = t["bfA"], t["bfB"], t["ost"]
    evA, evB, Tb, Pb = t["evA"], t["evB"], t["Tb"], t["Pb"]
    banks = p.banks

    p.wait("gpsimd", SW, SW.n)
    p.wait("sync", SW, SW.n)
    p.wait("sync", ST, ST.n)
    p.wait("gpsimd", ST, ST.n)
    if debug and li == 0:
        p.wait("gpsimd", ST, ST.n)
        p.dma("gpsimd", t["dbg_xe"], in_ext, GL)
    p.dma("gpsimd", t["fqkv_sb"][:].rearrange("p (c f) -> p c f", f=1536),
          t[f"fwqkv{li}"].rearrange("(c p) f -> p c f", p=128), GL)
    p.dma("gpsimd", t["fwo_sb"][:].rearrange("p (c f) -> p c f", f=1024),
          t[f"fwo{li}"].rearrange("(c p) f -> p c f", p=128), GL)
    p.dma("sync", t["qs_r"][:], t[f"qs_row{li}"], LD)
    p.dma("sync", t["ks_r"][:], t[f"ks_row{li}"], LD)
    p.dma("sync", t["s_mP0"][:].rearrange("p (a q) -> p a q", q=512),
          t["mP0"].rearrange("(a p) q -> p a q", p=128), LD)
    p.dma("sync", t["s_lng"][:], t[f"lng{li}"], LD)
    p.dma("sync", t["s_lnb"][:], t[f"lnb{li}"], LD)
    p.dma("sync", t["s_b2c"][:], t[f"b2c{li}"], LD)
    p.dma("sync", t["s_b1a"][:], t[f"b1a{li}"], LD)
    p.dma("sync", t["s_b1g"][:], t[f"b1g{li}"], LD)
    p.wait("tensor", GL, GL.n)
    p.wait("tensor", LD, LD.n)
    for nm_r, nm_b in (("qs_r", "s_qsb"), ("ks_r", "s_ksb")):
        p.mm(ps[3][:, 0:64], t["s_ones_r"][:], t[nm_r][:], True, True, inc=MM)
        p.wait("vector", MM, MM.n)
        p.op("vector", lambda e, nb=nm_b: e.tensor_copy(
            t[nb][:], ps[3][:, 0:64]), [(EV, 1)])
        p.wait("tensor", EV, EV.n)

    ev_slots = 0
    mm_v_prev = 0
    for tt in range(6):
        if tt > 0:
            p.wait("sync", MM, mm_v_prev)
        for hf in range(2):
            coff = (2 if hf == 0 else 1) + 512 * tt
            p.dma("sync",
                  t["xslab"][:, 2048 * hf:2048 * hf + 2048].rearrange(
                      "p (c f) -> p c f", f=512),
                  in_ext[512 * hf:512 * hf + 512, coff:coff + 512].rearrange(
                      "(c p) f -> p c f", p=128), LD)
        p.wait("tensor", LD, LD.n)
        if tt > 0:
            p.wait("tensor", EV, ev_slots)
        for grp in range(2):
            for tsub in range(4):
                bank, bcol = banks[4 * grp + tsub]
                for c in range(8):
                    xslab = bfA if c < 4 else bfB
                    scol = 512 * (c % 4)
                    p.mm(bank[:, bcol:bcol + 512],
                         xslab[:, scol + 128 * tsub:scol + 128 * tsub + 128],
                         t["fqkv_sb"][:, 1536 * c + 512 * grp:
                                      1536 * c + 512 * grp + 512],
                         start=(c == 0), stop=(c == 7),
                         inc=MM if c == 7 else None)
        for tsub in range(4):
            tok = 4 * tt + tsub
            rsl = tsub % 2
            if tt + tsub > 0:
                p.wait("sync", EV, EV.n)
            p.dma("sync", t["rot_sb"][:, 256 * rsl:256 * rsl + 256],
                  t["rotq"][128 * tok:128 * tok + 128, :], LD)
            for grp in range(2):
                bank, bcol = banks[4 * grp + tsub]
                qsb = t["s_qsb"] if grp == 0 else t["s_ksb"]
                c1o = 256 * rsl + 128 * grp
                c2o = c1o + 64
                p.wait("vector", MM, MM.n)
                p.wait("vector", LD, LD.n)
                p.op("vector", lambda e, b=bank, bc=bcol: e.tensor_copy(
                    evA[:, 0:512], b[:, bc:bc + 512]), [])
                p.op("vector", lambda e: e.tensor_mul(
                    evB[:, 0:512], evA[:, 0:512], evA[:, 0:512]), [(EV, 1)])
                p.wait("vector", EV, EV.n)
                snap = debug and tt == 2 and tsub == 0 and grp == 0 and li == 0
                if snap:
                    p.op("vector", lambda e: e.tensor_copy(
                        t["xb"][:, 0:512], evA[:, 0:512]), [])
                p.op("vector", lambda e: e.reduce_sum(
                    t["mcol"][:, 0:8].rearrange("p h -> p h ()"),
                    evB[:, 0:512].rearrange("p (h d) -> p h d", d=64),
                    axis=AX.X), [(EV, 1)])
                p.wait("vector", EV, EV.n)
                if snap:
                    p.op("vector", lambda e: e.tensor_copy(
                        t["xb"][:, 512:520], t["mcol"][:, 0:8]), [])
                p.op("vector", lambda e: e.tensor_single_scalar(
                    t["mcol"][:, 0:8], t["mcol"][:, 0:8], 1e-24, ALU.max),
                    [(EV, 1)])
                p.wait("scalar", EV, EV.n)
                p.op("scalar", lambda e: e.activation(
                    t["mcol"][:, 0:8], t["mcol"][:, 0:8], AF.Ln), [])
                p.op("scalar", lambda e: e.activation(
                    t["mcol"][:, 0:8], t["mcol"][:, 0:8], AF.Exp, scale=-0.5),
                    [(SC, 1)])
                p.wait("vector", SC, SC.n)
                if snap:
                    p.op("vector", lambda e: e.tensor_copy(
                        t["xb"][:, 520:528], t["mcol"][:, 0:8]), [])
                p.op("vector", lambda e: e.tensor_mul(
                    brep(evA[:, 0:512], [[1, 64], [64, 8]]),
                    brep(evA[:, 0:512], [[1, 64], [64, 8]]),
                    brep(t["mcol"][:, 0:8], [[0, 64], [1, 8]])), [])
                if snap:
                    p.op("vector", lambda e: e.tensor_copy(
                        t["xb"][:, 528:1040], evA[:, 0:512]), [])
                p.op("vector", lambda e, qsb=qsb: e.tensor_mul(
                    evA[:, 0:512].rearrange("p (h d) -> p h d", d=64),
                    evA[:, 0:512].rearrange("p (h d) -> p h d", d=64),
                    brep(qsb[:, 0:64], [[0, 8], [1, 64]])), [])
                if snap:
                    p.op("vector", lambda e: e.tensor_copy(
                        t["xb"][:, 1040:1552], evA[:, 0:512]), [])
                p.op("vector", lambda e, rsl=rsl, c1o=c1o: e.tensor_mul(
                    evB[:, 0:512].rearrange("p (h d) -> p h d", d=64),
                    evA[:, 0:512].rearrange("p (h d) -> p h d", d=64),
                    brep(t["rot_sb"][:, c1o:c1o + 64], [[0, 8], [1, 64]])), [])
                p.op("vector", lambda e, c2o=c2o: e.tensor_mul(
                    Tb[:, 0:512].rearrange("p (h d) -> p h d", d=64)[:, :, 0:32],
                    evA[:, 0:512].rearrange("p (h d) -> p h d", d=64)[:, :, 32:64],
                    brep(t["rot_sb"][:, c2o:c2o + 32], [[0, 8], [1, 32]])), [])
                p.op("vector", lambda e, c2o=c2o: e.tensor_mul(
                    Tb[:, 0:512].rearrange("p (h d) -> p h d", d=64)[:, :, 32:64],
                    evA[:, 0:512].rearrange("p (h d) -> p h d", d=64)[:, :, 0:32],
                    brep(t["rot_sb"][:, c2o + 32:c2o + 64], [[0, 8], [1, 32]])),
                    [])
                p.op("vector", lambda e: e.tensor_add(
                    evB[:, 0:512], evB[:, 0:512], Tb[:, 0:512]), [(EV, 1)])
                p.wait("tensor", EV, EV.n)
                for hh2 in range(4):
                    p.op("tensor", lambda e, b=bank, bc=bcol, hh2=hh2:
                         e.transpose(b[:, bc + 128 * hh2:bc + 128 * hh2 + 128],
                                     evB[:, 128 * hh2:128 * hh2 + 128],
                                     t["s_id"][:]),
                         [(MM, 1)] if hh2 == 3 else [])
                p.wait("vector", MM, MM.n)
                pslot = 2 * tsub + grp
                p.wait("vector", ST, p.pb_st.get(pslot, 0))
                p.op("vector", lambda e, b=bank, bc=bcol, psl=pslot:
                     e.tensor_copy(Pb[:, 512 * psl:512 * psl + 512],
                                   b[:, bc:bc + 512]), [(EV, 1)])
                p.wait("sync", EV, EV.n)
                for hh2 in range(4):
                    p.dma("sync",
                          t["qkv_d"][512 * grp + 128 * hh2:
                                     512 * grp + 128 * hh2 + 128,
                                     128 * tok:128 * tok + 128],
                          Pb[:, 512 * pslot + 128 * hh2:
                             512 * pslot + 128 * hh2 + 128], ST)
                p.pb_st[pslot] = ST.n
        ev_slots = EV.n
        p.wait("tensor", EV, EV.n)
        for tsub in range(4):
            bank, bcol = banks[tsub]
            for c in range(8):
                xslab = bfA if c < 4 else bfB
                scol = 512 * (c % 4)
                p.mm(bank[:, bcol:bcol + 512],
                     xslab[:, scol + 128 * tsub:scol + 128 * tsub + 128],
                     t["fqkv_sb"][:, 1536 * c + 1024:1536 * c + 1536],
                     start=(c == 0), stop=(c == 7),
                     inc=MM if c == 7 else None)
        mm_v_prev = MM.n
        p.wait("vector", MM, MM.n)
        for tsub in range(4):
            bank, bcol = banks[tsub]
            p.wait("vector", ST, ST.n)
            p.op("vector", lambda e: e.memset(
                ost[:, 0:520].rearrange("p (h d) -> p h d", d=65)
                [:, :, 64:65], 1.0), [(EV, 1)])
            p.op("vector", lambda e, b=bank, bc=bcol: e.tensor_copy(
                ost[:, 0:520].rearrange("p (h d) -> p h d", d=65)[:, :, 0:64],
                b[:, bc:bc + 512].rearrange("p (h d) -> p h d", d=64)),
                [(EV, 1)])
            p.wait("sync", EV, EV.n)
            tok = 4 * tt + tsub
            p.dma("sync", t["v_d"][128 * tok:128 * tok + 128, :],
                  ost[:, 0:520], ST)
        ev_slots = EV.n
    st_phaseA = ST.n
    if debug and li == 0:
        p.wait("gpsimd", ST, st_phaseA)
        p.dma("gpsimd", t["dbg_qkv"], t["qkv_d"], GL)
        p.dma("gpsimd", t["dbg_v2"], t["v_d"], GL)
        p.wait("gpsimd", EV, EV.n)
        p.dma("gpsimd", t["dbg_ph"], t["xb"], GL)
    if FSUB <= 1:
        return

    p.wait("sync", ST, st_phaseA)
    kb, qb2, vti, Pi, subT, Cbuf = (t[k] for k in
                                    ("kb", "qb2", "vti", "Pi", "subT", "Cbuf"))
    mrow = t["r3"][0:1, 0:512]
    ev_oD = 0
    qm_ev = 0
    mm_wi_end = 0
    for wi in range(1, 6):
        if wi > 1:
            p.wait("sync", MM, mm_wi_end)
        for fc in range(4):
            p.dma("sync", kb[:, 1024 * fc:1024 * fc + 1024],
                  t["qkv_d"][512 + 128 * fc:512 + 128 * fc + 128,
                             512 * (wi - 1):512 * (wi - 1) + 1024], LD)
            p.dma("sync", qb2[:, 512 * fc:512 * fc + 512],
                  t["qkv_d"][128 * fc:128 * fc + 128,
                             512 * wi:512 * wi + 512], LD)
        p.wait("tensor", LD, LD.n)
        for hp in range(4):
            for hh in range(2):
                base = 64 * hh
                for qs in range(4):
                    if qm_ev:
                        p.wait("tensor", EV, qm_ev)
                    p.mm(ps[2][:, 128 * qs:512],
                         qb2[base:base + 64,
                             512 * hp + 128 * qs:512 * hp + 128 * qs + 128],
                         kb[base:base + 64,
                            1024 * hp + 128 * qs:1024 * hp + 512], True, True)
                    p.mm(ps[2][:, 512:640 + 128 * qs],
                         qb2[base:base + 64,
                             512 * hp + 128 * qs:512 * hp + 128 * qs + 128],
                         kb[base:base + 64,
                            1024 * hp + 512:1024 * hp + 640 + 128 * qs],
                         True, True, inc=MM)
                    p.wait("vector", MM, MM.n)
                    p.op("vector", lambda e, qs=qs: e.tensor_add(
                        ps[2][:, 128 * qs:128 * qs + 128],
                        ps[2][:, 128 * qs:128 * qs + 128], t["s_trif"][:]), [])
                    p.op("vector", lambda e, qs=qs: e.tensor_add(
                        ps[2][:, 512 + 128 * qs:512 + 128 * qs + 128],
                        ps[2][:, 512 + 128 * qs:512 + 128 * qs + 128],
                        t["s_trip"][:]), [(EV, 1)])
                    if wi == 2:
                        p.op("vector", lambda e, qs=qs: e.tensor_add(
                            ps[2][:, 128 * qs:512], ps[2][:, 128 * qs:512],
                            t["s_mP0"][:, 512 * qs + 128 * qs:
                                       512 * qs + 512]), [(EV, 1)])
                    p.wait("vector", EV, EV.n)
                    p.op("vector", lambda e, qs=qs: e.reduce_max(
                        t["mcol"][:, qs:qs + 1],
                        ps[2][:, 128 * qs:640 + 128 * qs], axis=AX.X),
                        [(EV, 1)])
                    qm_ev = EV.n
                p.wait("tensor", EV, qm_ev)
                p.wait("tensor", EV, ev_oD)
                for qs in range(4):
                    p.op("tensor", lambda e, qs=qs: e.transpose(
                        ps[3][0:1, 512 + 128 * qs:512 + 128 * qs + 128],
                        t["mcol"][:, qs:qs + 1], t["s_id"][:]),
                        [(MM, 1)] if qs == 3 else [])
                p.wait("vector", MM, MM.n)
                p.op("vector", lambda e: e.tensor_copy(
                    mrow, ps[3][0:1, 512:1024]), [(EV, 1)])
                p.wait("tensor", EV, EV.n)
                p.mm(ps[3][:, 512:1024], t["s_mones_r"][:], mrow, True, True,
                     inc=MM)
                p.wait("vector", MM, MM.n)
                p.op("vector", lambda e, hh=hh: e.tensor_copy(
                    Cbuf[:, 512 * hh:512 * hh + 512], ps[3][:, 512:1024]),
                    [(EV, 1)])
                p.wait("tensor", EV, EV.n)
            sub_ev = {}
            pv_mm = {}
            for kc in range(8):
                Sps = ps[kc % 2]
                if kc >= 2:
                    p.wait("tensor", EV, sub_ev[kc - 2])
                if kc < 4:
                    q0, q1 = 0, 128 * (kc + 1)
                else:
                    q0, q1 = 128 * (kc - 4), 512
                for hh in range(2):
                    base = 64 * hh
                    p.mm(Sps[:, 512 * hh + q0:512 * hh + q1],
                         kb[base:base + 64,
                            1024 * hp + 128 * kc:1024 * hp + 128 * kc + 128],
                         qb2[base:base + 64, 512 * hp + q0:512 * hp + q1],
                         True, True, inc=MM if hh == 1 else None)
                p.wait("vector", MM, MM.n)
                for hh in range(2):
                    so = 1024 * (kc % 2) + 512 * hh
                    p.op("vector", lambda e, so=so, q0=q0, q1=q1, hh=hh,
                         Sps=Sps: e.tensor_add(
                             subT[:, so + q0:so + q1],
                             Sps[:, 512 * hh + q0:512 * hh + q1],
                             Cbuf[:, 512 * hh + q0:512 * hh + q1]), [])
                    cpos = 128 * kc if kc < 4 else 128 * (kc - 4)
                    tri = t["s_trip"] if kc < 4 else t["s_trif"]
                    p.op("vector", lambda e, so=so, cp=cpos, tri=tri:
                         e.tensor_add(subT[:, so + cp:so + cp + 128],
                                      subT[:, so + cp:so + cp + 128],
                                      tri[:]), [])
                    if wi == 2 and kc < 4:
                        p.op("vector", lambda e, so=so, kc=kc, q0=q0, q1=q1:
                             e.tensor_add(
                                 subT[:, so + q0:so + q1],
                                 subT[:, so + q0:so + q1],
                                 t["s_mP0"][:, 512 * kc + q0:512 * kc + q1]),
                             [])
                    if hh == 1:
                        p.op("vector", lambda e, so=so: e.tensor_copy(
                            subT[:, so:so + 1], subT[:, so:so + 1]),
                            [(EV, 1)])
                sub_ev[kc] = EV.n
                p.wait("scalar", EV, sub_ev[kc])
                if kc >= 2:
                    p.wait("scalar", MM, pv_mm[kc - 2])
                p.op("scalar", lambda e, kc=kc: e.activation(
                    Pi[:, 1024 * (kc % 2):1024 * (kc % 2) + 1024],
                    subT[:, 1024 * (kc % 2):1024 * (kc % 2) + 1024],
                    AF.Exp), [(SC, 1)])
                if kc >= 2:
                    p.wait("sync", MM, pv_mm[kc - 2])
                tokb = 512 * (wi - 1) + 128 * kc
                p.dma("sync", vti[:, 520 * (kc % 2):520 * (kc % 2) + 520],
                      t["v_d"][tokb:tokb + 128, :], LD)
                p.wait("tensor", SC, SC.n)
                p.wait("tensor", LD, LD.n)
                if kc == 0:
                    p.wait("tensor", EV, ev_oD)
                for hh in range(2):
                    h = 2 * hp + hh
                    p.mm(ps[3][0:65, 512 * hh + q0:512 * hh + q1],
                         vti[:, 520 * (kc % 2) + 65 * h:
                             520 * (kc % 2) + 65 * h + 65],
                         Pi[:, 1024 * (kc % 2) + 512 * hh + q0:
                            1024 * (kc % 2) + 512 * hh + q1],
                         start=(kc == 0), stop=(kc == 7),
                         inc=MM if hh == 1 else None)
                pv_mm[kc] = MM.n
            p.wait("vector", MM, MM.n)
            p.wait("vector", ST, ST.n)
            for hh in range(2):
                p.op("vector", lambda e, hh=hh: e.tensor_copy(
                    ost[0:64, 512 * hh:512 * hh + 512],
                    ps[3][0:64, 512 * hh:512 * hh + 512]), [])
                p.op("vector", lambda e, hh=hh: e.tensor_copy(
                    t["lst"][0:1, 512 * hh:512 * hh + 512],
                    ps[3][64:65, 512 * hh:512 * hh + 512]), [(EV, 1)])
            ev_oD = EV.n
            p.wait("sync", EV, EV.n)
            il_flat = t["il_row"].rearrange("a b -> (a b) ()").rearrange(
                "(x f) y -> x (f y)", x=1)
            for hh in range(2):
                h = 2 * hp + hh
                p.dma("sync", t["o_d"][65 * h:65 * h + 64,
                                       512 * (wi - 1):512 * (wi - 1) + 512],
                      ost[0:64, 512 * hh:512 * hh + 512], ST)
                i0 = 512 * (8 * (wi - 1) + h)
                p.dma("sync", il_flat[0:1, i0:i0 + 512],
                      t["lst"][0:1, 512 * hh:512 * hh + 512], ST)
        mm_wi_end = MM.n
    st_attn = ST.n
    if FSUB <= 2:
        if debug and li == 0:
            p.wait("gpsimd", ST, st_attn)
            p.dma("gpsimd", t["dbg_oi"], t["o_d"], GL)
            p.dma("gpsimd", t["dbg_il"], t["il_row"], GL)
        return

    p.wait("sync", ST, st_attn)
    p.dma("sync", t["li_sb"][:], t["il_row"], LD)
    p.wait("vector", LD, LD.n)
    p.op("vector", lambda e: e.reciprocal(t["li_sb"][:], t["li_sb"][:]),
         [(EV, 1)])
    p.wait("sync", EV, EV.n)
    p.dma("sync", t["il_row"], t["li_sb"][:], ST)
    p.wait("sync", ST, ST.n)
    il_flat = t["il_row"].rearrange("a b -> (a b) ()").rearrange(
        "(x f) y -> x (f y)", x=1)
    lr2 = t["lr2"]
    xa, xb = t["xa"], t["xb"]
    for wi in range(1, 6):
        ev_onorm = []
        for hp in range(4):
            oslot = hp % 2
            if hp >= 2:
                p.wait("sync", EV, ev_onorm[hp - 2])
            p.dma("sync", bfA[0:64, 512 * oslot:512 * oslot + 512],
                  t["o_d"][65 * 2 * hp:65 * 2 * hp + 64,
                           512 * (wi - 1):512 * (wi - 1) + 512], LD)
            p.dma("sync", bfA[64:128, 512 * oslot:512 * oslot + 512],
                  t["o_d"][65 * (2 * hp + 1):65 * (2 * hp + 1) + 64,
                           512 * (wi - 1):512 * (wi - 1) + 512], LD)
            i0 = 512 * (8 * (wi - 1) + 2 * hp)
            p.wait("sync", MM, MM.n)
            p.dma("sync", lr2[0:1, :], il_flat[0:1, i0:i0 + 512], LD)
            p.dma("sync", lr2[1:2, :], il_flat[0:1, i0 + 512:i0 + 1024], LD)
            p.wait("tensor", LD, LD.n)
            if hp >= 1 or wi > 1:
                p.wait("tensor", EV, EV.n)
            p.mm(ps[3][:, 0:512], t["s_e2"][:], lr2[:], True, True, inc=MM)
            p.wait("vector", MM, MM.n)
            p.op("vector", lambda e, hp=hp, osl=oslot: e.tensor_mul(
                Pb[:, 512 * hp:512 * hp + 512],
                bfA[:, 512 * osl:512 * osl + 512], ps[3][:, 0:512]),
                [(EV, 1)])
            ev_onorm.append(EV.n)
        p.wait("tensor", EV, EV.n)
        p.wait("vector", ST, ST.n)
        ev_x = []
        for ddc in range(8):
            dst = ps[ddc % 2][:, 512 * ((ddc // 2) % 2):
                              512 * ((ddc // 2) % 2) + 512]
            if ddc >= 4:
                p.wait("tensor", EV, ev_x[ddc - 4])
            for hp in range(4):
                p.mm(dst, t["fwo_sb"][:, 1024 * hp + 128 * ddc:
                                      1024 * hp + 128 * ddc + 128],
                     Pb[:, 512 * hp:512 * hp + 512],
                     start=(hp == 0), stop=(hp == 3),
                     inc=MM if hp == 3 else None)
            xt = xa if ddc % 2 == 0 else xb
            if ddc >= 2:
                p.wait("gpsimd", EV, ev_x[ddc - 2])
            if wi == 1:
                p.dma("gpsimd", xt[:, 0:1],
                      in_halo[128 * ddc:128 * ddc + 128, HALO - 1:HALO], GL,
                      slow=True)
            else:
                p.dma("gpsimd", xt[:, 0:512],
                      in_full[128 * ddc:128 * ddc + 128,
                              512 * (wi - 2):512 * (wi - 2) + 512], GL)
            p.wait("vector", MM, MM.n)
            p.wait("vector", LD, LD.n)
            p.wait("vector", GL, GL.n)
            if wi == 1:
                p.op("vector", lambda e, d=ddc, dst=dst, xt=xt: e.tensor_add(
                    Tb[:, 512 * d:512 * d + 1], dst[:, 511:512], xt[:, 0:1]),
                    [(EV, 1)])
            else:
                p.op("vector", lambda e, d=ddc, dst=dst, xt=xt: e.tensor_add(
                    Tb[:, 512 * d:512 * d + 512], dst, xt[:, 0:512]),
                    [(EV, 1)])
            ev_x.append(EV.n)
            p.wait("sync", EV, EV.n)
            if wi == 1:
                p.dma("sync", t["xmid_d"][128 * ddc:128 * ddc + 128, 0:1],
                      Tb[:, 512 * ddc:512 * ddc + 1], ST, slow=True)
            else:
                j0 = 512 * wi - 1023
                p.dma("sync",
                      t["xmid_d"][128 * ddc:128 * ddc + 128, j0:j0 + 512],
                      Tb[:, 512 * ddc:512 * ddc + 512], ST)
    st_xmid = ST.n
    if debug and li == 0:
        p.wait("gpsimd", ST, st_xmid)
        p.dma("gpsimd", t["dbg_oi"], t["o_d"], GL)
        p.dma("gpsimd", t["dbg_xm"], t["xmid_d"], GL)
    if FSUB <= 3:
        return

    p.wait("sync", ST, st_xmid)
    p.wait("gpsimd", ST, st_xmid)
    _emit_mlp(p, li, out_full, debug)

    if li == 0 and FSUB >= 7:
        p.wait("gpsimd", ST, ST.n)
        p.op("gpsimd", lambda e: e.collective_compute(
            "AllGather", ALU.bypass, replica_groups=GROUPS4,
            ins=[t["ag_y1_in"]], outs=[t["ag_y1_out"]]), [(CC, 1)])
        p.wait("gpsimd", CC, CC.n)
        p.op("vector", lambda e: e.memset(t["evA"][:, 0:HALO], 0.0),
             [(EV, 1)])
        p.wait("gpsimd", EV, EV.n)
        _emit_halo_switch(p, "y1")
    if debug and li == 0:
        p.wait("gpsimd", ST, ST.n)
        p.dma("gpsimd", t["dbg_y1"], out_full, GL)


def _emit_mlp(p, li, out_full, debug):
    t, S, ps = p.t, p.S, p.ps
    LD, GL, MM, EV, SC, ST = (S[k] for k in
                              ("LD", "GL", "MM", "EV", "SC", "ST"))
    r0, r1, r2, r3 = t["r0"], t["r1"], t["r2"], t["r3"]
    gated, lnout = t["gated"], t["lnout"]
    evA, evB, Pb, bfA, Tb = t["evA"], t["evB"], t["Pb"], t["bfA"], t["Tb"]
    banks = p.banks
    for qt in range(2):
        if debug and li == 0 and qt == 1 and getattr(p, "mlp_gl_tap", 0):
            p.wait("vector", GL, p.mlp_gl_tap)
        p.wait("vector", ST, ST.n)
        ev_st1 = {}
        mm_st1 = {}
        for c in range(8):
            coff = (1 if c < 4 else 0) + 1024 * qt
            xt = evA if c % 2 == 0 else evB
            if c >= 2:
                p.wait("gpsimd", EV, ev_st1[c - 2])
                p.wait("gpsimd", MM, mm_st1[c - 2])
            p.dma("gpsimd", xt[:, 0:1024],
                  t["xmid_d"][128 * c:128 * c + 128, coff:coff + 1024], GL)
            p.wait("vector", GL, GL.n)
            if c >= 2:
                p.wait("vector", MM, MM.n)
            sqb = Tb[:, 2048 + 1024 * (c % 2):2048 + 1024 * (c % 2) + 1024]
            p.op("vector", lambda e, xt=xt, sqb=sqb: e.tensor_mul(
                sqb, xt[:, 0:1024], xt[:, 0:1024]), [(EV, 1)])
            ev_st1[c] = EV.n
            p.wait("tensor", EV, EV.n)
            for half in range(2):
                p.mm(ps[2][0:1, 512 * half:512 * half + 512],
                     t["s_ones_cf"][:],
                     xt[:, 512 * half:512 * half + 512],
                     start=(c == 0), stop=(c == 7))
                p.mm(ps[3][0:1, 512 * half:512 * half + 512],
                     t["s_ones_cf"][:],
                     sqb[:, 512 * half:512 * half + 512],
                     start=(c == 0), stop=(c == 7),
                     inc=MM if half == 1 else None)
            mm_st1[c] = MM.n
        p.wait("vector", MM, MM.n)
        p.op("vector", lambda e: e.tensor_scalar_mul(
            r0[0:1, 0:1024], ps[2][0:1, 0:1024], 1.0 / 1024.0), [])
        p.op("vector", lambda e: e.tensor_scalar_mul(
            r1[0:1, 0:1024], ps[3][0:1, 0:1024], 1.0 / 1024.0), [])
        if debug and li == 0 and qt == 0:
            p.op("vector", lambda e: e.tensor_copy(
                t["lst"][0:1, 0:1024], r0[0:1, 0:1024]), [])
            p.op("vector", lambda e: e.tensor_copy(
                t["dbgrow"][0:1, 0:1024], r1[0:1, 0:1024]), [])
        p.op("vector", lambda e: e.tensor_mul(
            r2[0:1, 0:1024], r0[0:1, 0:1024], r0[0:1, 0:1024]), [])
        p.op("vector", lambda e: e.tensor_sub(
            r1[0:1, 0:1024], r1[0:1, 0:1024], r2[0:1, 0:1024]), [])
        p.op("vector", lambda e: e.tensor_scalar_add(
            r1[0:1, 0:1024], r1[0:1, 0:1024], 1e-5), [(EV, 1)])
        p.wait("scalar", EV, EV.n)
        p.op("scalar", lambda e: e.activation(
            r2[0:1, 0:1024], r1[0:1, 0:1024], AF.Ln), [])
        p.op("scalar", lambda e: e.activation(
            r2[0:1, 0:1024], r2[0:1, 0:1024], AF.Exp, scale=-0.5), [(SC, 1)])
        p.wait("vector", SC, SC.n)
        p.op("vector", lambda e: e.tensor_mul(
            r3[0:1, 0:1024], r0[0:1, 0:1024], r2[0:1, 0:1024]), [])
        p.op("vector", lambda e: e.tensor_scalar_mul(
            r3[0:1, 0:1024], r3[0:1, 0:1024], -1.0), [(EV, 1)])
        p.wait("tensor", EV, EV.n)
        for half in range(2):
            p.mm(ps[2][:, 512 * half:512 * half + 512], t["s_ones_r"][:],
                 r2[0:1, 512 * half:512 * half + 512], True, True)
            p.mm(ps[3][:, 512 * half:512 * half + 512], t["s_ones_r"][:],
                 r3[0:1, 512 * half:512 * half + 512], True, True,
                 inc=MM if half == 1 else None)
        p.wait("vector", MM, MM.n)
        for c in range(8):
            coff = (1 if c < 4 else 0) + 1024 * qt
            xt = evA if c % 2 == 0 else evB
            if c >= 2:
                p.wait("gpsimd", EV, EV.n)
            p.dma("gpsimd", xt[:, 0:1024],
                  t["xmid_d"][128 * c:128 * c + 128, coff:coff + 1024], GL)
            p.wait("vector", GL, GL.n)
            p.op("vector", lambda e, xt=xt: e.tensor_mul(
                xt[:, 0:1024], xt[:, 0:1024], ps[2][:, 0:1024]), [])
            p.op("vector", lambda e, xt=xt: e.tensor_add(
                xt[:, 0:1024], xt[:, 0:1024], ps[3][:, 0:1024]), [])
            p.op("vector", lambda e, xt=xt, c=c: e.tensor_scalar(
                lnout[:, 1024 * c:1024 * c + 1024], xt[:, 0:1024],
                t["s_lng"][:, c:c + 1], t["s_lnb"][:, c:c + 1],
                ALU.mult, ALU.add), [(EV, 1)])
        ev_ln = EV.n
        p.wait("tensor", EV, ev_ln)
        if FSUB <= 4:
            continue
        a_done = 0
        g_done = 0
        gate_done = 0
        for ft in range(22):
            ch = F1_CH[ft]
            for half in range(2):
                wslot = (2 * ft + half) % 4
                off = F1_OFF[ft] + HID * half
                p.wait("sync", EV, p.wst_ev)
                p.dma("sync",
                      t["wst"][:, 0:8 * ch].rearrange("p (c f) -> p c f",
                                                      f=ch),
                      t[f"fw1{li}"][:, off:off + ch].rearrange(
                          "(c p) f -> p c f", p=128), LD)
                if p.w1_slot_mm[wslot]:
                    p.wait("vector", MM, p.w1_slot_mm[wslot])
                p.wait("vector", LD, LD.n)
                p.op("vector", lambda e, ch=ch, wslot=wslot: e.tensor_copy(
                    t["w1t"][:, 1024 * wslot:1024 * wslot + 8 * ch],
                    t["wst"][:, 0:8 * ch]), [(EV, 1)])
                p.wst_ev = EV.n
                p.wait("tensor", EV, EV.n)
                if half == 0 and a_done:
                    p.wait("tensor", EV, a_done)
                if half == 1 and g_done:
                    p.wait("tensor", SC, g_done)
                bank = ps[half]
                for c in range(8):
                    for nn in range(2):
                        p.mm(bank[0:ch, 512 * nn:512 * nn + 512],
                             t["w1t"][:, 1024 * wslot + ch * c:
                                      1024 * wslot + ch * c + ch],
                             lnout[:, 1024 * c + 512 * nn:
                                   1024 * c + 512 * nn + 512],
                             start=(c == 0), stop=(c == 7),
                             inc=MM if (c == 7 and nn == 1) else None)
                p.w1_slot_mm[wslot] = MM.n
                if half == 0:
                    p.wait("vector", MM, MM.n)
                    p.op("vector", lambda e, ch=ch, ft=ft: e.tensor_scalar(
                        Tb[0:ch, 0:1024], ps[0][0:ch, 0:1024],
                        t["s_b1a"][0:ch, ft:ft + 1], None, ALU.add),
                        [(EV, 1)])
                    a_done = EV.n
                else:
                    p.wait("scalar", MM, MM.n)
                    if gate_done:
                        p.wait("scalar", EV, gate_done)
                    p.op("scalar", lambda e, ch=ch, ft=ft: e.activation(
                        Tb[0:ch, 1024:2048], ps[1][0:ch, 0:1024], AF.Gelu,
                        bias=t["s_b1g"][0:ch, ft:ft + 1]), [(SC, 1)])
                    g_done = SC.n
            p.wait("vector", SC, SC.n)
            p.op("vector", lambda e, ch=ch, ft=ft: e.tensor_mul(
                gated[0:ch, 1024 * ft:1024 * ft + 1024],
                Tb[0:ch, 0:1024], Tb[0:ch, 1024:2048]), [(EV, 1)])
            gate_done = EV.n
        if debug and li == 0 and qt == 0:
            p.wait("gpsimd", EV, gate_done)
            p.dma("gpsimd", t["dbg_ln"], lnout[:], GL)
            p.dma("gpsimd", t["dbg_gt"], gated[:], GL)
            p.dma("gpsimd", t["dbg_ph"][0:1, 0:1024], t["lst"][:], GL)
            p.dma("gpsimd", t["dbg_ph"][0:1, 1024:2048], t["dbgrow"][:], GL)
            p.mlp_gl_tap = GL.n
        if FSUB <= 5:
            continue
        nft = min(22, W2CAP)
        for sub in range(0 if W2ALT == 7 else min(2, W2SUB)):
            for ft in range(nft):
                ch = F1_CH[ft]
                wslot = ft % 2
                if W2LOAD:
                    p.wait("sync", EV, p.wst_ev)
                    p.dma("sync", t["wst"][0:ch, 0:1024],
                          t[f"fw2{li}"][F1_OFF[ft]:F1_OFF[ft] + ch, :], LD)
                    if p.w2_slot_mm[wslot]:
                        p.wait("vector", MM, p.w2_slot_mm[wslot])
                    p.wait("vector", LD, LD.n)
                    p.op("vector", lambda e, ch=ch, wslot=wslot:
                         e.tensor_copy(
                             t["w2u"][0:ch, 1024 * wslot:1024 * wslot + 1024],
                             t["wst"][0:ch, 0:1024]), [(EV, 1)])
                    p.wst_ev = EV.n
                    p.wait("tensor", EV, EV.n)
                if ft == 0 and sub == 0:
                    p.wait("tensor", EV, gate_done)
                    p.wait("tensor", SC, g_done)
                if ft == 0 and sub == 1:
                    p.wait("tensor", EV, EV.n)
                for dd in range(8):
                    bank, bcol = banks[dd]
                    lhs = (t["w2u"][0:ch, 1024 * wslot + 128 * dd:
                                    1024 * wslot + 128 * dd + 128]
                           if W2ALT not in (2, 3) else Pb[0:ch, 0:128])
                    rhs = (gated[0:ch, 1024 * ft + 512 * sub:
                                 1024 * ft + 512 * sub + 512]
                           if W2ALT not in (1, 3) else lnout[0:ch, 0:512])
                    if W2ALT == 8:
                        if 8 * ft + dd < NMM:
                            p.mm(bank[:, bcol:bcol + 512], Pb[0:128, 0:128],
                                 lnout[0:128, 0:512], start=True, stop=True)
                    elif W2ALT in (4, 6):
                        p.mm(bank[:, bcol:bcol + 512], Pb[0:128, 0:128],
                             lnout[0:128, 0:512], start=True, stop=True,
                             inc=MM if dd == 7 else None)
                    else:
                        p.mm(bank[:, bcol:bcol + 512], lhs, rhs,
                             start=(ft == 0), stop=(ft == nft - 1),
                             inc=MM if dd == 7 else None)
                p.w2_slot_mm[wslot] = MM.n
            if W2ALT not in (6, 8):
                p.wait("vector", MM, MM.n)
                p.wait("vector", ST, ST.n)
            if debug and li == 0 and qt == 0 and getattr(p, "stat_gl_tap", 0):
                p.wait("vector", GL, p.stat_gl_tap)
            ev_w2 = {}
            for dd in range(8 if W2EVAC else 0):
                bank, bcol = banks[dd]
                xt = evA if dd % 2 == 0 else evB
                ucol = 1024 * qt + 512 * sub
                if dd >= 2:
                    p.wait("gpsimd", EV, ev_w2[dd - 2])
                p.dma("gpsimd", xt[:, 1024:1536],
                      t["xmid_d"][128 * dd:128 * dd + 128,
                                  1 + ucol:1 + ucol + 512], GL)
                p.wait("vector", GL, GL.n)
                p.op("vector", lambda e, b=bank, bc=bcol, xt=xt, dd=dd:
                     e.scalar_tensor_tensor(
                         Tb[:, 512 * dd:512 * dd + 512], b[:, bc:bc + 512],
                         t["s_b2c"][:, dd:dd + 1], xt[:, 1024:1536],
                         ALU.add, ALU.add), [(EV, 1)])
                ev_w2[dd] = EV.n
                p.wait("sync", EV, EV.n)
                p.dma("sync", out_full[128 * dd:128 * dd + 128,
                                       ucol:ucol + 512],
                      Tb[:, 512 * dd:512 * dd + 512], ST)
                if li == 0:
                    if dd >= 4:
                        p.wait("vector", ST, p.st_ye[dd - 4])
                    p.op("vector", lambda e, dd=dd: e.tensor_copy(
                        bfA[:, 512 * (dd % 4):512 * (dd % 4) + 512],
                        Tb[:, 512 * dd:512 * dd + 512]), [(EV, 1)])
                    p.wait("sync", EV, EV.n)
                    p.dma("sync",
                          t["ye_d"][128 * dd:128 * dd + 128,
                                    HALO + ucol:HALO + ucol + 512],
                          bfA[:, 512 * (dd % 4):512 * (dd % 4) + 512], ST)
                    p.st_ye[dd] = ST.n
                    u0 = max(0, 1022 - ucol)
                    if u0 < 512:
                        p.dma("sync",
                              t["ag_y1_in"][128 * dd:128 * dd + 128,
                                            ucol + u0 - 1022:
                                            ucol + 512 - 1022],
                              Tb[:, 512 * dd + u0:512 * dd + 512], ST)
            if W2ALT not in (6, 8):
                p.wait("tensor", EV, EV.n)


def build_final(p, debug):
    t, S, ps = p.t, p.S, p.ps
    LD, MM, EV, SC, ST = (S[k] for k in ("LD", "MM", "EV", "SC", "ST"))
    xa, xb, Tb, Pb, bfA, bfB = (t[k] for k in
                                ("xa", "xb", "Tb", "Pb", "bfA", "bfB"))
    r2, r3 = t["r2"], t["r3"]
    for tt in range(4):
        p.wait("vector", ST, ST.n)
        ev_fc = {}
        for c in range(8):
            xt = xa if c % 2 == 0 else xb
            if c >= 2:
                p.wait("sync", EV, ev_fc[c - 2])
            p.dma("sync", xt[:, 0:512],
                  t["x1_full"][128 * c:128 * c + 128,
                               512 * tt:512 * tt + 512], LD)
            p.dma("sync", xt[:, 512:1024],
                  t["y2_full"][128 * c:128 * c + 128,
                               512 * tt:512 * tt + 512], LD)
            p.wait("vector", LD, LD.n)
            if c >= 2:
                p.wait("vector", MM, MM.n)
            p.op("vector", lambda e, xt=xt, c=c: e.tensor_add(
                Tb[:, 512 * c:512 * c + 512], xt[:, 0:512], xt[:, 512:1024]),
                [])
            p.op("vector", lambda e, c=c: e.tensor_copy(
                Pb[:, 512 * (c % 4):512 * (c % 4) + 512],
                Tb[:, 512 * c:512 * c + 512]), [])
            p.op("vector", lambda e, c=c: e.tensor_mul(
                bfA[:, 512 * (c % 4):512 * (c % 4) + 512],
                Pb[:, 512 * (c % 4):512 * (c % 4) + 512],
                Pb[:, 512 * (c % 4):512 * (c % 4) + 512]), [(EV, 1)])
            ev_fc[c] = EV.n
            p.wait("tensor", EV, EV.n)
            p.mm(ps[2][0:1, 0:512], t["s_ones_cb"][:],
                 Pb[:, 512 * (c % 4):512 * (c % 4) + 512],
                 start=(c == 0), stop=(c == 7))
            p.mm(ps[2][0:1, 512:1024], t["s_ones_cb"][:],
                 bfA[:, 512 * (c % 4):512 * (c % 4) + 512],
                 start=(c == 0), stop=(c == 7), inc=MM)
        _ln_rows_512(p)
        p.wait("tensor", EV, EV.n)
        p.mm(ps[2][:, 0:512], t["s_ones_r"][:], r2[0:1, 0:512], True, True)
        p.mm(ps[2][:, 512:1024], t["s_ones_r"][:], r3[0:1, 0:512], True, True,
             inc=MM)
        p.wait("vector", MM, MM.n)
        for c in range(8):
            p.op("vector", lambda e, c=c: e.tensor_mul(
                Tb[:, 512 * c:512 * c + 512], Tb[:, 512 * c:512 * c + 512],
                ps[2][:, 0:512]), [])
            p.op("vector", lambda e, c=c: e.tensor_add(
                Tb[:, 512 * c:512 * c + 512], Tb[:, 512 * c:512 * c + 512],
                ps[2][:, 512:1024]), [(EV, 1)])
            p.wait("sync", EV, EV.n)
            p.dma("sync", t["outT"][128 * c:128 * c + 128,
                                    512 * tt:512 * tt + 512],
                  Tb[:, 512 * c:512 * c + 512], ST)
        p.wait("tensor", EV, EV.n)
    p.wait("sync", ST, ST.n)


_CACHE = {}


def kernel(**inputs):
    maps = stage_inputs(inputs)
    key = ("nc", DEBUG, UPTO)
    if key not in _CACHE:
        _CACHE[key] = build(debug=DEBUG)
    ncobj = _CACHE[key]
    res = run_bass_kernel_spmd(ncobj, maps, core_ids=list(range(8)))
    out = np.empty((B2, N, D), np.float32)
    for r in range(8):
        B, Qq = r // 4, r % 4
        s0 = 2048 * Qq
        out[B, s0:s0 + TOK, :] = res.results[r]["outT"].T
    if DEBUG:
        kernel._last = res
    return out
